# version 09
# baseline (speedup 1.0000x reference)
import contextlib
import numpy as np
import ml_dtypes
import concourse.bass as bass
import concourse.mybir as mybir
from concourse.bass_utils import run_bass_kernel_spmd

F32 = mybir.dt.float32
F32R = mybir.dt.float32r
BF16 = mybir.dt.bfloat16
AF = mybir.ActivationFunctionType
OP = mybir.AluOpType
AX = mybir.AxisListType

bfd = ml_dtypes.bfloat16

B, T, D = 4, 4096, 1024
H, HD = 16, 64
C = 128
NCH = T // C
SEG = 256
NSEG = T // SEG
CPS = SEG // C
WIN = 2176
NW = WIN // 128
E, FE = 64, 512
HFF = 4096
QB = 127.0
MAGIC = 1.5 * 2.0 ** 23
NLVL = 7


def r32(ap):
    return ap.bitcast(F32R)


def rep_ap(ap, n, pos):
    lay = [list(x) for x in ap.ap]
    lay.insert(pos, [0, n])
    return bass.AP(ap.tensor, ap.offset, lay)


def split3(ap, a, b):
    lay = [list(ap.ap[0]), [b, a], [1, b]]
    return bass.AP(ap.tensor, ap.offset, lay)


class Ctr:
    def __init__(self, nc):
        self.nc = nc
        self.sems = {}
        self.n = {}

    def make(self, stack, names):
        for nm in names:
            self.sems[nm] = stack.enter_context(self.nc.semaphore(nm))
            self.n[nm] = 0

    def inc(self, nm, instr, amt=1):
        instr.then_inc(self.sems[nm], amt)
        self.n[nm] += amt
        return self.n[nm]

    def wait(self, eng, nm, v=None):
        eng.wait_ge(self.sems[nm], self.n[nm] if v is None else v)

    def incd(self, nm, eng, amt=1):
        eng.drain()
        return self.inc(nm, eng.nop(), amt)

    def dmaq(self, nm, eng, *args, **kw):
        eng.wait_ge(self.sems[nm], self.n[nm])
        return self.inc(nm, eng.dma_start(*args, **kw), 16)


def _host_prep(inp):
    f32 = np.float32
    x = np.asarray(inp["L0_attn_x"], f32)

    com = {}
    for nm, src in (("Wq", "L0_attn_Wq"), ("Wk", "L0_attn_Wk"), ("Wv", "L0_attn_Wv"),
                    ("Wg", "L0_attn_Wg"), ("Wo", "L0_attn_Wo"), ("Wb", "L0_attn_Wb"),
                    ("Ws_up", "L0_ffn_Ws_up"), ("Ws_down", "L0_ffn_Ws_down"),
                    ("W_up", "L0_ffn_W_up"), ("W_down", "L0_ffn_W_down")):
        com[nm] = np.ascontiguousarray(np.asarray(inp[src], f32))
    for nm, src in (("cqw", "L0_attn_cq_w"), ("ckw", "L0_attn_ck_w"), ("cvw", "L0_attn_cv_w")):
        com[nm] = np.ascontiguousarray(
            np.asarray(inp[src], f32).reshape(8, 128, 4).transpose(1, 0, 2))
    for nm, src in (("cqb", "L0_attn_cq_b"), ("ckb", "L0_attn_ck_b"), ("cvb", "L0_attn_cv_b")):
        com[nm] = np.ascontiguousarray(np.asarray(inp[src], f32).reshape(8, 128).T)
    com["nw_rep"] = np.ascontiguousarray(
        np.broadcast_to(np.tile(np.asarray(inp["L0_attn_nw"], f32), H), (128, D)))
    com["centroidsT"] = np.ascontiguousarray(np.asarray(inp["L0_ffn_centroids"], f32).T)
    com["ebias_rep"] = np.ascontiguousarray(
        np.broadcast_to(np.asarray(inp["L0_ffn_expert_biases"], f32), (128, E)))
    com["w_linT"] = np.ascontiguousarray(np.asarray(inp["L1_attn_w_lin"], f32).T)
    com["W1T"] = np.ascontiguousarray(np.asarray(inp["L1_ffn_W1"], f32).T)
    com["W2T"] = np.ascontiguousarray(np.asarray(inp["L1_ffn_W2"], f32).T)
    com["b1"] = np.asarray(inp["L1_ffn_b1"], f32).reshape(1, HFF)
    com["b2"] = np.asarray(inp["L1_ffn_b2"], f32).reshape(1, D)

    ti = np.arange(128)
    com["ident128"] = np.eye(128, dtype=bfd)
    com["identf"] = np.eye(128, dtype=f32)
    com["ltmask"] = (ti[:, None] > ti[None, :]).astype(bfd)
    com["utmask"] = (ti[:, None] < ti[None, :]).astype(bfd)
    com["utimask"] = (ti[:, None] <= ti[None, :]).astype(bfd)
    com["ones_col"] = np.ones((1, 128), f32)
    Ec = np.zeros((8, 16, 128), bfd)
    for c in range(8):
        Ec[c, 2 * c, 0:64] = 1
        Ec[c, 2 * c + 1, 64:128] = 1
    com["Ec"] = np.ascontiguousarray(Ec.transpose(1, 0, 2))
    com["Esq"] = np.ascontiguousarray(Ec.transpose(2, 0, 1))

    in_maps = []
    for core in range(8):
        b, g = core // 2, core % 2
        m = dict(com)
        m["x_full"] = np.ascontiguousarray(x[b])
        start = g * 2048
        lo = start - 128
        xw = np.zeros((WIN, D), f32)
        s0 = max(lo, 0)
        xw[s0 - lo:, :] = x[b, s0:start + 2048, :]
        m["x_win"] = xw
        mo = np.zeros(NCH, f32)
        mo[g * 16:(g + 1) * 16] = 1.0
        mh = np.zeros(NCH, f32)
        if g == 1:
            mh[15] = 1.0
        m["maskO"] = np.ascontiguousarray(np.broadcast_to(mo, (128, NCH)))
        m["maskH"] = np.ascontiguousarray(np.broadcast_to(mh, (128, NCH)))
        in_maps.append(m)
    return in_maps


class Prog:

    def __init__(self, nc, blk, ct, sb, ps, dram, dbg_out):
        self.nc, self.blk, self.ct, self.sb, self.dram = nc, blk, ct, sb, dram
        self.ps0, self.ps1, self.ps2, self.psT = ps
        self.dbg_out = dbg_out
        self.trtick = {"bf": [0, 0], "f32": [0, 0]}

    def tr_seq(self, items, ident_ap=None, f32psum=False, pre=None):
        blk, ct = self.blk, self.ct
        ident = ident_ap if ident_ap is not None else self.sb["ident"][:, :]
        slot_tick = self.trtick["f32" if f32psum else "bf"]
        for idx, (src, dst, fn, bias) in enumerate(items):
            sl = idx % 2
            p_in = src.ap[0][1]
            f_in = src.ap[-1][1]
            if f32psum:
                slot = (self.ps2[0:f_in, 0:p_in], self.ps2[0:f_in, 512:512 + p_in])[sl]
            else:
                slot = (self.psT[0:f_in, 0:p_in], self.psT[0:f_in, 1024:1024 + p_in])[sl]

            @blk.tensor
            def _(t, src=src, slot=slot, sl=sl, idx=idx):
                if idx == 0 and pre is not None:
                    t.wait_ge(ct.sems[pre[0]], pre[1])
                if slot_tick[sl]:
                    t.wait_ge(ct.sems["S"], slot_tick[sl])
                t.transpose(slot, src, ident)
                ct.incd("T", t)
            t_tick = ct.n["T"]

            @blk.scalar
            def _(s, dst=dst, fn=fn, bias=bias, slot=slot, t_tick=t_tick):
                s.wait_ge(ct.sems["T"], t_tick)
                if fn is None:
                    s.copy(dst, slot)
                elif bias is None:
                    s.activation(dst, slot, fn)
                else:
                    s.activation(dst, slot, fn, bias=bias)
                ct.incd("S", s)
            slot_tick[sl] = ct.n["S"]


def _emit_segment(P, seg, ticks):
    STG = getattr(P, "stage", 99)
    nc, blk, ct, sb, dram = P.nc, P.blk, P.ct, P.sb, P.dram
    ps0, ps1, ps2, psT = P.ps0, P.ps1, P.ps2, P.psT
    xseg, lnst, xnT = sb["xseg"], sb["lnst"], sb["xnT"]
    qpT, kpT, vpT = sb["qpT"], sb["kpT"], sb["vpT"]
    sqb, sqb2 = sb["sqb"], sb["sqb2"]
    qnT, knT, kbnT, vbT = sb["qnT"], sb["knT"], sb["kbnT"], sb["vbT"]
    vtok, ktok = sb["vtok"], sb["ktok"]
    beta, betaT, btb = sb["beta"], sb["betaT"], sb["btb"]
    ssq, rsn = sb["ssq"], sb["rsn"]
    cw_conv, cb_conv = sb["cw_conv"], sb["cb_conv"]
    ec, esq = sb["ec"], sb["esq"]
    wqkvg, wb_s = sb["wqkvg"], sb["wb_s"]
    xn_st = sb["xn_st"]

    @blk.gpsimd
    def _(gp):
        if seg == 0:
            gp.wait_ge(ct.sems["DX"], ticks["const"])
            gp.wait_ge(ct.sems["V"], ticks["zinit"])
        else:
            ct.wait(gp, "V")
        src = dram["x_full"][seg * SEG:(seg + 1) * SEG, :].rearrange("(i p) d -> p i d", p=128)
        ct.dmaq("DW", gp, out=xseg[:], in_=src)
    x_tick = ct.n["DW"]

    @blk.vector
    def _(v):
        v.wait_ge(ct.sems["DW"], x_tick)
        if seg > 0:
            ct.wait(v, "T")
        for i in range(CPS):
            xt = xseg[:, i]
            v.bn_stats(lnst[:, 12 * i:12 * i + 6], xt[:, 0:512])
            v.bn_stats(lnst[:, 12 * i + 6:12 * i + 12], xt[:, 512:1024])
        v.drain()
        for i in range(CPS):
            v.bn_aggr(lnst[:, 24 + 2 * i:26 + 2 * i], lnst[:, 12 * i:12 * i + 12])
        v.drain()
        for i in range(CPS):
            v.tensor_scalar_add(lnst[:, 28 + i:29 + i], lnst[:, 25 + 2 * i:26 + 2 * i], 1e-5)
        v.drain()
        for i in range(CPS):
            v.reciprocal(lnst[:, 30 + i:31 + i], lnst[:, 28 + i:29 + i])
        ct.incd("V", v)
    lnv_tick = ct.n["V"]

    @blk.scalar
    def _(s):
        s.wait_ge(ct.sems["V"], lnv_tick)
        for i in range(CPS):
            s.activation(lnst[:, 28 + i:29 + i], lnst[:, 30 + i:31 + i], AF.Sqrt)
        ct.incd("S", s)
    rstd_tick = ct.n["S"]

    @blk.vector
    def _(v):
        v.wait_ge(ct.sems["S"], rstd_tick)
        for i in range(CPS):
            v.tensor_scalar(xn_st[:, i], xseg[:, i],
                            lnst[:, 24 + 2 * i:25 + 2 * i], lnst[:, 28 + i:29 + i],
                            op0=OP.subtract, op1=OP.mult)
        ct.incd("V", v)
    xn_tick = ct.n["V"]

    if STG < 2:
        return
    @blk.tensor
    def _(t):
        if seg == 0:
            t.wait_ge(ct.sems["DW"], ticks["l0w"])
        t.nop()
    P.tr_seq([(xn_st[:, i, m * 128:(m + 1) * 128], xnT[:, m, i * 128:(i + 1) * 128], None, None)
              for i in range(CPS) for m in range(8)], pre=("V", xn_tick))
    xnT_tick = ct.n["S"]

    if STG < 3:
        return
    slots = [ps0[:, 0:256], ps0[:, 512:768], ps1[:, 0:256], ps1[:, 512:768]]
    slot_tick = [0, 0, 0, 0]
    kproj = 0
    for mi in range(3):
        for m in range(8):
            sl = kproj % 4
            kproj += 1
            pst = slots[sl]

            @blk.tensor
            def _(t, mi=mi, m=m, pst=pst, sl=sl, first=(mi == 0 and m == 0)):
                if first:
                    t.wait_ge(ct.sems["S"], xnT_tick)
                    ct.wait(t, "V")
                if slot_tick[sl]:
                    t.wait_ge(ct.sems["S"], slot_tick[sl])
                for kc in range(8):
                    t.matmul(pst, wqkvg[:, mi, kc, m * 128:(m + 1) * 128], xnT[:, kc, :],
                             start=(kc == 0), stop=(kc == 7))
                ct.incd("T", t)
            t_tick = ct.n["T"]

            @blk.scalar
            def _(s, mi=mi, m=m, pst=pst, t_tick=t_tick):
                s.wait_ge(ct.sems["T"], t_tick)
                s.copy((qpT, kpT, vpT)[mi][:, m, 3:3 + SEG], pst)
                ct.incd("S", s)
            slot_tick[sl] = ct.n["S"]
    proj_tick = ct.n["S"]

    if STG < 4:
        return
    bslot_tick = 0
    for i in range(CPS):
        @blk.tensor
        def _(t, i=i):
            if bslot_tick:
                t.wait_ge(ct.sems["S"], bslot_tick)
            else:
                t.wait_ge(ct.sems["S"], xnT_tick)
            for kc in range(8):
                t.matmul(ps2[:, 0:16], xnT[:, kc, i * 128:(i + 1) * 128], wb_s[:, kc],
                         start=(kc == 0), stop=(kc == 7))
            ct.incd("T", t)
        t_tick = ct.n["T"]

        @blk.scalar
        def _(s, i=i, t_tick=t_tick):
            s.wait_ge(ct.sems["T"], t_tick)
            s.activation(beta[:, i], ps2[:, 0:16], AF.Sigmoid)
            ct.incd("S", s)
        bslot_tick = ct.n["S"]

    if STG < 5:
        return
    P.tr_seq([(beta[:, i], betaT[:, i * 128:(i + 1) * 128], None, None) for i in range(CPS)],
             ident_ap=sb["identf_s"][:, :], f32psum=True, pre=("S", bslot_tick))
    betaT_tick = ct.n["S"]

    @blk.vector
    def _(v):
        v.wait_ge(ct.sems["S"], betaT_tick)
        v.tensor_copy(btb[:, :], betaT[:, :])
        ct.incd("V", v)
    btb_tick = ct.n["V"]

    if STG < 6:
        return
    cacc8 = sb["cacc8"]
    sil_tick = 0
    for mi, dst in ((0, qnT), (1, knT), (2, vbT)):
        src = (qpT, kpT, vpT)[mi]

        @blk.vector
        def _(v, mi=mi, src=src, sil_tick=sil_tick, first=(mi == 0)):
            if first:
                v.wait_ge(ct.sems["S"], proj_tick)
            if sil_tick:
                v.wait_ge(ct.sems["S"], sil_tick)
            for m in range(8):
                v.tensor_scalar_mul(cacc8[:, m], src[:, m, 0:SEG], cw_conv[:, mi, m, 0:1])
            for j in (1, 2, 3):
                v.drain()
                for m in range(8):
                    v.scalar_tensor_tensor(out=cacc8[:, m], in0=src[:, m, j:j + SEG],
                                           scalar=cw_conv[:, mi, m, j:j + 1],
                                           in1=cacc8[:, m], op0=OP.mult, op1=OP.add)
            ct.incd("V", v)
        v_tick = ct.n["V"]

        @blk.scalar
        def _(s, mi=mi, dst=dst, v_tick=v_tick):
            s.wait_ge(ct.sems["V"], v_tick)
            for m in range(8):
                s.activation(dst[:, m, :], cacc8[:, m], AF.Silu, bias=cb_conv[:, mi, m:m + 1])
            ct.incd("S", s)
        sil_tick = ct.n["S"]
    conv_tick = ct.n["S"]

    if STG < 7:
        return
    sq_slot = [0, 0]
    for qi, srcb in ((0, qnT), (1, knT)):
        pdst = ps2[0:16, 0:256] if qi == 0 else ps2[0:16, 512:768]
        for m in range(8):
            @blk.scalar
            def _(s, qi=qi, m=m, srcb=srcb, w=sq_slot[m % 2], first=(qi == 0 and m == 0)):
                if first:
                    s.drain()
                if w:
                    s.wait_ge(ct.sems["T"], w)
                sq = (sqb, sqb2)[m % 2]
                s.activation(sq[:, :], srcb[:, m, :], AF.Square)
                ct.incd("S", s)
            s_tick = ct.n["S"]

            @blk.tensor
            def _(t, qi=qi, m=m, pdst=pdst, s_tick=s_tick):
                t.wait_ge(ct.sems["S"], s_tick)
                sq = (sqb, sqb2)[m % 2]
                t.matmul(pdst, esq[:, m], sq[:, :], start=(m == 0), stop=(m == 7))
                ct.incd("T", t)
            sq_slot[m % 2] = ct.n["T"]
    ssq_mm_tick = ct.n["T"]

    @blk.vector
    def _(v):
        v.wait_ge(ct.sems["T"], ssq_mm_tick)
        v.tensor_scalar_add(ssq[:, 0, :], ps2[0:16, 0:256], 1e-6)
        v.tensor_scalar_add(ssq[:, 1, :], ps2[0:16, 512:768], 1e-6)
        v.drain()
        v.reciprocal(ssq[:, 0, :], ssq[:, 0, :])
        v.reciprocal(ssq[:, 1, :], ssq[:, 1, :])
        ct.incd("V", v)
    rs_tick = ct.n["V"]

    @blk.scalar
    def _(s):
        s.wait_ge(ct.sems["V"], rs_tick)
        s.activation(rsn[:, 0, :], ssq[:, 0, :], AF.Sqrt, scale=1.0 / 64.0)
        s.activation(rsn[:, 1, :], ssq[:, 1, :], AF.Sqrt)
        ct.incd("S", s)
    rsn_tick = ct.n["S"]

    if STG < 8:
        return
    rep_tick = [0, 0]
    for m in range(8):
        pq = (ps0, ps1)[m % 2]

        @blk.tensor
        def _(t, m=m, pq=pq):
            if m == 0:
                t.wait_ge(ct.sems["S"], rsn_tick)
                ct.wait(t, "V")
            if rep_tick[m % 2]:
                t.wait_ge(ct.sems["V"], rep_tick[m % 2])
            t.matmul(pq[:, 0:256], ec[:, m], rsn[:, 0, :], start=True, stop=True)
            t.matmul(pq[:, 256:512], ec[:, m], rsn[:, 1, :], start=True, stop=True)
            t.matmul(pq[:, 512:768], ec[:, m], btb[:, :], start=True, stop=True)
            ct.incd("T", t)
        t_tick = ct.n["T"]

        @blk.vector
        def _(v, m=m, pq=pq, t_tick=t_tick):
            v.wait_ge(ct.sems["T"], t_tick)
            v.tensor_tensor(qnT[:, m, :], qnT[:, m, :], pq[:, 0:256], op=OP.mult)
            v.tensor_tensor(knT[:, m, :], knT[:, m, :], pq[:, 256:512], op=OP.mult)
            v.tensor_tensor(vbT[:, m, :], vbT[:, m, :], pq[:, 512:768], op=OP.mult)
            v.drain()
            v.scalar_tensor_tensor(out=kbnT[:, m, :], in0=knT[:, m, :], scalar=-1.0,
                                   in1=pq[:, 512:768], op0=OP.mult, op1=OP.mult)
            ct.incd("V", v)
        rep_tick[m % 2] = ct.n["V"]
    pres_tick = ct.n["V"]

    if STG < 9:
        return
    @blk.sync
    def _(sy):
        sy.wait_ge(ct.sems["V"], pres_tick)
        for nm in ("qnT", "knT", "kbnT"):
            ct.dmaq("DX", sy, out=sb[nm + "0"][:, :, :], in_=sb[nm][64:128, :, :])
    h0_tick = ct.n["DX"]

    P.tr_seq([(vbT[:, m, i * 128:(i + 1) * 128], vtok[:, i, m * 128:(m + 1) * 128], None, None)
              for i in range(CPS) for m in range(8)] +
             [(knT[:, m, i * 128:(i + 1) * 128], ktok[:, i, m * 128:(m + 1) * 128], None, None)
              for i in range(CPS) for m in range(8)], pre=("V", pres_tick))
    vk_tick = ct.n["S"]

    if STG < 10:
        return
    if P.do_chunks:
        for i in range(CPS):
            _emit_chunk(P, seg * CPS + i, i, vk_tick, h0_tick)

    @blk.vector
    def _(v):
        ct.wait(v, "T")
        for srcb in (qpT, kpT, vpT):
            v.tensor_copy(srcb[:, :, 0:3], srcb[:, :, SEG:SEG + 3])
        ct.incd("V", v)


def _emit_chunk(P, ci, i, vk_tick, h0_tick):
    CS = getattr(P, "cstage", 99)
    nc, blk, ct, sb = P.nc, P.blk, P.ct, P.sb
    ps0, ps1, ps2, psT = P.ps0, P.ps1, P.ps2, P.psT
    qnT, knT, kbnT = sb["qnT"], sb["knT"], sb["kbnT"]
    qnT0, knT0, kbnT0 = sb["qnT0"], sb["knT0"], sb["kbnT0"]
    vtok, ktok = sb["vtok"], sb["ktok"]
    S_f32, S_bf = sb["S_f32"], sb["S_bf"]
    Xs = sb["Xs"]
    XTb = [sb["XT0"], sb["XT1"]]
    Yv = [sb["Ya"], sb["Yb"]]
    cb_ = sb["cacc8"][:, 4:6, :]
    Ubf = bass.AP(cb_.tensor, cb_.offset, [list(cb_.ap[0]), [1, 512]]).bitcast(BF16)
    MTb, rsc, owin = sb["MTb"], sb["rsc"], sb["owin"]
    ca = sb["cacc8"][:, 0:4, :]
    onrm = bass.AP(ca.tensor, ca.offset, [list(ca.ap[0]), [1, 1024]])
    ltm, utm, utim = sb["ltm"], sb["utm"], sb["utim"]
    nwrep, maskO, maskH = sb["nwrep"], sb["maskO_s"], sb["maskH_s"]

    pairs = {id(qnT): qnT0, id(knT): knT0, id(kbnT): kbnT0}

    def hsl(buf, h):
        if h % 2 == 0:
            return buf[0:64, h // 2, i * 128:(i + 1) * 128]
        return pairs[id(buf)][0:64, h // 2, i * 128:(i + 1) * 128]

    def Ssl(buf, h):
        return buf[0:64, h, :]

    def xsl(buf, h):
        return buf[:, 128 * h:128 * h + 128]

    def ysl(buf, h):
        if isinstance(buf, bass.AP):
            return bass.AP(buf.tensor, buf.offset + 64 * h * buf.ap[-1][0],
                           [list(buf.ap[0]), [buf.ap[-1][0], 64]])
        return buf[:, 64 * h:64 * h + 64]

    @blk.tensor
    def _(t):
        t.wait_ge(ct.sems["S"], vk_tick)
        t.wait_ge(ct.sems["DX"], h0_tick)
        ct.wait(t, "V")
        for h in range(H):
            t.matmul(ps0[:, 64 * h:64 * h + 64], hsl(kbnT, h), Ssl(S_bf, h),
                     start=True, stop=True)
        ct.incd("T", t)
    r_tick = ct.n["T"]

    Y0MODE = getattr(P, "y0mode", 2)
    @blk.vector
    def _(v):
        v.wait_ge(ct.sems["T"], r_tick)
        if Y0MODE == 2:
            v.scalar_tensor_tensor(out=r32(Yv[0][:, :]), in0=ps0[:, :], scalar=1.0,
                                   in1=vtok[:, i, :], op0=OP.mult, op1=OP.add)
        elif Y0MODE == 1:
            v.scalar_tensor_tensor(out=Yv[0][:, :], in0=ps0[:, :], scalar=1.0,
                                   in1=vtok[:, i, :], op0=OP.mult, op1=OP.add)
        ct.incd("V", v)
    y0_tick = ct.n["V"]

    if CS < 2:
        return
    @blk.tensor
    def _(t):
        ct.wait(t, "V")
        for h in range(H):
            pst = (ps1, ps2)[h // 8]
            t.matmul(pst[:, 128 * (h % 8):128 * (h % 8) + 128], hsl(kbnT, h), hsl(knT, h),
                     start=True, stop=True)
        ct.incd("T", t)
    xb_tick = ct.n["T"]

    @blk.vector
    def _(v):
        v.wait_ge(ct.sems["T"], xb_tick)
        for pi, pst in enumerate((ps1, ps2)):
            v.tensor_tensor(r32(split3(Xs[:, 1024 * pi:1024 * (pi + 1)], 8, 128)),
                            split3(pst[:, :], 8, 128),
                            rep_ap(ltm[:, :], 8, 1), op=OP.mult)
        ct.incd("V", v)
    xev_tick = ct.n["V"]

    if CS < 3:
        return
    @blk.tensor
    def _(t):
        t.wait_ge(ct.sems["V"], xev_tick)
        for h in range(H):
            pst = (ps1, ps2)[h // 8]
            t.matmul(pst[:, 128 * (h % 8):128 * (h % 8) + 128], hsl(knT, h), hsl(kbnT, h),
                     start=True, stop=True)
        ct.incd("T", t)
    xtb_tick = ct.n["T"]

    @blk.vector
    def _(v):
        v.wait_ge(ct.sems["T"], xtb_tick)
        for pi, pst in enumerate((ps1, ps2)):
            v.tensor_tensor(r32(split3(XTb[0][:, 1024 * pi:1024 * (pi + 1)], 8, 128)),
                            split3(pst[:, :], 8, 128),
                            rep_ap(utm[:, :], 8, 1), op=OP.mult)
        ct.incd("V", v)
    xtev_tick = ct.n["V"]

    if CS < 4:
        return
    ycur = 0
    xtc = 0
    for j in range(min(NLVL, CS - 3) if CS < 99 else NLVL):
        @blk.tensor
        def _(t, xtc=xtc, ycur=ycur, j=j):
            ct.wait(t, "V")
            if j > 0:
                ct.wait(t, "S")
            for h in range(H):
                t.matmul(ps0[:, 64 * h:64 * h + 64], r32(xsl(XTb[xtc], h)),
                         r32(ysl(Yv[ycur], h)), start=True, stop=True)
            ct.incd("T", t)
        sv_tick = ct.n["T"]

        @blk.vector
        def _(v, ycur=ycur, sv_tick=sv_tick):
            v.wait_ge(ct.sems["T"], sv_tick)
            v.drain()
            v.scalar_tensor_tensor(out=r32(Yv[1 - ycur][:, :]), in0=ps0[:, :], scalar=1.0,
                                   in1=Yv[ycur][:, :], op0=OP.mult, op1=OP.add)
            ct.incd("V", v)
        ycur = 1 - ycur

        if j < 6:
            @blk.tensor
            def _(t, xtc=xtc):
                ct.wait(t, "V")
                ct.wait(t, "S")
                for h in range(H):
                    pst = (ps1, ps2)[h // 8]
                    t.matmul(pst[:, 128 * (h % 8):128 * (h % 8) + 128],
                             r32(xsl(Xs, h)), r32(xsl(XTb[xtc], h)),
                             start=True, stop=True)
                ct.incd("T", t)
            xt2_tick = ct.n["T"]

            if j < 5:
                @blk.scalar
                def _(s, xtc=xtc, xt2_tick=xt2_tick):
                    s.wait_ge(ct.sems["T"], xt2_tick)
                    s.copy(r32(XTb[1 - xtc][:, 0:1024]), ps1[:, :])
                    s.copy(r32(XTb[1 - xtc][:, 1024:2048]), ps2[:, :])
                    ct.incd("S", s)
                xtn_tick = ct.n["S"]

                @blk.tensor
                def _(t, xtc=xtc, xtn_tick=xtn_tick):
                    t.wait_ge(ct.sems["S"], xtn_tick)
                    for h in range(H):
                        pst = (ps1, ps2)[h // 8]
                        t.matmul(pst[:, 128 * (h % 8):128 * (h % 8) + 128],
                                 r32(xsl(XTb[xtc], h)), r32(xsl(Xs, h)),
                                 start=True, stop=True)
                    ct.incd("T", t)
                x2_tick = ct.n["T"]

                @blk.scalar
                def _(s, x2_tick=x2_tick):
                    s.wait_ge(ct.sems["T"], x2_tick)
                    s.copy(r32(Xs[:, 0:1024]), ps1[:, :])
                    s.copy(r32(Xs[:, 1024:2048]), ps2[:, :])
                    ct.incd("S", s)
            else:
                @blk.scalar
                def _(s, xtc=xtc, xt2_tick=xt2_tick):
                    s.wait_ge(ct.sems["T"], xt2_tick)
                    s.copy(r32(XTb[1 - xtc][:, 0:1024]), ps1[:, :])
                    s.copy(r32(XTb[1 - xtc][:, 1024:2048]), ps2[:, :])
                    ct.incd("S", s)
            xtc = 1 - xtc

    if CS < 11:
        return
    yfin = ycur

    @blk.vector
    def _(v):
        v.drain()
        v.tensor_copy(Ubf, Yv[yfin][:, :])
        ct.incd("V", v)
    ubf_tick = ct.n["V"]

    if CS < 12:
        return
    @blk.tensor
    def _(t):
        ct.wait(t, "S")
        ct.wait(t, "V")
        for h in range(H):
            pst = (ps1, ps2)[h // 8]
            t.matmul(pst[:, 128 * (h % 8):128 * (h % 8) + 128], hsl(knT, h), hsl(qnT, h),
                     start=True, stop=True)
        ct.incd("T", t)
    mt_tick = ct.n["T"]

    @blk.vector
    def _(v):
        v.wait_ge(ct.sems["T"], mt_tick)
        for pi, pst in enumerate((ps1, ps2)):
            v.tensor_tensor(split3(MTb[:, 1024 * pi:1024 * (pi + 1)], 8, 128),
                            split3(pst[:, :], 8, 128),
                            rep_ap(utim[:, :], 8, 1), op=OP.mult)
        ct.incd("V", v)
    mtev_tick = ct.n["V"]

    if CS < 13:
        return
    @blk.tensor
    def _(t):
        t.wait_ge(ct.sems["V"], mtev_tick)
        for h in range(H):
            t.matmul(ps0[:, 64 * h:64 * h + 64], MTb[:, 128 * h:128 * h + 128],
                     ysl(Ubf, h), start=True, stop=False)
            t.matmul(ps0[:, 64 * h:64 * h + 64], hsl(qnT, h), Ssl(S_bf, h),
                     start=False, stop=True)
        ct.incd("T", t)
    o_tick = ct.n["T"]

    if CS < 14:
        return
    @blk.tensor
    def _(t):
        for h in range(H):
            t.matmul(ps1[0:64, 64 * h:64 * h + 64], ktok[:, i, 64 * h:64 * h + 64],
                     ysl(Ubf, h), start=True, stop=True)
        ct.incd("T", t)
    su_tick = ct.n["T"]

    @blk.vector
    def _(v):
        v.wait_ge(ct.sems["T"], su_tick)
        v.scalar_tensor_tensor(out=S_f32[:, :, :], in0=split3(ps1[0:64, :], H, HD),
                               scalar=1.0, in1=S_f32[:, :, :], op0=OP.mult, op1=OP.add)
        v.drain()
        v.tensor_copy(S_bf[:, :, :], S_f32[:, :, :])
        ct.incd("V", v)

    if CS < 15:
        return
    @blk.scalar
    def _(s):
        s.wait_ge(ct.sems["T"], o_tick)
        s.activation(onrm, ps0[:, :], AF.Square)
        ct.incd("S", s)
    osq_tick = ct.n["S"]

    @blk.vector
    def _(v):
        v.wait_ge(ct.sems["S"], osq_tick)
        v.tensor_reduce(rsc[:, 0:16], split3(onrm, 16, 64), axis=AX.X, op=OP.add)
        v.drain()
        v.tensor_scalar(rsc[:, 16:32], rsc[:, 0:16], 1.0 / 64.0, 1e-5,
                        op0=OP.mult, op1=OP.add)
        v.drain()
        v.reciprocal(rsc[:, 16:32], rsc[:, 16:32])
        ct.incd("V", v)
    rms1_tick = ct.n["V"]

    @blk.scalar
    def _(s):
        s.wait_ge(ct.sems["V"], rms1_tick)
        s.activation(rsc[:, 32:48], rsc[:, 16:32], AF.Sqrt)
        ct.incd("S", s)
    rms2_tick = ct.n["S"]

    @blk.vector
    def _(v):
        v.wait_ge(ct.sems["S"], rms2_tick)
        v.tensor_tensor(split3(onrm, 16, 64), split3(ps0[:, :], 16, 64),
                        rep_ap(rsc[:, 32:48], 64, 2), op=OP.mult)
        v.drain()
        v.tensor_tensor(onrm, onrm, nwrep[:, :], op=OP.mult)
        v.drain()
        slot = 1 + (ci % 16)
        if ci < 16:
            v.tensor_scalar_mul(owin[:, slot, :], onrm, maskO[:, ci:ci + 1])
        else:
            v.scalar_tensor_tensor(out=owin[:, slot, :], in0=onrm,
                                   scalar=maskO[:, ci:ci + 1], in1=owin[:, slot, :],
                                   op0=OP.mult, op1=OP.add)
        if ci == 15:
            v.tensor_scalar_mul(owin[:, 0, :], onrm, maskH[:, ci:ci + 1])
        ct.incd("V", v)


def build_program(dbg=(), n_seg=NSEG, do_chunks=True, stage=99, cstage=99, y0mode=2, rh=None,
                  p2=True, p3=True, p45=True):
    nc = bass.Bass("TRN2", target_bir_lowering=False, debug=False)
    dram = {}

    def din(name, shape, dtype=F32):
        dram[name] = nc.dram_tensor(name, list(shape), dtype, kind="ExternalInput").ap()
        return dram[name]

    din("x_full", (T, D)); din("x_win", (WIN, D))
    din("maskO", (128, NCH)); din("maskH", (128, NCH))
    for nm in ("Wq", "Wk", "Wv", "Wg", "Wo"):
        din(nm, (D, D))
    din("Wb", (D, H))
    for nm in ("cqw", "ckw", "cvw"):
        din(nm, (128, 8, 4))
    for nm in ("cqb", "ckb", "cvb"):
        din(nm, (128, 8))
    din("nw_rep", (128, D))
    din("centroidsT", (D, E)); din("ebias_rep", (128, E))
    din("Ws_up", (D, FE)); din("Ws_down", (FE, D))
    din("W_up", (E, D, FE)); din("W_down", (E, FE, D))
    din("w_linT", (D, 496)); din("W1T", (D, HFF)); din("W2T", (HFF, D))
    din("b1", (1, HFF)); din("b2", (1, D))
    din("ident128", (128, 128), BF16); din("identf", (128, 128))
    din("ltmask", (128, 128), BF16); din("utmask", (128, 128), BF16)
    din("utimask", (128, 128), BF16)
    din("ones_col", (1, 128))
    din("Ec", (16, 8, 128), BF16); din("Esq", (128, 8, 16), BF16)

    y = nc.dram_tensor("y", [2048, D], F32, kind="ExternalOutput").ap()
    stag = nc.dram_tensor("stag", [H, 16 * 128 * 159], BF16, kind="Internal").ap()
    hbuf = nc.dram_tensor("hbuf", [2048, HFF], BF16,
                          kind="ExternalOutput" if any(d[0] == "tap_h" for d in dbg)
                          else "Internal").ap()
    dbg_out = {}
    for (nm, shape, dt) in dbg:
        if shape is not None:
            dbg_out[nm] = nc.dram_tensor(nm, list(shape), dt, kind="ExternalOutput").ap()

    ct = Ctr(nc)
    stk = contextlib.ExitStack()
    sb = {}

    def sbuf(name, shape, dtype, st=None):
        tlocal = (st or stk).enter_context(nc.sbuf_tensor(name, list(shape), dtype))
        sb[name] = tlocal
        return tlocal

    with stk:
        ct.make(stk, ["T", "V", "S", "G", "DX", "DW", "DS", "DO"])
        sbuf("ident", (128, 128), BF16)
        sbuf("identf_s", (128, 128), F32)
        sbuf("ltm", (128, 128), BF16)
        sbuf("utm", (128, 128), BF16)
        sbuf("utim", (128, 128), BF16)
        sbuf("onesc", (1, 128), F32)
        sbuf("ec", (16, 8, 128), BF16)
        sbuf("esq", (128, 8, 16), BF16)
        sbuf("nwrep", (128, D), F32)
        sbuf("maskO_s", (128, NCH), F32)
        sbuf("maskH_s", (128, NCH), F32)
        sbuf("magic", (128, 1), F32)
        sbuf("S_f32", (64, H, HD), F32)
        sbuf("S_bf", (64, H, HD), BF16)
        sbuf("cw_conv", (128, 3, 8, 4), F32)
        sbuf("cb_conv", (128, 3, 8), F32)

        ps0 = stk.enter_context(nc.psum_tensor("ps0", [128, 1024], F32))
        ps1 = stk.enter_context(nc.psum_tensor("ps1", [128, 1024], F32))
        ps2 = stk.enter_context(nc.psum_tensor("ps2", [128, 1024], F32))
        psT = stk.enter_context(nc.psum_tensor("psT", [128, 2048], BF16))

        blk = stk.enter_context(nc.Block())
        P = Prog(nc, blk, ct, sb, (ps0, ps1, ps2, psT), dram, dbg_out)
        P.do_chunks = do_chunks
        P.stage = stage
        P.cstage = cstage
        P.y0mode = y0mode
        if rh is not None:
            P.rh = rh

        ow = contextlib.ExitStack()
        sb["owin"] = ow.enter_context(nc.sbuf_tensor("owin", [128, NW, D], BF16, side="right"))

        @blk.sync
        def _(sy):
            for dst, src in (("ident", "ident128"), ("identf_s", "identf"),
                             ("ltm", "ltmask"), ("utm", "utmask"), ("utim", "utimask"),
                             ("onesc", "ones_col"), ("ec", "Ec"), ("esq", "Esq"),
                             ("nwrep", "nw_rep"), ("maskO_s", "maskO"), ("maskH_s", "maskH")):
                ct.dmaq("DX", sy, out=sb[dst][:], in_=dram[src][:])
            for i, nm in enumerate(("cqw", "ckw", "cvw")):
                ct.dmaq("DX", sy, out=sb["cw_conv"][:, i], in_=dram[nm][:])
            for i, nm in enumerate(("cqb", "ckb", "cvb")):
                ct.dmaq("DX", sy, out=sb["cb_conv"][:, i], in_=dram[nm][:])
        ticks = {"const": ct.n["DX"]}

        with contextlib.ExitStack() as p1:
            sbuf("wqkvg", (128, 3, 8, D), BF16, p1)
            sbuf("wb_s", (128, 8, H), BF16, p1)
            sbuf("xseg", (128, CPS, D), BF16, p1)
            sbuf("lnst", (128, 32), F32, p1)
            sbuf("xn_st", (128, CPS, D), BF16, p1)
            sbuf("xnT", (128, 8, SEG), BF16, p1)
            sbuf("qpT", (128, 8, 3 + SEG), BF16, p1)
            sbuf("kpT", (128, 8, 3 + SEG), BF16, p1)
            sbuf("vpT", (128, 8, 3 + SEG), BF16, p1)
            sbuf("cacc8", (128, 8, SEG), F32, p1)
            sbuf("sqb", (128, SEG), BF16, p1)
            sbuf("sqb2", (128, SEG), BF16, p1)
            sbuf("qnT", (128, 8, SEG), BF16, p1)
            sbuf("knT", (128, 8, SEG), BF16, p1)
            sbuf("kbnT", (128, 8, SEG), BF16, p1)
            sbuf("qnT0", (64, 8, SEG), BF16, p1)
            sbuf("knT0", (64, 8, SEG), BF16, p1)
            sbuf("kbnT0", (64, 8, SEG), BF16, p1)
            sbuf("vbT", (128, 8, SEG), BF16, p1)
            sbuf("vtok", (128, CPS, D), BF16, p1)
            sbuf("ktok", (128, CPS, D), BF16, p1)
            sbuf("beta", (128, CPS, H), F32, p1)
            sbuf("betaT", (16, SEG), F32, p1)
            sbuf("btb", (16, SEG), BF16, p1)
            sbuf("ssq", (16, 2, SEG), F32, p1)
            sbuf("rsn", (16, 2, SEG), BF16, p1)
            sbuf("Xs", (128, H * 128), F32, p1)
            sbuf("XT0", (128, H * 128), F32, p1)
            sbuf("XT1", (128, H * 128), F32, p1)
            sbuf("MTb", (128, H * 128), BF16, p1)
            sbuf("Ya", (128, H * HD), F32, p1)
            sbuf("Yb", (128, H * HD), F32, p1)
            sbuf("rsc", (128, 48), F32, p1)

            @blk.gpsimd
            def _(gp):
                for mi, nm in enumerate(("Wq", "Wk", "Wv")):
                    src = dram[nm].rearrange("(kc kp) d -> kp kc d", kp=128)
                    ct.dmaq("DW", gp, out=sb["wqkvg"][:, mi], in_=src)
                ct.dmaq("DW", gp, 
                    out=sb["wb_s"][:], in_=dram["Wb"].rearrange("(kc kp) h -> kp kc h", kp=128))
            ticks["l0w"] = ct.n["DW"]

            @blk.vector
            def _(v):
                v.memset(sb["magic"][:], MAGIC)
                v.memset(sb["S_f32"][:], 0.0)
                v.memset(sb["S_bf"][:], 0.0)
                v.memset(sb["owin"][:], 0.0)
                v.memset(sb["qpT"][:, :, 0:3], 0.0)
                v.memset(sb["kpT"][:, :, 0:3], 0.0)
                v.memset(sb["vpT"][:, :, 0:3], 0.0)
                ct.incd("V", v)
            ticks["zinit"] = ct.n["V"]

            for seg in range(n_seg):
                with nc.named_scope(f"seg{seg}"):
                    _emit_segment(P, seg, ticks)

            if "tap_owin" in dbg_out:
                @blk.sync
                def _(sy):
                    ct.wait(sy, "V")
                    ct.dmaq("DX", sy, 
                        out=dbg_out["tap_owin"].rearrange("(i p) d -> p i d", p=128),
                        in_=sb["owin"][:])

        sbuf("x1acc", (128, NW, D), F32)
        with contextlib.ExitStack() as p23:
            sbuf("xnwT", (128, 8, NW * 128), BF16, p23)
            if p2:
                with nc.named_scope("phase2"):
                    _emit_phase2(P)
            if "tap_p2" in dbg_out:
                @blk.sync
                def _(sy):
                    ct.wait(sy, "V")
                    ct.dmaq("DX", sy, 
                        out=dbg_out["tap_p2"].rearrange("(i p) d -> p i d", p=128),
                        in_=sb["x1acc"][:])
                p2tap = ct.n["DX"]

                @blk.vector
                def _(v):
                    v.wait_ge(ct.sems["DX"], p2tap)
                    ct.incd("V", v)
            ow.close()
            if p3:
                with nc.named_scope("phase3"):
                    _emit_phase3(P)
            if "tap_p3" in dbg_out:
                @blk.sync
                def _(sy):
                    ct.wait(sy, "V")
                    ct.dmaq("DX", sy, 
                        out=dbg_out["tap_p3"].rearrange("(i p) d -> p i d", p=128),
                        in_=sb["x1acc"][:])
                p3tap = ct.n["DX"]

                @blk.vector
                def _(v):
                    v.wait_ge(ct.sems["DX"], p3tap)
                    ct.incd("V", v)
        if p45:
            with nc.named_scope("phase45"):
                _emit_phase45(P, y, stag, hbuf)
        else:
            @blk.sync
            def _(sy):
                ct.wait(sy, "V")
                ct.dmaq("DO", sy, 
                    out=y.rearrange("(i p) d -> p i d", p=128),
                    in_=sb["x1acc"][:, 1:17, :])
                ct.wait(sy, "DO")

    return nc


def _emit_ln_tile(P, src_ap, dst_ap, ln, pre=None):
    blk, ct = P.blk, P.ct

    @blk.vector
    def _(v):
        if pre is not None:
            v.wait_ge(ct.sems[pre[0]], pre[1])
        v.drain()
        v.bn_stats(ln[:, 0:6], src_ap[:, 0:512])
        v.bn_stats(ln[:, 6:12], src_ap[:, 512:1024])
        v.drain()
        v.bn_aggr(ln[:, 12:14], ln[:, 0:12])
        v.drain()
        v.tensor_scalar_add(ln[:, 14:15], ln[:, 13:14], 1e-5)
        v.drain()
        v.reciprocal(ln[:, 15:16], ln[:, 14:15])
        ct.incd("V", v)
    t1 = ct.n["V"]

    @blk.scalar
    def _(s):
        s.wait_ge(ct.sems["V"], t1)
        s.activation(ln[:, 16:17], ln[:, 15:16], AF.Sqrt)
        ct.incd("S", s)
    t2 = ct.n["S"]

    @blk.vector
    def _(v):
        v.wait_ge(ct.sems["S"], t2)
        v.tensor_scalar(dst_ap, src_ap, ln[:, 12:13], ln[:, 16:17],
                        op0=OP.subtract, op1=OP.mult)
        ct.incd("V", v)
    return ct.n["V"]


def _phase_barrier(P, eng):
    ct = P.ct
    ct.wait(eng, "T")
    ct.wait(eng, "V")
    ct.wait(eng, "S")


def _emit_phase2(P):
    nc, blk, ct, sb, dram = P.nc, P.blk, P.ct, P.sb, P.dram
    ps0, ps1, ps2, psT = P.ps0, P.ps1, P.ps2, P.psT
    x1acc, xnwT, owin = sb["x1acc"], sb["xnwT"], sb["owin"]

    with contextlib.ExitStack() as p2:
        wg = p2.enter_context(nc.sbuf_tensor("wg_s", [128, 8, D], BF16))
        wo = p2.enter_context(nc.sbuf_tensor("wo_s", [128, 8, D], BF16))
        xnb = p2.enter_context(nc.sbuf_tensor("xnb", [128, 2, D], BF16))
        sgt = p2.enter_context(nc.sbuf_tensor("sgt", [128, 2, D], BF16))
        otT = p2.enter_context(nc.sbuf_tensor("otT", [128, 2, 8, 128], BF16))
        ln2 = p2.enter_context(nc.sbuf_tensor("ln2", [128, 32], F32))

        @blk.sync
        def _(sy):
            _phase_barrier(P, sy)
            ct.dmaq("DX", sy, 
                out=x1acc[:], in_=dram["x_win"].rearrange("(i p) d -> p i d", p=128))
        xw_tick = ct.n["DX"]

        @blk.gpsimd
        def _(gp):
            _phase_barrier(P, gp)
            for dst, nm in ((wg, "Wg"), (wo, "Wo")):
                ct.dmaq("DW", gp, 
                    out=dst[:], in_=dram[nm].rearrange("(kc kp) d -> kp kc d", kp=128))
        w2_tick = ct.n["DW"]

        xnb_tick = [0, 0]
        sg_tick = [0, 0]
        ot_tick = [0, 0]
        for i in range(NW):
            sl = i % 2
            pre = ("DX", xw_tick) if i == 0 else ("T", xnb_tick[sl])
            ln_tick = _emit_ln_tile(P, x1acc[:, i, :], xnb[:, sl, :], ln2, pre=pre)

            P.tr_seq([(xnb[:, sl, m * 128:(m + 1) * 128],
                       xnwT[:, m, i * 128:(i + 1) * 128], None, None) for m in range(8)],
                     pre=("V", ln_tick))
            xnb_tick[sl] = ct.n["T"]
            ev_tick = ct.n["S"]

            @blk.tensor
            def _(t, i=i, ev_tick=ev_tick, w=max(sg_tick)):
                t.wait_ge(ct.sems["S"], ev_tick)
                if i == 0:
                    t.wait_ge(ct.sems["DW"], w2_tick)
                    ct.wait(t, "V")
                if w:
                    t.wait_ge(ct.sems["S"], w)
                for half in range(2):
                    for kc in range(8):
                        t.matmul(ps0[:, 512 * half:512 * half + 512],
                                 xnwT[:, kc, i * 128:(i + 1) * 128],
                                 wg[:, kc, 512 * half:512 * half + 512],
                                 start=(kc == 0), stop=(kc == 7))
                ct.incd("T", t)
            g_tick = ct.n["T"]

            @blk.scalar
            def _(s, sl=sl, g_tick=g_tick):
                s.wait_ge(ct.sems["T"], g_tick)
                s.activation(sgt[:, sl, 0:512], ps0[:, 0:512], AF.Silu)
                s.activation(sgt[:, sl, 512:1024], ps0[:, 512:1024], AF.Silu)
                ct.incd("S", s)
            sg_tick[sl] = ct.n["S"]

            @blk.vector
            def _(v, i=i, sl=sl, w=sg_tick[sl]):
                v.wait_ge(ct.sems["S"], w)
                v.tensor_tensor(owin[:, i, :], owin[:, i, :], sgt[:, sl, :], op=OP.mult)
                ct.incd("V", v)
            om_tick = ct.n["V"]

            P.tr_seq([(owin[:, i, m * 128:(m + 1) * 128], otT[:, sl, m, :], None, None)
                      for m in range(8)], pre=("V", om_tick))
            otev_tick = ct.n["S"]

            @blk.tensor
            def _(t, sl=sl, otev_tick=otev_tick, w=ot_tick[sl]):
                t.wait_ge(ct.sems["S"], otev_tick)
                if w:
                    t.wait_ge(ct.sems["V"], w)
                for half in range(2):
                    for m in range(8):
                        t.matmul(ps1[:, 512 * half:512 * half + 512], otT[:, sl, m, :],
                                 wo[:, m, 512 * half:512 * half + 512],
                                 start=(m == 0), stop=(m == 7))
                ct.incd("T", t)
            woT_tick = ct.n["T"]

            @blk.vector
            def _(v, i=i, woT_tick=woT_tick):
                v.wait_ge(ct.sems["T"], woT_tick)
                for half in range(2):
                    v.scalar_tensor_tensor(
                        out=x1acc[:, i, 512 * half:512 * half + 512],
                        in0=ps1[:, 512 * half:512 * half + 512], scalar=1.0,
                        in1=x1acc[:, i, 512 * half:512 * half + 512],
                        op0=OP.mult, op1=OP.add)
                ct.incd("V", v)
            ot_tick[sl] = ct.n["V"]


def _emit_phase3(P):
    nc, blk, ct, sb, dram = P.nc, P.blk, P.ct, P.sb, P.dram
    ps0, ps1, ps2, psT = P.ps0, P.ps1, P.ps2, P.psT
    x1acc, xnwT = sb["x1acc"], sb["xnwT"]
    NT = [512, 512, 512, 512, 128]

    with contextlib.ExitStack() as p3:
        cenT = p3.enter_context(nc.sbuf_tensor("cenT", [128, 8, E], BF16))
        ebias = p3.enter_context(nc.sbuf_tensor("ebias", [128, E], F32))
        wsup = p3.enter_context(nc.sbuf_tensor("wsup", [128, 8, FE], BF16))
        wsdn = p3.enter_context(nc.sbuf_tensor("wsdn", [128, 4, D], BF16))
        aff = p3.enter_context(nc.sbuf_tensor("aff", [128, NW, E], F32))
        cw = p3.enter_context(nc.sbuf_tensor("cw", [128, NW, E], F32))
        hT = p3.enter_context(nc.sbuf_tensor("hT", [128, 4, NW * 128], BF16))
        wu = [p3.enter_context(nc.sbuf_tensor(f"wu{j}", [128, 8, FE], BF16)) for j in range(2)]
        wd = [p3.enter_context(nc.sbuf_tensor(f"wd{j}", [128, 4, D], BF16)) for j in range(2)]
        tkb = p3.enter_context(nc.sbuf_tensor("tkb", [128, E], F32))
        oha = p3.enter_context(nc.sbuf_tensor("oha", [128, 8, E], F32))
        ohs = p3.enter_context(nc.sbuf_tensor("ohs", [128, E], F32))
        tks = p3.enter_context(nc.sbuf_tensor("tks", [128, 32], F32))

        @blk.sync
        def _(sy):
            _phase_barrier(P, sy)
            ct.dmaq("DX", sy, out=ebias[:], in_=dram["ebias_rep"][:])
        eb_tick = ct.n["DX"]

        @blk.gpsimd
        def _(gp):
            _phase_barrier(P, gp)
            ct.dmaq("DW", gp, 
                out=cenT[:], in_=dram["centroidsT"].rearrange("(kc kp) e -> kp kc e", kp=128))
            ct.dmaq("DW", gp, 
                out=wsup[:], in_=dram["Ws_up"].rearrange("(kc kp) f -> kp kc f", kp=128))
            ct.dmaq("DW", gp, 
                out=wsdn[:], in_=dram["Ws_down"].rearrange("(fc fp) d -> fp fc d", fp=128))
        w3_tick = ct.n["DW"]

        aslot_tick = [0, 0]
        for i in range(NW):
            sl = i % 2
            pa = (ps2[0:128, 0:64], ps2[0:128, 512:576])[sl]

            @blk.tensor
            def _(t, i=i, pa=pa, w=aslot_tick[sl]):
                if i == 0:
                    t.wait_ge(ct.sems["DW"], w3_tick)
                    ct.wait(t, "V")
                if w:
                    t.wait_ge(ct.sems["V"], w)
                for kc in range(8):
                    t.matmul(pa, xnwT[:, kc, i * 128:(i + 1) * 128], cenT[:, kc],
                             start=(kc == 0), stop=(kc == 7))
                ct.incd("T", t)
            t_tick = ct.n["T"]

            @blk.vector
            def _(v, i=i, pa=pa, t_tick=t_tick):
                v.wait_ge(ct.sems["T"], t_tick)
                v.tensor_copy(aff[:, i, :], pa)
                ct.incd("V", v)
            aslot_tick[sl] = ct.n["V"]

        for i in range(NW):
            @blk.vector
            def _(v, i=i):
                if i == 0:
                    v.wait_ge(ct.sems["DX"], eb_tick)
                v.drain()
                v.tensor_tensor(tkb[:, :], aff[:, i, :], ebias[:, :], op=OP.add)
                for r in range(8):
                    v.drain()
                    v.reduce_max(tks[:, 0:1], tkb[:, :], axis=AX.X)
                    v.drain()
                    v.tensor_scalar(oha[:, r, :], tkb[:, :], tks[:, 0:1], None, op0=OP.is_equal)
                    v.drain()
                    v.tensor_tensor(ohs[:, :], aff[:, i, :], oha[:, r, :], op=OP.mult)
                    v.drain()
                    v.reduce_sum(tks[:, 8 + r:9 + r], ohs[:, :], axis=AX.X)
                    v.drain()
                    v.scalar_tensor_tensor(out=tkb[:, :], in0=oha[:, r, :], scalar=-1e30,
                                           in1=tkb[:, :], op0=OP.mult, op1=OP.add)
                ct.incd("V", v)
            sel_tick = ct.n["V"]

            @blk.scalar
            def _(s, sel_tick=sel_tick):
                s.wait_ge(ct.sems["V"], sel_tick)
                s.activation(tks[:, 16:24], tks[:, 8:16], AF.Sigmoid)
                ct.incd("S", s)
            sig_tick = ct.n["S"]

            @blk.vector
            def _(v, i=i, sig_tick=sig_tick):
                v.wait_ge(ct.sems["S"], sig_tick)
                v.tensor_reduce(tks[:, 24:25], tks[:, 16:24], axis=AX.X, op=OP.add)
                v.drain()
                v.tensor_scalar_add(tks[:, 24:25], tks[:, 24:25], 1e-8)
                v.drain()
                v.reciprocal(tks[:, 25:26], tks[:, 24:25])
                v.drain()
                v.tensor_scalar(tks[:, 16:24], tks[:, 16:24], tks[:, 25:26], None, op0=OP.mult)
                v.drain()
                v.tensor_scalar_mul(cw[:, i, :], oha[:, 0, :], tks[:, 16:17])
                for r in range(1, 8):
                    v.drain()
                    v.scalar_tensor_tensor(out=cw[:, i, :], in0=oha[:, r, :],
                                           scalar=tks[:, 16 + r:17 + r], in1=cw[:, i, :],
                                           op0=OP.mult, op1=OP.add)
                ct.incd("V", v)
        cw_tick = ct.n["V"]

        hs_tick = 0
        yv_tick = 0
        yT_tick = 0
        wdma_tick = [0, 0]
        pyb = (ps2[:, 0:512], ps2[:, 512:1024])
        for eidx in range(-1, E):
            par = eidx % 2
            if eidx >= 0:
                @blk.gpsimd
                def _(gp, eidx=eidx, par=par, w=wdma_tick[par]):
                    if w:
                        gp.wait_ge(ct.sems["T"], w)
                    ct.dmaq("DW", gp, 
                        out=wu[par][:],
                        in_=dram["W_up"][eidx].rearrange("(kc kp) f -> kp kc f", kp=128))
                    ct.dmaq("DW", gp, 
                        out=wd[par][:],
                        in_=dram["W_down"][eidx].rearrange("(fc fp) d -> fp fc d", fp=128))
                wdma = ct.n["DW"]
                wub, wdb = wu[par], wd[par]
            else:
                wdma = w3_tick
                wub, wdb = wsup, wsdn

            off = 0
            for nt, ntok in enumerate(NT):
                @blk.tensor
                def _(t, eidx=eidx, nt=nt, off=off, ntok=ntok, wdma=wdma,
                      hs_tick=hs_tick, yv_prev=yv_tick):
                    if nt == 0:
                        t.wait_ge(ct.sems["DW"], wdma)
                        if eidx == -1:
                            ct.wait(t, "V")
                        if yv_prev:
                            t.wait_ge(ct.sems["V"], yv_prev)
                    if hs_tick:
                        t.wait_ge(ct.sems["S"], hs_tick)
                    for m in range(4):
                        pb = (ps0, ps1)[m // 2][:, 512 * (m % 2):512 * (m % 2) + 512]
                        for kc in range(8):
                            t.matmul(pb[:, 0:ntok], wub[:, kc, 128 * m:128 * m + 128],
                                     xnwT[:, kc, off:off + ntok],
                                     start=(kc == 0), stop=(kc == 7))
                    ct.incd("T", t)
                h_tick = ct.n["T"]

                @blk.scalar
                def _(s, nt=nt, off=off, ntok=ntok, h_tick=h_tick):
                    s.wait_ge(ct.sems["T"], h_tick)
                    for m in range(4):
                        pb = (ps0, ps1)[m // 2][:, 512 * (m % 2):512 * (m % 2) + 512]
                        s.activation(hT[:, m, off:off + ntok], pb[:, 0:ntok], AF.Silu)
                    ct.incd("S", s)
                hs_tick = ct.n["S"]
                off += ntok

            for tt in range(NW):
                @blk.tensor
                def _(t, tt=tt, hs_tick=hs_tick, yv_tick=yv_tick):
                    if tt == 0:
                        t.wait_ge(ct.sems["S"], hs_tick)
                    if yv_tick:
                        t.wait_ge(ct.sems["V"], yv_tick)
                    for half in range(2):
                        for m in range(4):
                            t.matmul(pyb[half], hT[:, m, 128 * tt:128 * tt + 128],
                                     wdb[:, m, 512 * half:512 * half + 512],
                                     start=(m == 0), stop=(m == 3))
                    ct.incd("T", t)
                yT_tick = ct.n["T"]

                @blk.vector
                def _(v, eidx=eidx, tt=tt, yT_tick=yT_tick):
                    v.wait_ge(ct.sems["T"], yT_tick)
                    if eidx == -1 and tt == 0:
                        v.wait_ge(ct.sems["V"], cw_tick)
                    for half in range(2):
                        dst = x1acc[:, tt, 512 * half:512 * half + 512]
                        if eidx < 0:
                            v.scalar_tensor_tensor(out=dst, in0=pyb[half], scalar=1.0,
                                                   in1=dst, op0=OP.mult, op1=OP.add)
                        else:
                            v.scalar_tensor_tensor(out=dst, in0=pyb[half],
                                                   scalar=cw[:, tt, eidx:eidx + 1],
                                                   in1=dst, op0=OP.mult, op1=OP.add)
                    ct.incd("V", v)
                yv_tick = ct.n["V"]
            if eidx >= 0:
                wdma_tick[par] = yT_tick


def _emit_phase45(P, y, stag, hbuf):
    nc, blk, ct, sb, dram = P.nc, P.blk, P.ct, P.sb, P.dram
    ps0, ps1, ps2, psT = P.ps0, P.ps1, P.ps2, P.psT
    x1acc = sb["x1acc"]
    onesc = sb["onesc"]

    with contextlib.ExitStack() as p4:
        ln4 = p4.enter_context(nc.sbuf_tensor("ln4", [128, 32], F32))
        wsmstk = contextlib.ExitStack()
        wsm = wsmstk.enter_context(nc.sbuf_tensor("wsm", [128, 8, 16, 31], BF16))
        xq1stk = contextlib.ExitStack()
        xq1T = xq1stk.enter_context(nc.sbuf_tensor("xq1T", [128, 8, 2048], BF16, side="right"))
        xn1stk = contextlib.ExitStack()
        xn1 = xn1stk.enter_context(nc.sbuf_tensor("xn1", [128, NW, D], BF16, side="right"))

        with contextlib.ExitStack() as p4a:
            xn1T = p4a.enter_context(nc.sbuf_tensor("xn1T", [128, 8, 2048], BF16))
            wlt = p4a.enter_context(nc.sbuf_tensor("wlt", [128, 8, 496], BF16))
            wr = p4a.enter_context(nc.sbuf_tensor("wr", [128, 1, 496], F32))
            zer = p4a.enter_context(nc.sbuf_tensor("zer", [128, 1272], BF16))

            @blk.vector
            def _(v):
                v.memset(zer[:], 0.0)
                ct.incd("V", v)
            z_tick = ct.n["V"]

            @blk.scalar
            def _(s):
                s.wait_ge(ct.sems["V"], z_tick)
                half = 16 * 128 * 159 // 2
                for h in range(H):
                    for q in range(2):
                        ct.dmaq("DS", s, 
                            out=stag[h, q * half:(q + 1) * half].rearrange(
                                "(p f) -> p f", p=128),
                            in_=zer[:, :])
            zf_tick = ct.n["DS"]

            @blk.gpsimd
            def _(gp):
                _phase_barrier(P, gp)
                ct.dmaq("DW", gp, 
                    out=wlt[:], in_=dram["w_linT"].rearrange("(kc kp) f -> kp kc f", kp=128))
            wl_tick = ct.n["DW"]

            for i in range(NW):
                ln_tick = _emit_ln_tile(P, x1acc[:, i, :], xn1[:, i, :], ln4, pre=None)
                if i >= 1:
                    P.tr_seq([(xn1[:, i, m * 128:(m + 1) * 128],
                               xn1T[:, m, (i - 1) * 128:i * 128], None, None)
                              for m in range(8)], pre=("V", ln_tick))
            xn1_tick = ct.n["V"]
            xn1T_tick = ct.n["S"]

            wsl_tick = [0, 0]
            for bi in range(16):
                sl = 0
                pw = (ps2[0:128, 0:496], ps2[0:128, 512:1008])[bi % 2]

                @blk.tensor
                def _(t, bi=bi, pw=pw, w=wsl_tick[bi % 2]):
                    if bi == 0:
                        t.wait_ge(ct.sems["S"], xn1T_tick)
                        t.wait_ge(ct.sems["DW"], wl_tick)
                        ct.wait(t, "V")
                    if w:
                        t.wait_ge(ct.sems["V"], w)
                    t.wait_ge(ct.sems["V"], wsl_tick[0]) if False else None
                    tok0 = 128 * bi
                    for kc in range(8):
                        t.matmul(pw, xn1T[:, kc, tok0:tok0 + 128], wlt[:, kc],
                                 start=(kc == 0), stop=(kc == 7))
                    ct.incd("T", t)
                wp_tick = ct.n["T"]

                @blk.vector
                def _(v, pw=pw, wp_tick=wp_tick):
                    v.wait_ge(ct.sems["T"], wp_tick)
                    ct.wait(v, "S")
                    v.drain()
                    v.tensor_reduce(ln4[:, 0:16], split3(pw, 16, 31), axis=AX.X, op=OP.max)
                    v.drain()
                    v.tensor_tensor(split3(wr[:, 0, :], 16, 31), split3(pw, 16, 31),
                                    rep_ap(ln4[:, 0:16], 31, 2), op=OP.subtract)
                    ct.incd("V", v)
                sub_tick = ct.n["V"]

                @blk.scalar
                def _(s, sub_tick=sub_tick):
                    s.wait_ge(ct.sems["V"], sub_tick)
                    s.activation(wr[:, 0, :], wr[:, 0, :], AF.Exp)
                    ct.incd("S", s)
                ex_tick = ct.n["S"]

                @blk.vector
                def _(v, bi=bi, ex_tick=ex_tick):
                    v.wait_ge(ct.sems["S"], ex_tick)
                    if bi == 8:
                        v.wait_ge(ct.sems["DX"], stw1_tick)
                    w3 = split3(wr[:, 0, :], 16, 31)
                    v.drain()
                    v.tensor_reduce(ln4[:, 0:16], w3, axis=AX.X, op=OP.add)
                    v.drain()
                    v.reciprocal(ln4[:, 0:16], ln4[:, 0:16])
                    v.drain()
                    v.tensor_tensor(wsm[:, bi % 8], w3, rep_ap(ln4[:, 0:16], 31, 2),
                                    op=OP.mult)
                    ct.incd("V", v)
                wsl_tick[bi % 2] = ct.n["V"]
                if bi == 7:
                    sm_half = ct.n["V"]

                    @blk.sync
                    def _(sy, sm_half=sm_half):
                        sy.wait_ge(ct.sems["V"], sm_half)
                        sy.wait_ge(ct.sems["DS"], zf_tick)
                        dst = stag.rearrange("h (b t x) -> t b h x", b=16, t=128, x=159)
                        for hh in range(H):
                            ct.dmaq("DX", sy, out=dst[:, 0:8, hh, 0:31],
                                                      in_=wsm[:, :, hh, :])
                    stw1_tick = ct.n["DX"]
            sm_tick = ct.n["V"]

            @blk.sync
            def _(sy):
                sy.wait_ge(ct.sems["V"], sm_tick)
                dst = stag.rearrange("h (b t x) -> t b h x", b=16, t=128, x=159)
                for hh in range(H):
                    ct.dmaq("DX", sy, out=dst[:, 8:16, hh, 0:31],
                                              in_=wsm[:, :, hh, :])
            stw_tick = ct.n["DX"]

        with contextlib.ExitStack() as p4b:
            shift = p4b.enter_context(nc.sbuf_tensor("shift", [128, NW, D], BF16))
            weAt = p4b.enter_context(nc.sbuf_tensor("weAt", [128, 16, 158], BF16))
            weA = p4b.enter_context(nc.sbuf_tensor("weA", [128, 16, 128], BF16))
            weB = p4b.enter_context(nc.sbuf_tensor("weB", [32, 16, 128], BF16))

            @blk.scalar
            def _(s):
                _phase_barrier(P, s)
                s.wait_ge(ct.sems["V"], xn1_tick)
                for i in range(NW):
                    ct.dmaq("DS", s, out=shift[0:30, i, :], in_=xn1[98:128, i, :])
                    if i < NW - 1:
                        ct.dmaq("DS", s, out=shift[30:128, i, :],
                                                 in_=xn1[0:98, i + 1, :])
            sh_tick = ct.n["DS"]

            pslots = [ps0[:, 0:64], ps0[:, 512:576], ps1[:, 0:64], ps1[:, 512:576]]
            slot_v = [0, 0, 0, 0]
            web_tick = 0
            for h in range(H):
                @blk.sync
                def _(sy, h=h, w=web_tick):
                    if h == 0:
                        sy.wait_ge(ct.sems["DX"], stw_tick)
                        sy.wait_ge(ct.sems["DS"], sh_tick)
                        _phase_barrier(P, sy)
                    if w:
                        ct.wait(sy, "T", w)
                    srcT = bass.AP(stag.tensor, h * (16 * 128 * 159),
                                   [[158, 128], [20352, 16], [1, 158]])
                    ct.dmaq("DX", sy, out=weAt[:, :, :], in_=srcT)
                rb_tick = ct.n["DX"]

                @blk.tensor
                def _(t, rb_tick=rb_tick):
                    t.wait_ge(ct.sems["DX"], rb_tick)
                    ct.wait(t, "S")
                    for b_ in range(16):
                        t.transpose(psT[0:128, 128 * b_:128 * b_ + 128],
                                    weAt[:, b_, 0:128], sb["ident"][:, :])
                    ct.incd("T", t)
                trA_tick = ct.n["T"]

                @blk.scalar
                def _(s, trA_tick=trA_tick):
                    s.wait_ge(ct.sems["T"], trA_tick)
                    s.copy(weA[:, :, :], split3(psT[0:128, 0:2048], 16, 128))
                    ct.incd("S", s)
                evA_tick = ct.n["S"]

                @blk.tensor
                def _(t, evA_tick=evA_tick):
                    t.wait_ge(ct.sems["S"], evA_tick)
                    for b_ in range(16):
                        t.transpose(psT[0:30, 128 * b_:128 * b_ + 128],
                                    weAt[:, b_, 128:158], sb["ident"][:, :])
                    ct.incd("T", t)
                trB_tick = ct.n["T"]

                @blk.scalar
                def _(s, trB_tick=trB_tick):
                    s.wait_ge(ct.sems["T"], trB_tick)
                    s.copy(weB[0:30, :, :],
                           bass.AP(psT[0:30, 0:2048].tensor, psT[0:30, 0:2048].offset,
                                   [list(psT[0:30, 0:2048].ap[0]), [128, 16], [1, 128]]))
                    ct.incd("S", s)
                rb_tick = ct.n["S"]

                for bi in range(16):
                    k = h * 16 + bi
                    sl = k % 4
                    pband = pslots[sl]

                    @blk.tensor
                    def _(t, h=h, bi=bi, sl=sl, pband=pband, rb_tick=rb_tick, w=slot_v[sl]):
                        if bi == 0:
                            t.wait_ge(ct.sems["S"], rb_tick)
                        if w:
                            t.wait_ge(ct.sems["V"], w)
                        t.matmul(pband, weA[:, bi, :], shift[:, bi, 64 * h:64 * h + 64],
                                 start=True, stop=False)
                        t.matmul(pband, weB[0:30, bi, :], shift[0:30, bi + 1, 64 * h:64 * h + 64],
                                 start=False, stop=True)
                        ct.incd("T", t)
                    bT_tick = ct.n["T"]

                    @blk.vector
                    def _(v, h=h, bi=bi, pband=pband, bT_tick=bT_tick):
                        v.wait_ge(ct.sems["T"], bT_tick)
                        dst = x1acc[:, 1 + bi, 64 * h:64 * h + 64]
                        v.scalar_tensor_tensor(out=dst, in0=pband, scalar=1.0, in1=dst,
                                               op0=OP.mult, op1=OP.add)
                        ct.incd("V", v)
                    slot_v[sl] = ct.n["V"]
                web_tick = ct.n["T"]

            if "tap_p4" in P.dbg_out:
                @blk.sync
                def _(sy):
                    ct.wait(sy, "V")
                    ct.dmaq("DX", sy, 
                        out=P.dbg_out["tap_p4"].rearrange("(i p) d -> p i d", p=128),
                        in_=x1acc[:])
                p4tap = ct.n["DX"]

                @blk.vector
                def _(v):
                    v.wait_ge(ct.sems["DX"], p4tap)
                    ct.incd("V", v)

        with contextlib.ExitStack() as p4c:
            qf = p4c.enter_context(nc.sbuf_tensor("qf", [128, D], F32))
            qx = p4c.enter_context(nc.sbuf_tensor("qx", [128, 2, D], BF16))
            for ii in range(16):
                i = 1 + ii
                sl = ii % 2

                c0 = 3 * sl

                @blk.vector
                def _(v, i=i, sl=sl, c0=c0):
                    if i == 1:
                        ct.wait(v, "T")
                    else:
                        ct.wait(v, "S")
                    v.drain()
                    v.reduce_max(ln4[:, c0:c0 + 1], xn1[:, i, :], axis=AX.X,
                                 apply_absolute_value=True)
                    v.drain()
                    v.tensor_scalar(ln4[:, c0 + 1:c0 + 2], ln4[:, c0:c0 + 1], 1e-5, 1.0 / QB,
                                    op0=OP.max, op1=OP.mult)
                    v.drain()
                    v.reciprocal(ln4[:, c0 + 2:c0 + 3], ln4[:, c0 + 1:c0 + 2])
                    ct.incd("V", v)
                q1_tick = ct.n["V"]

                @blk.scalar
                def _(s, i=i, q1_tick=q1_tick, c0=c0):
                    s.wait_ge(ct.sems["V"], q1_tick)
                    ct.wait(s, "T")
                    s.activation(qf[:, :], xn1[:, i, :], AF.Identity,
                                 scale=ln4[:, c0 + 2:c0 + 3], bias=sb["magic"][:, :])
                    ct.incd("S", s)
                q2_tick = ct.n["S"]

                @blk.vector
                def _(v, sl=sl, q2_tick=q2_tick, c0=c0):
                    v.wait_ge(ct.sems["S"], q2_tick)
                    v.tensor_scalar(qf[:, :], qf[:, :], MAGIC, QB, op0=OP.subtract, op1=OP.min)
                    v.drain()
                    v.tensor_scalar(qx[:, sl, :], qf[:, :], -QB - 1.0, ln4[:, c0 + 1:c0 + 2],
                                    op0=OP.max, op1=OP.mult)
                    ct.incd("V", v)
                q3_tick = ct.n["V"]

                P.tr_seq([(qx[:, sl, m * 128:(m + 1) * 128],
                           xq1T[:, m, ii * 128:(ii + 1) * 128], None, None) for m in range(8)],
                         pre=("V", q3_tick))
        xn1stk.close()
        wsmstk.close()

        with contextlib.ExitStack() as p5:
            b1s = p5.enter_context(nc.sbuf_tensor("b1s", [1, HFF], BF16))
            b2s = p5.enter_context(nc.sbuf_tensor("b2s", [1, D], BF16))
            onesb = p5.enter_context(nc.sbuf_tensor("onesb", [1, 128], BF16))
            o128 = p5.enter_context(nc.sbuf_tensor("o128", [128, 1], F32))
            wsc = p5.enter_context(nc.sbuf_tensor("wsc", [1, 1], F32))

            @blk.gpsimd
            def _(gp):
                _phase_barrier(P, gp)
                ct.dmaq("DW", gp, out=b1s[:], in_=dram["b1"][:])
                ct.dmaq("DW", gp, out=b2s[:], in_=dram["b2"][:])
                ct.dmaq("DW", gp, out=onesb[:], in_=dram["ones_col"][:])
            bias_tick = ct.n["DW"]

            @blk.sync
            def _(sy):
                _phase_barrier(P, sy)
                ct.dmaq("DX", sy, out=o128[:], in_=dram["ones_col"].rearrange(
                    "o p -> p o"))
            o128_tick = ct.n["DX"]

            if "tap_b1" in P.dbg_out:
                @blk.sync
                def _(sy):
                    sy.wait_ge(ct.sems["DW"], bias_tick)
                    ct.dmaq("DX", sy, out=P.dbg_out["tap_b1"], in_=b1s[:])
                btap = ct.n["DX"]

                @blk.vector
                def _(v):
                    v.wait_ge(ct.sems["DX"], btap)
                    ct.incd("V", v)

            def quantize_W(name, nchunks, cw_, dst, wstg):
                tot = float(nchunks * 128 * cw_)
                wld = 0
                for ck in range(nchunks):
                    @blk.sync
                    def _(sy, ck=ck, wld=wld):
                        if ck == 0:
                            _phase_barrier(P, sy)
                            ct.wait(sy, "DS")
                        if wld:
                            ct.wait(sy, "V", wld)
                        ct.dmaq("DX", sy, out=wstg[:, 0:cw_],
                                                  in_=dram[name][128 * ck:128 * ck + 128, :])
                    d_tick = ct.n["DX"]

                    @blk.vector
                    def _(v, ck=ck, d_tick=d_tick):
                        v.wait_ge(ct.sems["DX"], d_tick)
                        v.drain()
                        if ck < 16:
                            v.reduce_sum(ln4[:, 8 + ck:9 + ck],
                                         wstg[:, 0:cw_], axis=AX.X, apply_absolute_value=True)
                        else:
                            v.reduce_sum(ln4[:, 24:25], wstg[:, 0:cw_], axis=AX.X,
                                         apply_absolute_value=True)
                            v.drain()
                            v.tensor_tensor(ln4[:, 8 + ck - 16:9 + ck - 16],
                                            ln4[:, 8 + ck - 16:9 + ck - 16],
                                            ln4[:, 24:25], op=OP.add)
                        ct.incd("V", v)
                    wld = ct.n["V"]
                ncol = 8 if nchunks <= 8 else 16

                @blk.tensor
                def _(t, wld=wld, ncol=ncol):
                    t.wait_ge(ct.sems["V"], wld)
                    t.wait_ge(ct.sems["DX"], o128_tick)
                    ct.wait(t, "V")
                    t.matmul(ps2[0:1, 0:ncol], o128[:, :], ln4[:, 8:8 + ncol],
                             start=True, stop=True)
                    ct.incd("T", t)
                s1_tick = ct.n["T"]

                @blk.vector
                def _(v, s1_tick=s1_tick, tot=tot, ncol=ncol):
                    v.wait_ge(ct.sems["T"], s1_tick)
                    v.tensor_reduce(wsc[0:1, 0:1], ps2[0:1, 0:ncol], axis=AX.X, op=OP.add)
                    v.drain()
                    v.tensor_scalar(wsc[0:1, 0:1], wsc[0:1, 0:1], 1.0 / tot, 1e-5,
                                    op0=OP.mult, op1=OP.max)
                    ct.incd("V", v)
                ws_tick = ct.n["V"]

                @blk.tensor
                def _(t, ws_tick=ws_tick):
                    t.wait_ge(ct.sems["V"], ws_tick)
                    t.matmul(ps2[0:128, 64:65], onesc[:, :], wsc[:, :],
                             start=True, stop=True)
                    ct.incd("T", t)
                bc_tick = ct.n["T"]

                @blk.vector
                def _(v, bc_tick=bc_tick):
                    v.wait_ge(ct.sems["T"], bc_tick)
                    v.tensor_copy(ln4[:, 0:1], ps2[0:128, 64:65])
                    v.drain()
                    v.reciprocal(ln4[:, 1:2], ln4[:, 0:1])
                    ct.incd("V", v)
                wsb_tick = ct.n["V"]

                qdone = 0
                for ck in range(nchunks):
                    @blk.sync
                    def _(sy, ck=ck, qdone=qdone, wsb_tick=wsb_tick):
                        if ck == 0:
                            sy.wait_ge(ct.sems["V"], wsb_tick)
                        if qdone:
                            sy.wait_ge(ct.sems["V"], qdone)
                        ct.dmaq("DX", sy, out=wstg[:, 0:cw_],
                                                  in_=dram[name][128 * ck:128 * ck + 128, :])
                    d_tick = ct.n["DX"]

                    @blk.scalar
                    def _(s, d_tick=d_tick):
                        s.wait_ge(ct.sems["DX"], d_tick)
                        ct.wait(s, "V")
                        s.activation(wstg[:, 0:cw_], wstg[:, 0:cw_], AF.Identity,
                                     scale=ln4[:, 1:2], bias=sb["magic"][:, :])
                        ct.incd("S", s)
                    qdone = ct.n["S"]

                    @blk.vector
                    def _(v, ck=ck, qdone=qdone):
                        v.wait_ge(ct.sems["S"], qdone)
                        v.tensor_scalar(wstg[:, 0:cw_], wstg[:, 0:cw_], MAGIC, 1.0,
                                        op0=OP.subtract, op1=OP.min)
                        v.drain()
                        v.tensor_scalar(dst[:, ck, :], wstg[:, 0:cw_], -1.0, ln4[:, 0:1],
                                        op0=OP.max, op1=OP.mult)
                        ct.incd("V", v)
                    qdone = ct.n["V"]
                return qdone

            with contextlib.ExitStack() as p5a:
                W1q = p5a.enter_context(nc.sbuf_tensor("W1q", [128, 8, HFF], BF16))
                with contextlib.ExitStack() as p5w:
                    wst1 = p5w.enter_context(nc.sbuf_tensor("wst1", [128, HFF], F32))
                    w1q_tick = quantize_W("W1T", 8, HFF, W1q, wst1)
                hbt = p5a.enter_context(nc.sbuf_tensor("hbt", [128, HFF], BF16))


                pslots5 = [ps0[:, 0:512], ps0[:, 512:1024], ps1[:, 0:512], ps1[:, 512:1024]]
                sl_tick = [0, 0, 0, 0]
                for tt in range(16):
                    hb_tick = 0
                    for ns in range(8):
                        k = tt * 8 + ns
                        sl = k % 4
                        pb = pslots5[sl]

                        @blk.tensor
                        def _(t, tt=tt, ns=ns, pb=pb, w=sl_tick[sl], w1q_tick=w1q_tick):
                            if tt == 0 and ns == 0:
                                t.wait_ge(ct.sems["V"], w1q_tick)
                                t.wait_ge(ct.sems["DW"], bias_tick)
                                ct.wait(t, "S")
                            if w:
                                t.wait_ge(ct.sems["S"], w)
                            import os
                            _nb = os.environ.get("KNB", "0") == "1"
                            for kc in range(8):
                                t.matmul(pb, xq1T[:, kc, 128 * tt:128 * tt + 128],
                                         W1q[:, kc, 512 * ns:512 * ns + 512],
                                         start=(kc == 0), stop=(_nb and kc == 7))
                            if not _nb:
                                t.matmul(pb, onesb[0:1, :], b1s[0:1, 512 * ns:512 * ns + 512],
                                         start=False, stop=True)
                            ct.incd("T", t)
                        m_tick = ct.n["T"]

                        @blk.scalar
                        def _(s, ns=ns, pb=pb, m_tick=m_tick, hb_tick=hb_tick):
                            import os
                            s.wait_ge(ct.sems["T"], m_tick)
                            if ns == 0 and hb_tick == 0:
                                ct.wait(s, "DS")
                            fn = AF.Identity if os.environ.get("KID", "0") == "1" else AF.Gelu
                            s.activation(hbt[:, 512 * ns:512 * ns + 512], pb, fn)
                            ct.incd("S", s)
                        sl_tick[sl] = ct.n["S"]
                    ge_tick = ct.n["S"]

                    @blk.scalar
                    def _(s, tt=tt, ge_tick=ge_tick):
                        s.wait_ge(ct.sems["S"], ge_tick)
                        s.drain()
                        ct.dmaq("DS", s, out=hbuf[128 * tt:128 * tt + 128, :],
                                                 in_=hbt[:, :])

                if "tap_w1q" in P.dbg_out:
                    @blk.sync
                    def _(sy):
                        ct.wait(sy, "S")
                        ct.dmaq("DX", sy, out=P.dbg_out["tap_w1q"], in_=W1q[:])
                    wtap2 = ct.n["DX"]

                    @blk.vector
                    def _(v):
                        v.wait_ge(ct.sems["DX"], wtap2)
                        ct.incd("V", v)
                if "tap_xq1" in P.dbg_out:
                    @blk.sync
                    def _(sy):
                        ct.wait(sy, "S")
                        ct.dmaq("DX", sy, out=P.dbg_out["tap_xq1"], in_=xq1T[:])
                    xtap2 = ct.n["DX"]

                    @blk.vector
                    def _(v):
                        v.wait_ge(ct.sems["DX"], xtap2)
                        ct.incd("V", v)
            xq1stk.close()
            with contextlib.ExitStack() as p5b:
                wst2 = p5b.enter_context(nc.sbuf_tensor("wst2", [128, D], F32))
                W2q = p5b.enter_context(nc.sbuf_tensor("W2q", [128, 32, D], BF16))
                hbt2 = p5b.enter_context(nc.sbuf_tensor("hbt2", [128, HFF], BF16))
                qf2 = p5b.enter_context(nc.sbuf_tensor("qf2", [128, HFF], F32))
                qx2 = p5b.enter_context(nc.sbuf_tensor("qx2", [128, HFF], BF16))
                xq2T = p5b.enter_context(nc.sbuf_tensor("xq2T", [128, 32, 128], BF16))
                w2q_tick = quantize_W("W2T", 32, D, W2q, wst2)

                for tt in range(16):
                    @blk.sync
                    def _(sy, tt=tt):
                        if tt == 0:
                            ct.wait(sy, "DS")
                        else:
                            ct.wait(sy, "S")
                        ct.dmaq("DX", sy, out=hbt2[:, :],
                                                  in_=hbuf[128 * tt:128 * tt + 128, :])
                    hld_tick = ct.n["DX"]

                    @blk.vector
                    def _(v, hld_tick=hld_tick):
                        v.wait_ge(ct.sems["DX"], hld_tick)
                        v.drain()
                        v.reduce_max(ln4[:, 4:5], hbt2[:, :], axis=AX.X, apply_absolute_value=True)
                        v.drain()
                        v.tensor_scalar(ln4[:, 5:6], ln4[:, 4:5], 1e-5, 1.0 / QB,
                                        op0=OP.max, op1=OP.mult)
                        v.drain()
                        v.reciprocal(ln4[:, 6:7], ln4[:, 5:6])
                        ct.incd("V", v)
                    qq1 = ct.n["V"]

                    @blk.scalar
                    def _(s, qq1=qq1, tt=tt):
                        s.wait_ge(ct.sems["V"], qq1)
                        if tt > 0:
                            ct.wait(s, "T")
                        s.activation(qf2[:, :], hbt2[:, :], AF.Identity,
                                     scale=ln4[:, 6:7], bias=sb["magic"][:, :])
                        ct.incd("S", s)
                    qq2 = ct.n["S"]

                    @blk.vector
                    def _(v, qq2=qq2):
                        v.wait_ge(ct.sems["S"], qq2)
                        v.tensor_scalar(qf2[:, :], qf2[:, :], MAGIC, QB,
                                        op0=OP.subtract, op1=OP.min)
                        v.drain()
                        v.tensor_scalar(qx2[:, :], qf2[:, :], -QB - 1.0, ln4[:, 5:6],
                                        op0=OP.max, op1=OP.mult)
                        ct.incd("V", v)
                    qq3 = ct.n["V"]

                    P.tr_seq([(qx2[:, m * 128:(m + 1) * 128], xq2T[:, m, :], None, None)
                              for m in range(32)], pre=("V", qq3))
                    x2_tick = ct.n["S"]

                    @blk.tensor
                    def _(t, tt=tt, x2_tick=x2_tick, w2q_tick=w2q_tick):
                        t.wait_ge(ct.sems["S"], x2_tick)
                        if tt == 0:
                            t.wait_ge(ct.sems["V"], w2q_tick)
                        else:
                            ct.wait(t, "V")
                        for half in range(2):
                            for kc in range(32):
                                t.matmul(ps2[:, 512 * half:512 * half + 512],
                                         xq2T[:, kc, :],
                                         W2q[:, kc, 512 * half:512 * half + 512],
                                         start=(kc == 0), stop=False)
                            t.matmul(ps2[:, 512 * half:512 * half + 512], onesb[0:1, :],
                                     b2s[0:1, 512 * half:512 * half + 512],
                                     start=False, stop=True)
                        ct.incd("T", t)
                    mm2_tick = ct.n["T"]

                    @blk.vector
                    def _(v, tt=tt, mm2_tick=mm2_tick):
                        v.wait_ge(ct.sems["T"], mm2_tick)
                        for half in range(2):
                            dst = x1acc[:, 1 + tt, 512 * half:512 * half + 512]
                            v.scalar_tensor_tensor(out=dst, in0=ps2[:, 512 * half:512 * half + 512],
                                                   scalar=1.0, in1=dst, op0=OP.mult, op1=OP.add)
                        ct.incd("V", v)

        @blk.sync
        def _(sy):
            ct.wait(sy, "V")
            ct.dmaq("DO", sy, 
                out=y.rearrange("(i p) d -> p i d", p=128),
                in_=x1acc[:, 1:17, :])
            ct.wait(sy, "DO")


def kernel(**inputs):
    in_maps = _host_prep(inputs)
    nc = build_program()
    res = run_bass_kernel_spmd(nc, in_maps, core_ids=list(range(8)))
    out = np.empty((B, T, D), np.float32)
    for core in range(8):
        b, g = core // 2, core % 2
        out[b, g * 2048:(g + 1) * 2048, :] = res.results[core]["y"]
    return out


# version 11
# speedup vs baseline: 1.0273x; 1.0273x over previous
import contextlib
import numpy as np
import ml_dtypes
import concourse.bass as bass
import concourse.mybir as mybir
from concourse.bass_utils import run_bass_kernel_spmd

F32 = mybir.dt.float32
F32R = mybir.dt.float32r
BF16 = mybir.dt.bfloat16
AF = mybir.ActivationFunctionType
OP = mybir.AluOpType
AX = mybir.AxisListType

bfd = ml_dtypes.bfloat16

B, T, D = 4, 4096, 1024
H, HD = 16, 64
C = 128
NCH = T // C
SEG = 256
NSEG = T // SEG
CPS = SEG // C
WIN = 2176
NW = WIN // 128
E, FE = 64, 512
HFF = 4096
QB = 127.0
MAGIC = 1.5 * 2.0 ** 23
NLVL = 7


def r32(ap):
    return ap.bitcast(F32R)


def rep_ap(ap, n, pos):
    lay = [list(x) for x in ap.ap]
    lay.insert(pos, [0, n])
    return bass.AP(ap.tensor, ap.offset, lay)


def split3(ap, a, b):
    lay = [list(ap.ap[0]), [b, a], [1, b]]
    return bass.AP(ap.tensor, ap.offset, lay)


_LAST_T = [None]


def _wrap_tensor_engine():
    if getattr(bass.BassTensorEngine, "_mk_wrapped", False):
        return
    _omm = bass.BassTensorEngine.matmul
    _otr = bass.BassTensorEngine.transpose

    def _mm(self, *args, **kw):
        r = _omm(self, *args, **kw)
        _LAST_T[0] = r
        return r

    def _tr(self, *args, **kw):
        r = _otr(self, *args, **kw)
        _LAST_T[0] = r
        return r

    bass.BassTensorEngine.matmul = _mm
    bass.BassTensorEngine.transpose = _tr
    bass.BassTensorEngine._mk_wrapped = True


class Ctr:
    def __init__(self, nc):
        self.nc = nc
        self.sems = {}
        self.n = {}

    def make(self, stack, names):
        for nm in names:
            self.sems[nm] = stack.enter_context(self.nc.semaphore(nm))
            self.n[nm] = 0

    def inc(self, nm, instr, amt=1):
        instr.then_inc(self.sems[nm], amt)
        self.n[nm] += amt
        return self.n[nm]

    def wait(self, eng, nm, v=None):
        eng.wait_ge(self.sems[nm], self.n[nm] if v is None else v)

    def incd(self, nm, eng, amt=1):
        if nm == "T" and _LAST_T[0] is not None:
            ins = _LAST_T[0]
            _LAST_T[0] = None
            return self.inc(nm, ins, amt)
        eng.drain()
        return self.inc(nm, eng.nop(), amt)

    def dmaq(self, nm, eng, *args, **kw):
        eng.wait_ge(self.sems[nm], self.n[nm])
        return self.inc(nm, eng.dma_start(*args, **kw), 16)


def _host_prep(inp):
    f32 = np.float32
    x = np.asarray(inp["L0_attn_x"], f32)

    com = {}
    for nm, src in (("Wq", "L0_attn_Wq"), ("Wk", "L0_attn_Wk"), ("Wv", "L0_attn_Wv"),
                    ("Wg", "L0_attn_Wg"), ("Wo", "L0_attn_Wo"), ("Wb", "L0_attn_Wb"),
                    ("Ws_up", "L0_ffn_Ws_up"), ("Ws_down", "L0_ffn_Ws_down"),
                    ("W_up", "L0_ffn_W_up"), ("W_down", "L0_ffn_W_down")):
        com[nm] = np.ascontiguousarray(np.asarray(inp[src], f32))
    for nm, src in (("cqw", "L0_attn_cq_w"), ("ckw", "L0_attn_ck_w"), ("cvw", "L0_attn_cv_w")):
        com[nm] = np.ascontiguousarray(
            np.asarray(inp[src], f32).reshape(8, 128, 4).transpose(1, 0, 2))
    for nm, src in (("cqb", "L0_attn_cq_b"), ("ckb", "L0_attn_ck_b"), ("cvb", "L0_attn_cv_b")):
        com[nm] = np.ascontiguousarray(np.asarray(inp[src], f32).reshape(8, 128).T)
    com["nw_rep"] = np.ascontiguousarray(
        np.broadcast_to(np.tile(np.asarray(inp["L0_attn_nw"], f32), H), (128, D)))
    com["centroidsT"] = np.ascontiguousarray(np.asarray(inp["L0_ffn_centroids"], f32).T)
    com["ebias_rep"] = np.ascontiguousarray(
        np.broadcast_to(np.asarray(inp["L0_ffn_expert_biases"], f32), (128, E)))
    com["w_linT"] = np.ascontiguousarray(np.asarray(inp["L1_attn_w_lin"], f32).T)
    com["W1T"] = np.ascontiguousarray(np.asarray(inp["L1_ffn_W1"], f32).T)
    com["W2T"] = np.ascontiguousarray(np.asarray(inp["L1_ffn_W2"], f32).T)
    com["b1"] = np.asarray(inp["L1_ffn_b1"], f32).reshape(1, HFF)
    com["b2"] = np.asarray(inp["L1_ffn_b2"], f32).reshape(1, D)

    ti = np.arange(128)
    com["ident128"] = np.eye(128, dtype=bfd)
    com["identf"] = np.eye(128, dtype=f32)
    com["ltmask"] = (ti[:, None] > ti[None, :]).astype(bfd)
    com["utmask"] = (ti[:, None] < ti[None, :]).astype(bfd)
    com["utimask"] = (ti[:, None] <= ti[None, :]).astype(bfd)
    com["ones_col"] = np.ones((1, 128), f32)
    Ec = np.zeros((8, 16, 128), bfd)
    for c in range(8):
        Ec[c, 2 * c, 0:64] = 1
        Ec[c, 2 * c + 1, 64:128] = 1
    com["Ec"] = np.ascontiguousarray(Ec.transpose(1, 0, 2))
    com["Esq"] = np.ascontiguousarray(Ec.transpose(2, 0, 1))

    in_maps = []
    for core in range(8):
        b, g = core // 2, core % 2
        m = dict(com)
        m["x_full"] = np.ascontiguousarray(x[b])
        start = g * 2048
        lo = start - 128
        xw = np.zeros((WIN, D), f32)
        s0 = max(lo, 0)
        xw[s0 - lo:, :] = x[b, s0:start + 2048, :]
        m["x_win"] = xw
        mo = np.zeros(NCH, f32)
        mo[g * 16:(g + 1) * 16] = 1.0
        mh = np.zeros(NCH, f32)
        if g == 1:
            mh[15] = 1.0
        m["maskO"] = np.ascontiguousarray(np.broadcast_to(mo, (128, NCH)))
        m["maskH"] = np.ascontiguousarray(np.broadcast_to(mh, (128, NCH)))
        in_maps.append(m)
    return in_maps


class Prog:

    def __init__(self, nc, blk, ct, sb, ps, dram, dbg_out):
        self.nc, self.blk, self.ct, self.sb, self.dram = nc, blk, ct, sb, dram
        self.ps0, self.ps1, self.ps2, self.psT = ps
        self.dbg_out = dbg_out
        self.trtick = {"bf": [0, 0], "f32": [0, 0]}

    def tr_seq(self, items, ident_ap=None, f32psum=False, pre=None):
        blk, ct = self.blk, self.ct
        ident = ident_ap if ident_ap is not None else self.sb["ident"][:, :]
        slot_tick = self.trtick["f32" if f32psum else "bf"]
        for idx, (src, dst, fn, bias) in enumerate(items):
            sl = idx % 2
            p_in = src.ap[0][1]
            f_in = src.ap[-1][1]
            if f32psum:
                slot = (self.ps2[0:f_in, 0:p_in], self.ps2[0:f_in, 512:512 + p_in])[sl]
            else:
                slot = (self.psT[0:f_in, 0:p_in], self.psT[0:f_in, 1024:1024 + p_in])[sl]

            @blk.tensor
            def _(t, src=src, slot=slot, sl=sl, idx=idx):
                if idx == 0 and pre is not None:
                    t.wait_ge(ct.sems[pre[0]], pre[1])
                if slot_tick[sl]:
                    t.wait_ge(ct.sems["S"], slot_tick[sl])
                t.transpose(slot, src, ident)
                ct.incd("T", t)
            t_tick = ct.n["T"]

            @blk.scalar
            def _(s, dst=dst, fn=fn, bias=bias, slot=slot, t_tick=t_tick):
                s.wait_ge(ct.sems["T"], t_tick)
                if fn is None:
                    s.copy(dst, slot)
                elif bias is None:
                    s.activation(dst, slot, fn)
                else:
                    s.activation(dst, slot, fn, bias=bias)
                ct.incd("S", s)
            slot_tick[sl] = ct.n["S"]


def _emit_segment(P, seg, ticks):
    STG = getattr(P, "stage", 99)
    nc, blk, ct, sb, dram = P.nc, P.blk, P.ct, P.sb, P.dram
    ps0, ps1, ps2, psT = P.ps0, P.ps1, P.ps2, P.psT
    xseg, lnst, xnT = sb["xseg"], sb["lnst"], sb["xnT"]
    qpT, kpT, vpT = sb["qpT"], sb["kpT"], sb["vpT"]
    sqb, sqb2 = sb["sqb"], sb["sqb2"]
    qnT, knT, kbnT, vbT = sb["qnT"], sb["knT"], sb["kbnT"], sb["vbT"]
    vtok, ktok = sb["vtok"], sb["ktok"]
    beta, betaT, btb = sb["beta"], sb["betaT"], sb["btb"]
    ssq, rsn = sb["ssq"], sb["rsn"]
    cw_conv, cb_conv = sb["cw_conv"], sb["cb_conv"]
    ec, esq = sb["ec"], sb["esq"]
    wqkvg, wb_s = sb["wqkvg"], sb["wb_s"]
    xn_st = sb["xn_st"]

    @blk.gpsimd
    def _(gp):
        if seg == 0:
            gp.wait_ge(ct.sems["DX"], ticks["const"])
            gp.wait_ge(ct.sems["V"], ticks["zinit"])
        else:
            ct.wait(gp, "V")
        src = dram["x_full"][seg * SEG:(seg + 1) * SEG, :].rearrange("(i p) d -> p i d", p=128)
        ct.dmaq("DW", gp, out=xseg[:], in_=src)
    x_tick = ct.n["DW"]

    @blk.vector
    def _(v):
        v.wait_ge(ct.sems["DW"], x_tick)
        if seg > 0:
            ct.wait(v, "T")
        for i in range(CPS):
            xt = xseg[:, i]
            v.bn_stats(lnst[:, 12 * i:12 * i + 6], xt[:, 0:512])
            v.bn_stats(lnst[:, 12 * i + 6:12 * i + 12], xt[:, 512:1024])
        v.drain()
        for i in range(CPS):
            v.bn_aggr(lnst[:, 24 + 2 * i:26 + 2 * i], lnst[:, 12 * i:12 * i + 12])
        v.drain()
        for i in range(CPS):
            v.tensor_scalar_add(lnst[:, 28 + i:29 + i], lnst[:, 25 + 2 * i:26 + 2 * i], 1e-5)
        v.drain()
        for i in range(CPS):
            v.reciprocal(lnst[:, 30 + i:31 + i], lnst[:, 28 + i:29 + i])
        ct.incd("V", v)
    lnv_tick = ct.n["V"]

    @blk.scalar
    def _(s):
        s.wait_ge(ct.sems["V"], lnv_tick)
        for i in range(CPS):
            s.activation(lnst[:, 28 + i:29 + i], lnst[:, 30 + i:31 + i], AF.Sqrt)
        ct.incd("S", s)
    rstd_tick = ct.n["S"]

    @blk.vector
    def _(v):
        v.wait_ge(ct.sems["S"], rstd_tick)
        for i in range(CPS):
            v.tensor_scalar(xn_st[:, i], xseg[:, i],
                            lnst[:, 24 + 2 * i:25 + 2 * i], lnst[:, 28 + i:29 + i],
                            op0=OP.subtract, op1=OP.mult)
        ct.incd("V", v)
    xn_tick = ct.n["V"]

    if STG < 2:
        return
    @blk.tensor
    def _(t):
        if seg == 0:
            t.wait_ge(ct.sems["DW"], ticks["l0w"])
        t.nop()
    P.tr_seq([(xn_st[:, i, m * 128:(m + 1) * 128], xnT[:, m, i * 128:(i + 1) * 128], None, None)
              for i in range(CPS) for m in range(8)], pre=("V", xn_tick))
    xnT_tick = ct.n["S"]

    if STG < 3:
        return
    slots = [ps0[:, 0:256], ps0[:, 512:768], ps1[:, 0:256], ps1[:, 512:768]]
    slot_tick = [0, 0, 0, 0]
    kproj = 0
    for mi in range(3):
        for m in range(8):
            sl = kproj % 4
            kproj += 1
            pst = slots[sl]

            @blk.tensor
            def _(t, mi=mi, m=m, pst=pst, sl=sl, first=(mi == 0 and m == 0)):
                if first:
                    t.wait_ge(ct.sems["S"], xnT_tick)
                    ct.wait(t, "V")
                if slot_tick[sl]:
                    t.wait_ge(ct.sems["S"], slot_tick[sl])
                for kc in range(8):
                    t.matmul(pst, wqkvg[:, mi, kc, m * 128:(m + 1) * 128], xnT[:, kc, :],
                             start=(kc == 0), stop=(kc == 7))
                ct.incd("T", t)
            t_tick = ct.n["T"]

            @blk.scalar
            def _(s, mi=mi, m=m, pst=pst, t_tick=t_tick):
                s.wait_ge(ct.sems["T"], t_tick)
                s.copy((qpT, kpT, vpT)[mi][:, m, 3:3 + SEG], pst)
                ct.incd("S", s)
            slot_tick[sl] = ct.n["S"]
    proj_tick = ct.n["S"]

    if STG < 4:
        return
    bslot_tick = 0
    for i in range(CPS):
        @blk.tensor
        def _(t, i=i):
            if bslot_tick:
                t.wait_ge(ct.sems["S"], bslot_tick)
            else:
                t.wait_ge(ct.sems["S"], xnT_tick)
            for kc in range(8):
                t.matmul(ps2[:, 0:16], xnT[:, kc, i * 128:(i + 1) * 128], wb_s[:, kc],
                         start=(kc == 0), stop=(kc == 7))
            ct.incd("T", t)
        t_tick = ct.n["T"]

        @blk.scalar
        def _(s, i=i, t_tick=t_tick):
            s.wait_ge(ct.sems["T"], t_tick)
            s.activation(beta[:, i], ps2[:, 0:16], AF.Sigmoid)
            ct.incd("S", s)
        bslot_tick = ct.n["S"]

    if STG < 5:
        return
    P.tr_seq([(beta[:, i], betaT[:, i * 128:(i + 1) * 128], None, None) for i in range(CPS)],
             ident_ap=sb["identf_s"][:, :], f32psum=True, pre=("S", bslot_tick))
    betaT_tick = ct.n["S"]

    @blk.vector
    def _(v):
        v.wait_ge(ct.sems["S"], betaT_tick)
        v.tensor_copy(btb[:, :], betaT[:, :])
        ct.incd("V", v)
    btb_tick = ct.n["V"]

    if STG < 6:
        return
    cacc8 = sb["cacc8"]
    sil_tick = 0
    for mi, dst in ((0, qnT), (1, knT), (2, vbT)):
        src = (qpT, kpT, vpT)[mi]

        @blk.vector
        def _(v, mi=mi, src=src, sil_tick=sil_tick, first=(mi == 0)):
            if first:
                v.wait_ge(ct.sems["S"], proj_tick)
            if sil_tick:
                v.wait_ge(ct.sems["S"], sil_tick)
            for m in range(8):
                v.tensor_scalar_mul(cacc8[:, m], src[:, m, 0:SEG], cw_conv[:, mi, m, 0:1])
            for j in (1, 2, 3):
                v.drain()
                for m in range(8):
                    v.scalar_tensor_tensor(out=cacc8[:, m], in0=src[:, m, j:j + SEG],
                                           scalar=cw_conv[:, mi, m, j:j + 1],
                                           in1=cacc8[:, m], op0=OP.mult, op1=OP.add)
            ct.incd("V", v)
        v_tick = ct.n["V"]

        @blk.scalar
        def _(s, mi=mi, dst=dst, v_tick=v_tick):
            s.wait_ge(ct.sems["V"], v_tick)
            for m in range(8):
                s.activation(dst[:, m, :], cacc8[:, m], AF.Silu, bias=cb_conv[:, mi, m:m + 1])
            ct.incd("S", s)
        sil_tick = ct.n["S"]
    conv_tick = ct.n["S"]

    if STG < 7:
        return
    sq_slot = [0, 0]
    for qi, srcb in ((0, qnT), (1, knT)):
        pdst = ps2[0:16, 0:256] if qi == 0 else ps2[0:16, 512:768]
        for m in range(8):
            @blk.scalar
            def _(s, qi=qi, m=m, srcb=srcb, w=sq_slot[m % 2], first=(qi == 0 and m == 0)):
                if first:
                    s.drain()
                if w:
                    s.wait_ge(ct.sems["T"], w)
                sq = (sqb, sqb2)[m % 2]
                s.activation(sq[:, :], srcb[:, m, :], AF.Square)
                ct.incd("S", s)
            s_tick = ct.n["S"]

            @blk.tensor
            def _(t, qi=qi, m=m, pdst=pdst, s_tick=s_tick):
                t.wait_ge(ct.sems["S"], s_tick)
                sq = (sqb, sqb2)[m % 2]
                t.matmul(pdst, esq[:, m], sq[:, :], start=(m == 0), stop=(m == 7))
                ct.incd("T", t)
            sq_slot[m % 2] = ct.n["T"]
    ssq_mm_tick = ct.n["T"]

    @blk.vector
    def _(v):
        v.wait_ge(ct.sems["T"], ssq_mm_tick)
        v.tensor_scalar_add(ssq[:, 0, :], ps2[0:16, 0:256], 1e-6)
        v.tensor_scalar_add(ssq[:, 1, :], ps2[0:16, 512:768], 1e-6)
        v.drain()
        v.reciprocal(ssq[:, 0, :], ssq[:, 0, :])
        v.reciprocal(ssq[:, 1, :], ssq[:, 1, :])
        ct.incd("V", v)
    rs_tick = ct.n["V"]

    @blk.scalar
    def _(s):
        s.wait_ge(ct.sems["V"], rs_tick)
        s.activation(rsn[:, 0, :], ssq[:, 0, :], AF.Sqrt, scale=1.0 / 64.0)
        s.activation(rsn[:, 1, :], ssq[:, 1, :], AF.Sqrt)
        ct.incd("S", s)
    rsn_tick = ct.n["S"]

    if STG < 8:
        return
    rep_tick = [0, 0]
    for m in range(8):
        pq = (ps0, ps1)[m % 2]

        @blk.tensor
        def _(t, m=m, pq=pq):
            if m == 0:
                t.wait_ge(ct.sems["S"], rsn_tick)
                ct.wait(t, "V")
            if rep_tick[m % 2]:
                t.wait_ge(ct.sems["V"], rep_tick[m % 2])
            t.matmul(pq[:, 0:256], ec[:, m], rsn[:, 0, :], start=True, stop=True)
            t.matmul(pq[:, 256:512], ec[:, m], rsn[:, 1, :], start=True, stop=True)
            t.matmul(pq[:, 512:768], ec[:, m], btb[:, :], start=True, stop=True)
            ct.incd("T", t)
        t_tick = ct.n["T"]

        @blk.vector
        def _(v, m=m, pq=pq, t_tick=t_tick):
            v.wait_ge(ct.sems["T"], t_tick)
            v.tensor_tensor(qnT[:, m, :], qnT[:, m, :], pq[:, 0:256], op=OP.mult)
            v.tensor_tensor(knT[:, m, :], knT[:, m, :], pq[:, 256:512], op=OP.mult)
            v.tensor_tensor(vbT[:, m, :], vbT[:, m, :], pq[:, 512:768], op=OP.mult)
            v.drain()
            v.scalar_tensor_tensor(out=kbnT[:, m, :], in0=knT[:, m, :], scalar=-1.0,
                                   in1=pq[:, 512:768], op0=OP.mult, op1=OP.mult)
            ct.incd("V", v)
        rep_tick[m % 2] = ct.n["V"]
    pres_tick = ct.n["V"]

    if STG < 9:
        return
    @blk.sync
    def _(sy):
        sy.wait_ge(ct.sems["V"], pres_tick)
        for nm in ("qnT", "knT", "kbnT"):
            ct.dmaq("DX", sy, out=sb[nm + "0"][:, :, :], in_=sb[nm][64:128, :, :])
    h0_tick = ct.n["DX"]

    P.tr_seq([(vbT[:, m, i * 128:(i + 1) * 128], vtok[:, i, m * 128:(m + 1) * 128], None, None)
              for i in range(CPS) for m in range(8)] +
             [(knT[:, m, i * 128:(i + 1) * 128], ktok[:, i, m * 128:(m + 1) * 128], None, None)
              for i in range(CPS) for m in range(8)], pre=("V", pres_tick))
    vk_tick = ct.n["S"]

    if STG < 10:
        return
    if P.do_chunks:
        for i in range(CPS):
            _emit_chunk(P, seg * CPS + i, i, vk_tick, h0_tick)

    @blk.vector
    def _(v):
        ct.wait(v, "T")
        for srcb in (qpT, kpT, vpT):
            v.tensor_copy(srcb[:, :, 0:3], srcb[:, :, SEG:SEG + 3])
        ct.incd("V", v)


def _emit_chunk(P, ci, i, vk_tick, h0_tick):
    CS = getattr(P, "cstage", 99)
    nc, blk, ct, sb = P.nc, P.blk, P.ct, P.sb
    ps0, ps1, ps2, psT = P.ps0, P.ps1, P.ps2, P.psT
    qnT, knT, kbnT = sb["qnT"], sb["knT"], sb["kbnT"]
    qnT0, knT0, kbnT0 = sb["qnT0"], sb["knT0"], sb["kbnT0"]
    vtok, ktok = sb["vtok"], sb["ktok"]
    S_f32, S_bf = sb["S_f32"], sb["S_bf"]
    Xs = sb["Xs"]
    XTb = [sb["XT0"], sb["XT1"]]
    Yv = [sb["Ya"], sb["Yb"]]
    cb_ = sb["cacc8"][:, 4:6, :]
    Ubf = bass.AP(cb_.tensor, cb_.offset, [list(cb_.ap[0]), [1, 512]]).bitcast(BF16)
    MTb, rsc, owin = sb["MTb"], sb["rsc"], sb["owin"]
    ca = sb["cacc8"][:, 0:4, :]
    onrm = bass.AP(ca.tensor, ca.offset, [list(ca.ap[0]), [1, 1024]])
    ltm, utm, utim = sb["ltm"], sb["utm"], sb["utim"]
    nwrep, maskO, maskH = sb["nwrep"], sb["maskO_s"], sb["maskH_s"]

    pairs = {id(qnT): qnT0, id(knT): knT0, id(kbnT): kbnT0}

    def hsl(buf, h):
        if h % 2 == 0:
            return buf[0:64, h // 2, i * 128:(i + 1) * 128]
        return pairs[id(buf)][0:64, h // 2, i * 128:(i + 1) * 128]

    def Ssl(buf, h):
        return buf[0:64, h, :]

    def xsl(buf, h):
        return buf[:, 128 * h:128 * h + 128]

    def ysl(buf, h):
        if isinstance(buf, bass.AP):
            return bass.AP(buf.tensor, buf.offset + 64 * h * buf.ap[-1][0],
                           [list(buf.ap[0]), [buf.ap[-1][0], 64]])
        return buf[:, 64 * h:64 * h + 64]

    @blk.tensor
    def _(t):
        t.wait_ge(ct.sems["S"], vk_tick)
        t.wait_ge(ct.sems["DX"], h0_tick)
        ct.wait(t, "V")
        for h in range(H):
            t.matmul(ps0[:, 64 * h:64 * h + 64], hsl(kbnT, h), Ssl(S_bf, h),
                     start=True, stop=True)
        ct.incd("T", t)
    r_tick = ct.n["T"]

    Y0MODE = getattr(P, "y0mode", 2)
    @blk.vector
    def _(v):
        v.wait_ge(ct.sems["T"], r_tick)
        if Y0MODE == 2:
            v.scalar_tensor_tensor(out=r32(Yv[0][:, :]), in0=ps0[:, :], scalar=1.0,
                                   in1=vtok[:, i, :], op0=OP.mult, op1=OP.add)
        elif Y0MODE == 1:
            v.scalar_tensor_tensor(out=Yv[0][:, :], in0=ps0[:, :], scalar=1.0,
                                   in1=vtok[:, i, :], op0=OP.mult, op1=OP.add)
        ct.incd("V", v)
    y0_tick = ct.n["V"]

    if CS < 2:
        return
    @blk.tensor
    def _(t):
        ct.wait(t, "V")
        for h in range(H):
            pst = (ps1, ps2)[h // 8]
            t.matmul(pst[:, 128 * (h % 8):128 * (h % 8) + 128], hsl(kbnT, h), hsl(knT, h),
                     start=True, stop=True)
        ct.incd("T", t)
    xb_tick = ct.n["T"]

    @blk.vector
    def _(v):
        v.wait_ge(ct.sems["T"], xb_tick)
        for pi, pst in enumerate((ps1, ps2)):
            v.tensor_tensor(r32(split3(Xs[:, 1024 * pi:1024 * (pi + 1)], 8, 128)),
                            split3(pst[:, :], 8, 128),
                            rep_ap(ltm[:, :], 8, 1), op=OP.mult)
        ct.incd("V", v)
    xev_tick = ct.n["V"]

    if CS < 3:
        return
    @blk.tensor
    def _(t):
        t.wait_ge(ct.sems["V"], xev_tick)
        for h in range(H):
            pst = (ps1, ps2)[h // 8]
            t.matmul(pst[:, 128 * (h % 8):128 * (h % 8) + 128], hsl(knT, h), hsl(kbnT, h),
                     start=True, stop=True)
        ct.incd("T", t)
    xtb_tick = ct.n["T"]

    @blk.vector
    def _(v):
        v.wait_ge(ct.sems["T"], xtb_tick)
        for pi, pst in enumerate((ps1, ps2)):
            v.tensor_tensor(r32(split3(XTb[0][:, 1024 * pi:1024 * (pi + 1)], 8, 128)),
                            split3(pst[:, :], 8, 128),
                            rep_ap(utm[:, :], 8, 1), op=OP.mult)
        ct.incd("V", v)
    xtev_tick = ct.n["V"]

    if CS < 4:
        return
    ycur = 0
    xtc = 0
    for j in range(min(NLVL, CS - 3) if CS < 99 else NLVL):
        @blk.tensor
        def _(t, xtc=xtc, ycur=ycur, j=j):
            ct.wait(t, "V")
            if j > 0:
                ct.wait(t, "S")
            for h in range(H):
                t.matmul(ps0[:, 64 * h:64 * h + 64], r32(xsl(XTb[xtc], h)),
                         r32(ysl(Yv[ycur], h)), start=True, stop=True)
            ct.incd("T", t)
        sv_tick = ct.n["T"]

        @blk.vector
        def _(v, ycur=ycur, sv_tick=sv_tick):
            v.wait_ge(ct.sems["T"], sv_tick)
            v.drain()
            v.scalar_tensor_tensor(out=r32(Yv[1 - ycur][:, :]), in0=ps0[:, :], scalar=1.0,
                                   in1=Yv[ycur][:, :], op0=OP.mult, op1=OP.add)
            ct.incd("V", v)
        ycur = 1 - ycur

        if j < 6:
            @blk.tensor
            def _(t, xtc=xtc):
                ct.wait(t, "V")
                ct.wait(t, "S")
                for h in range(H):
                    pst = (ps1, ps2)[h // 8]
                    t.matmul(pst[:, 128 * (h % 8):128 * (h % 8) + 128],
                             r32(xsl(Xs, h)), r32(xsl(XTb[xtc], h)),
                             start=True, stop=True)
                ct.incd("T", t)
            xt2_tick = ct.n["T"]

            if j < 5:
                @blk.scalar
                def _(s, xtc=xtc, xt2_tick=xt2_tick):
                    s.wait_ge(ct.sems["T"], xt2_tick)
                    s.copy(r32(XTb[1 - xtc][:, 0:1024]), ps1[:, :])
                    s.copy(r32(XTb[1 - xtc][:, 1024:2048]), ps2[:, :])
                    ct.incd("S", s)
                xtn_tick = ct.n["S"]

                @blk.tensor
                def _(t, xtc=xtc, xtn_tick=xtn_tick):
                    t.wait_ge(ct.sems["S"], xtn_tick)
                    for h in range(H):
                        pst = (ps1, ps2)[h // 8]
                        t.matmul(pst[:, 128 * (h % 8):128 * (h % 8) + 128],
                                 r32(xsl(XTb[xtc], h)), r32(xsl(Xs, h)),
                                 start=True, stop=True)
                    ct.incd("T", t)
                x2_tick = ct.n["T"]

                @blk.scalar
                def _(s, x2_tick=x2_tick):
                    s.wait_ge(ct.sems["T"], x2_tick)
                    s.copy(r32(Xs[:, 0:1024]), ps1[:, :])
                    s.copy(r32(Xs[:, 1024:2048]), ps2[:, :])
                    ct.incd("S", s)
            else:
                @blk.scalar
                def _(s, xtc=xtc, xt2_tick=xt2_tick):
                    s.wait_ge(ct.sems["T"], xt2_tick)
                    s.copy(r32(XTb[1 - xtc][:, 0:1024]), ps1[:, :])
                    s.copy(r32(XTb[1 - xtc][:, 1024:2048]), ps2[:, :])
                    ct.incd("S", s)
            xtc = 1 - xtc

    if CS < 11:
        return
    yfin = ycur

    @blk.vector
    def _(v):
        v.drain()
        v.tensor_copy(Ubf, Yv[yfin][:, :])
        ct.incd("V", v)
    ubf_tick = ct.n["V"]

    if CS < 12:
        return
    @blk.tensor
    def _(t):
        ct.wait(t, "S")
        ct.wait(t, "V")
        for h in range(H):
            pst = (ps1, ps2)[h // 8]
            t.matmul(pst[:, 128 * (h % 8):128 * (h % 8) + 128], hsl(knT, h), hsl(qnT, h),
                     start=True, stop=True)
        ct.incd("T", t)
    mt_tick = ct.n["T"]

    @blk.vector
    def _(v):
        v.wait_ge(ct.sems["T"], mt_tick)
        for pi, pst in enumerate((ps1, ps2)):
            v.tensor_tensor(split3(MTb[:, 1024 * pi:1024 * (pi + 1)], 8, 128),
                            split3(pst[:, :], 8, 128),
                            rep_ap(utim[:, :], 8, 1), op=OP.mult)
        ct.incd("V", v)
    mtev_tick = ct.n["V"]

    if CS < 13:
        return
    @blk.tensor
    def _(t):
        t.wait_ge(ct.sems["V"], mtev_tick)
        for h in range(H):
            t.matmul(ps0[:, 64 * h:64 * h + 64], MTb[:, 128 * h:128 * h + 128],
                     ysl(Ubf, h), start=True, stop=False)
            t.matmul(ps0[:, 64 * h:64 * h + 64], hsl(qnT, h), Ssl(S_bf, h),
                     start=False, stop=True)
        ct.incd("T", t)
    o_tick = ct.n["T"]

    if CS < 14:
        return
    @blk.tensor
    def _(t):
        for h in range(H):
            t.matmul(ps1[0:64, 64 * h:64 * h + 64], ktok[:, i, 64 * h:64 * h + 64],
                     ysl(Ubf, h), start=True, stop=True)
        ct.incd("T", t)
    su_tick = ct.n["T"]

    @blk.vector
    def _(v):
        v.wait_ge(ct.sems["T"], su_tick)
        v.scalar_tensor_tensor(out=S_f32[:, :, :], in0=split3(ps1[0:64, :], H, HD),
                               scalar=1.0, in1=S_f32[:, :, :], op0=OP.mult, op1=OP.add)
        v.drain()
        v.tensor_copy(S_bf[:, :, :], S_f32[:, :, :])
        ct.incd("V", v)

    if CS < 15:
        return
    @blk.scalar
    def _(s):
        s.wait_ge(ct.sems["T"], o_tick)
        s.activation(onrm, ps0[:, :], AF.Square)
        ct.incd("S", s)
    osq_tick = ct.n["S"]

    @blk.vector
    def _(v):
        v.wait_ge(ct.sems["S"], osq_tick)
        v.tensor_reduce(rsc[:, 0:16], split3(onrm, 16, 64), axis=AX.X, op=OP.add)
        v.drain()
        v.tensor_scalar(rsc[:, 16:32], rsc[:, 0:16], 1.0 / 64.0, 1e-5,
                        op0=OP.mult, op1=OP.add)
        v.drain()
        v.reciprocal(rsc[:, 16:32], rsc[:, 16:32])
        ct.incd("V", v)
    rms1_tick = ct.n["V"]

    @blk.scalar
    def _(s):
        s.wait_ge(ct.sems["V"], rms1_tick)
        s.activation(rsc[:, 32:48], rsc[:, 16:32], AF.Sqrt)
        ct.incd("S", s)
    rms2_tick = ct.n["S"]

    @blk.vector
    def _(v):
        v.wait_ge(ct.sems["S"], rms2_tick)
        v.tensor_tensor(split3(onrm, 16, 64), split3(ps0[:, :], 16, 64),
                        rep_ap(rsc[:, 32:48], 64, 2), op=OP.mult)
        v.drain()
        v.tensor_tensor(onrm, onrm, nwrep[:, :], op=OP.mult)
        v.drain()
        slot = 1 + (ci % 16)
        if ci < 16:
            v.tensor_scalar_mul(owin[:, slot, :], onrm, maskO[:, ci:ci + 1])
        else:
            v.scalar_tensor_tensor(out=owin[:, slot, :], in0=onrm,
                                   scalar=maskO[:, ci:ci + 1], in1=owin[:, slot, :],
                                   op0=OP.mult, op1=OP.add)
        if ci == 15:
            v.tensor_scalar_mul(owin[:, 0, :], onrm, maskH[:, ci:ci + 1])
        ct.incd("V", v)


def build_program(dbg=(), n_seg=NSEG, do_chunks=True, stage=99, cstage=99, y0mode=2, rh=None,
                  p2=True, p3=True, p45=True):
    _wrap_tensor_engine()
    _LAST_T[0] = None
    nc = bass.Bass("TRN2", target_bir_lowering=False, debug=False)
    dram = {}

    def din(name, shape, dtype=F32):
        dram[name] = nc.dram_tensor(name, list(shape), dtype, kind="ExternalInput").ap()
        return dram[name]

    din("x_full", (T, D)); din("x_win", (WIN, D))
    din("maskO", (128, NCH)); din("maskH", (128, NCH))
    for nm in ("Wq", "Wk", "Wv", "Wg", "Wo"):
        din(nm, (D, D))
    din("Wb", (D, H))
    for nm in ("cqw", "ckw", "cvw"):
        din(nm, (128, 8, 4))
    for nm in ("cqb", "ckb", "cvb"):
        din(nm, (128, 8))
    din("nw_rep", (128, D))
    din("centroidsT", (D, E)); din("ebias_rep", (128, E))
    din("Ws_up", (D, FE)); din("Ws_down", (FE, D))
    din("W_up", (E, D, FE)); din("W_down", (E, FE, D))
    din("w_linT", (D, 496)); din("W1T", (D, HFF)); din("W2T", (HFF, D))
    din("b1", (1, HFF)); din("b2", (1, D))
    din("ident128", (128, 128), BF16); din("identf", (128, 128))
    din("ltmask", (128, 128), BF16); din("utmask", (128, 128), BF16)
    din("utimask", (128, 128), BF16)
    din("ones_col", (1, 128))
    din("Ec", (16, 8, 128), BF16); din("Esq", (128, 8, 16), BF16)

    y = nc.dram_tensor("y", [2048, D], F32, kind="ExternalOutput").ap()
    stag = nc.dram_tensor("stag", [H, 16 * 128 * 159], BF16, kind="Internal").ap()
    hbuf = nc.dram_tensor("hbuf", [2048, HFF], BF16,
                          kind="ExternalOutput" if any(d[0] == "tap_h" for d in dbg)
                          else "Internal").ap()
    dbg_out = {}
    for (nm, shape, dt) in dbg:
        if shape is not None:
            dbg_out[nm] = nc.dram_tensor(nm, list(shape), dt, kind="ExternalOutput").ap()

    ct = Ctr(nc)
    stk = contextlib.ExitStack()
    sb = {}

    def sbuf(name, shape, dtype, st=None):
        tlocal = (st or stk).enter_context(nc.sbuf_tensor(name, list(shape), dtype))
        sb[name] = tlocal
        return tlocal

    with stk:
        ct.make(stk, ["T", "V", "S", "G", "DX", "DW", "DS", "DO"])
        sbuf("ident", (128, 128), BF16)
        sbuf("identf_s", (128, 128), F32)
        sbuf("ltm", (128, 128), BF16)
        sbuf("utm", (128, 128), BF16)
        sbuf("utim", (128, 128), BF16)
        sbuf("onesc", (1, 128), F32)
        sbuf("ec", (16, 8, 128), BF16)
        sbuf("esq", (128, 8, 16), BF16)
        sbuf("nwrep", (128, D), F32)
        sbuf("maskO_s", (128, NCH), F32)
        sbuf("maskH_s", (128, NCH), F32)
        sbuf("magic", (128, 1), F32)
        sbuf("S_f32", (64, H, HD), F32)
        sbuf("S_bf", (64, H, HD), BF16)
        sbuf("cw_conv", (128, 3, 8, 4), F32)
        sbuf("cb_conv", (128, 3, 8), F32)

        ps0 = stk.enter_context(nc.psum_tensor("ps0", [128, 1024], F32))
        ps1 = stk.enter_context(nc.psum_tensor("ps1", [128, 1024], F32))
        ps2 = stk.enter_context(nc.psum_tensor("ps2", [128, 1024], F32))
        psT = stk.enter_context(nc.psum_tensor("psT", [128, 2048], BF16))

        blk = stk.enter_context(nc.Block())
        P = Prog(nc, blk, ct, sb, (ps0, ps1, ps2, psT), dram, dbg_out)
        P.do_chunks = do_chunks
        P.stage = stage
        P.cstage = cstage
        P.y0mode = y0mode
        if rh is not None:
            P.rh = rh

        ow = contextlib.ExitStack()
        sb["owin"] = ow.enter_context(nc.sbuf_tensor("owin", [128, NW, D], BF16, side="right"))

        @blk.sync
        def _(sy):
            for dst, src in (("ident", "ident128"), ("identf_s", "identf"),
                             ("ltm", "ltmask"), ("utm", "utmask"), ("utim", "utimask"),
                             ("onesc", "ones_col"), ("ec", "Ec"), ("esq", "Esq"),
                             ("nwrep", "nw_rep"), ("maskO_s", "maskO"), ("maskH_s", "maskH")):
                ct.dmaq("DX", sy, out=sb[dst][:], in_=dram[src][:])
            for i, nm in enumerate(("cqw", "ckw", "cvw")):
                ct.dmaq("DX", sy, out=sb["cw_conv"][:, i], in_=dram[nm][:])
            for i, nm in enumerate(("cqb", "ckb", "cvb")):
                ct.dmaq("DX", sy, out=sb["cb_conv"][:, i], in_=dram[nm][:])
        ticks = {"const": ct.n["DX"]}

        with contextlib.ExitStack() as p1:
            sbuf("wqkvg", (128, 3, 8, D), BF16, p1)
            sbuf("wb_s", (128, 8, H), BF16, p1)
            sbuf("xseg", (128, CPS, D), BF16, p1)
            sbuf("lnst", (128, 32), F32, p1)
            sbuf("xn_st", (128, CPS, D), BF16, p1)
            sbuf("xnT", (128, 8, SEG), BF16, p1)
            sbuf("qpT", (128, 8, 3 + SEG), BF16, p1)
            sbuf("kpT", (128, 8, 3 + SEG), BF16, p1)
            sbuf("vpT", (128, 8, 3 + SEG), BF16, p1)
            sbuf("cacc8", (128, 8, SEG), F32, p1)
            sbuf("sqb", (128, SEG), BF16, p1)
            sbuf("sqb2", (128, SEG), BF16, p1)
            sbuf("qnT", (128, 8, SEG), BF16, p1)
            sbuf("knT", (128, 8, SEG), BF16, p1)
            sbuf("kbnT", (128, 8, SEG), BF16, p1)
            sbuf("qnT0", (64, 8, SEG), BF16, p1)
            sbuf("knT0", (64, 8, SEG), BF16, p1)
            sbuf("kbnT0", (64, 8, SEG), BF16, p1)
            sbuf("vbT", (128, 8, SEG), BF16, p1)
            sbuf("vtok", (128, CPS, D), BF16, p1)
            sbuf("ktok", (128, CPS, D), BF16, p1)
            sbuf("beta", (128, CPS, H), F32, p1)
            sbuf("betaT", (16, SEG), F32, p1)
            sbuf("btb", (16, SEG), BF16, p1)
            sbuf("ssq", (16, 2, SEG), F32, p1)
            sbuf("rsn", (16, 2, SEG), BF16, p1)
            sbuf("Xs", (128, H * 128), F32, p1)
            sbuf("XT0", (128, H * 128), F32, p1)
            sbuf("XT1", (128, H * 128), F32, p1)
            sbuf("MTb", (128, H * 128), BF16, p1)
            sbuf("Ya", (128, H * HD), F32, p1)
            sbuf("Yb", (128, H * HD), F32, p1)
            sbuf("rsc", (128, 48), F32, p1)

            @blk.gpsimd
            def _(gp):
                for mi, nm in enumerate(("Wq", "Wk", "Wv")):
                    src = dram[nm].rearrange("(kc kp) d -> kp kc d", kp=128)
                    ct.dmaq("DW", gp, out=sb["wqkvg"][:, mi], in_=src)
                ct.dmaq("DW", gp, 
                    out=sb["wb_s"][:], in_=dram["Wb"].rearrange("(kc kp) h -> kp kc h", kp=128))
            ticks["l0w"] = ct.n["DW"]

            @blk.vector
            def _(v):
                v.memset(sb["magic"][:], MAGIC)
                v.memset(sb["S_f32"][:], 0.0)
                v.memset(sb["S_bf"][:], 0.0)
                v.memset(sb["owin"][:], 0.0)
                v.memset(sb["qpT"][:, :, 0:3], 0.0)
                v.memset(sb["kpT"][:, :, 0:3], 0.0)
                v.memset(sb["vpT"][:, :, 0:3], 0.0)
                ct.incd("V", v)
            ticks["zinit"] = ct.n["V"]

            for seg in range(n_seg):
                _emit_segment(P, seg, ticks)

            if "tap_owin" in dbg_out:
                @blk.sync
                def _(sy):
                    ct.wait(sy, "V")
                    ct.dmaq("DX", sy, 
                        out=dbg_out["tap_owin"].rearrange("(i p) d -> p i d", p=128),
                        in_=sb["owin"][:])

        sbuf("x1acc", (128, NW, D), F32)
        with contextlib.ExitStack() as p23:
            sbuf("xnwT", (128, 8, NW * 128), BF16, p23)
            if p2:
                _emit_phase2(P)
            if "tap_p2" in dbg_out:
                @blk.sync
                def _(sy):
                    ct.wait(sy, "V")
                    ct.dmaq("DX", sy, 
                        out=dbg_out["tap_p2"].rearrange("(i p) d -> p i d", p=128),
                        in_=sb["x1acc"][:])
                p2tap = ct.n["DX"]

                @blk.vector
                def _(v):
                    v.wait_ge(ct.sems["DX"], p2tap)
                    ct.incd("V", v)
            ow.close()
            if p3:
                _emit_phase3(P)
            if "tap_p3" in dbg_out:
                @blk.sync
                def _(sy):
                    ct.wait(sy, "V")
                    ct.dmaq("DX", sy, 
                        out=dbg_out["tap_p3"].rearrange("(i p) d -> p i d", p=128),
                        in_=sb["x1acc"][:])
                p3tap = ct.n["DX"]

                @blk.vector
                def _(v):
                    v.wait_ge(ct.sems["DX"], p3tap)
                    ct.incd("V", v)
        if p45:
            _emit_phase45(P, y, stag, hbuf)
        else:
            @blk.sync
            def _(sy):
                ct.wait(sy, "V")
                ct.dmaq("DO", sy, 
                    out=y.rearrange("(i p) d -> p i d", p=128),
                    in_=sb["x1acc"][:, 1:17, :])
                ct.wait(sy, "DO")

    return nc


def _emit_ln_tile(P, src_ap, dst_ap, ln, pre=None):
    blk, ct = P.blk, P.ct

    @blk.vector
    def _(v):
        if pre is not None:
            v.wait_ge(ct.sems[pre[0]], pre[1])
        v.drain()
        v.bn_stats(ln[:, 0:6], src_ap[:, 0:512])
        v.bn_stats(ln[:, 6:12], src_ap[:, 512:1024])
        v.drain()
        v.bn_aggr(ln[:, 12:14], ln[:, 0:12])
        v.drain()
        v.tensor_scalar_add(ln[:, 14:15], ln[:, 13:14], 1e-5)
        v.drain()
        v.reciprocal(ln[:, 15:16], ln[:, 14:15])
        ct.incd("V", v)
    t1 = ct.n["V"]

    @blk.scalar
    def _(s):
        s.wait_ge(ct.sems["V"], t1)
        s.activation(ln[:, 16:17], ln[:, 15:16], AF.Sqrt)
        ct.incd("S", s)
    t2 = ct.n["S"]

    @blk.vector
    def _(v):
        v.wait_ge(ct.sems["S"], t2)
        v.tensor_scalar(dst_ap, src_ap, ln[:, 12:13], ln[:, 16:17],
                        op0=OP.subtract, op1=OP.mult)
        ct.incd("V", v)
    return ct.n["V"]


def _phase_barrier(P, eng):
    ct = P.ct
    ct.wait(eng, "T")
    ct.wait(eng, "V")
    ct.wait(eng, "S")


def _emit_phase2(P):
    nc, blk, ct, sb, dram = P.nc, P.blk, P.ct, P.sb, P.dram
    ps0, ps1, ps2, psT = P.ps0, P.ps1, P.ps2, P.psT
    x1acc, xnwT, owin = sb["x1acc"], sb["xnwT"], sb["owin"]

    with contextlib.ExitStack() as p2:
        wg = p2.enter_context(nc.sbuf_tensor("wg_s", [128, 8, D], BF16))
        wo = p2.enter_context(nc.sbuf_tensor("wo_s", [128, 8, D], BF16))
        xnb = p2.enter_context(nc.sbuf_tensor("xnb", [128, 2, D], BF16))
        sgt = p2.enter_context(nc.sbuf_tensor("sgt", [128, 2, D], BF16))
        otT = p2.enter_context(nc.sbuf_tensor("otT", [128, 2, 8, 128], BF16))
        ln2 = p2.enter_context(nc.sbuf_tensor("ln2", [128, 32], F32))

        @blk.sync
        def _(sy):
            _phase_barrier(P, sy)
            ct.dmaq("DX", sy, 
                out=x1acc[:], in_=dram["x_win"].rearrange("(i p) d -> p i d", p=128))
        xw_tick = ct.n["DX"]

        @blk.gpsimd
        def _(gp):
            _phase_barrier(P, gp)
            for dst, nm in ((wg, "Wg"), (wo, "Wo")):
                ct.dmaq("DW", gp, 
                    out=dst[:], in_=dram[nm].rearrange("(kc kp) d -> kp kc d", kp=128))
        w2_tick = ct.n["DW"]

        xnb_tick = [0, 0]
        sg_tick = [0, 0]
        ot_tick = [0, 0]
        for i in range(NW):
            sl = i % 2
            pre = ("DX", xw_tick) if i == 0 else ("T", xnb_tick[sl])
            ln_tick = _emit_ln_tile(P, x1acc[:, i, :], xnb[:, sl, :], ln2, pre=pre)

            P.tr_seq([(xnb[:, sl, m * 128:(m + 1) * 128],
                       xnwT[:, m, i * 128:(i + 1) * 128], None, None) for m in range(8)],
                     pre=("V", ln_tick))
            xnb_tick[sl] = ct.n["T"]
            ev_tick = ct.n["S"]

            @blk.tensor
            def _(t, i=i, ev_tick=ev_tick, w=max(sg_tick)):
                t.wait_ge(ct.sems["S"], ev_tick)
                if i == 0:
                    t.wait_ge(ct.sems["DW"], w2_tick)
                    ct.wait(t, "V")
                if w:
                    t.wait_ge(ct.sems["S"], w)
                for half in range(2):
                    for kc in range(8):
                        t.matmul(ps0[:, 512 * half:512 * half + 512],
                                 xnwT[:, kc, i * 128:(i + 1) * 128],
                                 wg[:, kc, 512 * half:512 * half + 512],
                                 start=(kc == 0), stop=(kc == 7))
                ct.incd("T", t)
            g_tick = ct.n["T"]

            @blk.scalar
            def _(s, sl=sl, g_tick=g_tick):
                s.wait_ge(ct.sems["T"], g_tick)
                s.activation(sgt[:, sl, 0:512], ps0[:, 0:512], AF.Silu)
                s.activation(sgt[:, sl, 512:1024], ps0[:, 512:1024], AF.Silu)
                ct.incd("S", s)
            sg_tick[sl] = ct.n["S"]

            @blk.vector
            def _(v, i=i, sl=sl, w=sg_tick[sl]):
                v.wait_ge(ct.sems["S"], w)
                v.tensor_tensor(owin[:, i, :], owin[:, i, :], sgt[:, sl, :], op=OP.mult)
                ct.incd("V", v)
            om_tick = ct.n["V"]

            P.tr_seq([(owin[:, i, m * 128:(m + 1) * 128], otT[:, sl, m, :], None, None)
                      for m in range(8)], pre=("V", om_tick))
            otev_tick = ct.n["S"]

            @blk.tensor
            def _(t, sl=sl, otev_tick=otev_tick, w=ot_tick[sl]):
                t.wait_ge(ct.sems["S"], otev_tick)
                if w:
                    t.wait_ge(ct.sems["V"], w)
                for half in range(2):
                    for m in range(8):
                        t.matmul(ps1[:, 512 * half:512 * half + 512], otT[:, sl, m, :],
                                 wo[:, m, 512 * half:512 * half + 512],
                                 start=(m == 0), stop=(m == 7))
                ct.incd("T", t)
            woT_tick = ct.n["T"]

            @blk.vector
            def _(v, i=i, woT_tick=woT_tick):
                v.wait_ge(ct.sems["T"], woT_tick)
                for half in range(2):
                    v.scalar_tensor_tensor(
                        out=x1acc[:, i, 512 * half:512 * half + 512],
                        in0=ps1[:, 512 * half:512 * half + 512], scalar=1.0,
                        in1=x1acc[:, i, 512 * half:512 * half + 512],
                        op0=OP.mult, op1=OP.add)
                ct.incd("V", v)
            ot_tick[sl] = ct.n["V"]


def _emit_phase3(P):
    nc, blk, ct, sb, dram = P.nc, P.blk, P.ct, P.sb, P.dram
    ps0, ps1, ps2, psT = P.ps0, P.ps1, P.ps2, P.psT
    x1acc, xnwT = sb["x1acc"], sb["xnwT"]
    NT = [512, 512, 512, 512, 128]

    with contextlib.ExitStack() as p3:
        cenT = p3.enter_context(nc.sbuf_tensor("cenT", [128, 8, E], BF16))
        ebias = p3.enter_context(nc.sbuf_tensor("ebias", [128, E], F32))
        wsup = p3.enter_context(nc.sbuf_tensor("wsup", [128, 8, FE], BF16))
        wsdn = p3.enter_context(nc.sbuf_tensor("wsdn", [128, 4, D], BF16))
        aff = p3.enter_context(nc.sbuf_tensor("aff", [128, NW, E], F32))
        cw = p3.enter_context(nc.sbuf_tensor("cw", [128, NW, E], F32))
        hT = p3.enter_context(nc.sbuf_tensor("hT", [128, 4, NW * 128], BF16))
        wu = [p3.enter_context(nc.sbuf_tensor(f"wu{j}", [128, 8, FE], BF16)) for j in range(2)]
        wd = [p3.enter_context(nc.sbuf_tensor(f"wd{j}", [128, 4, D], BF16)) for j in range(2)]
        tkb = p3.enter_context(nc.sbuf_tensor("tkb", [128, E], F32))
        oha = p3.enter_context(nc.sbuf_tensor("oha", [128, 8, E], F32))
        ohs = p3.enter_context(nc.sbuf_tensor("ohs", [128, E], F32))
        tks = p3.enter_context(nc.sbuf_tensor("tks", [128, 32], F32))

        @blk.sync
        def _(sy):
            _phase_barrier(P, sy)
            ct.dmaq("DX", sy, out=ebias[:], in_=dram["ebias_rep"][:])
        eb_tick = ct.n["DX"]

        @blk.gpsimd
        def _(gp):
            _phase_barrier(P, gp)
            ct.dmaq("DW", gp, 
                out=cenT[:], in_=dram["centroidsT"].rearrange("(kc kp) e -> kp kc e", kp=128))
            ct.dmaq("DW", gp, 
                out=wsup[:], in_=dram["Ws_up"].rearrange("(kc kp) f -> kp kc f", kp=128))
            ct.dmaq("DW", gp, 
                out=wsdn[:], in_=dram["Ws_down"].rearrange("(fc fp) d -> fp fc d", fp=128))
        w3_tick = ct.n["DW"]

        aslot_tick = [0, 0]
        for i in range(NW):
            sl = i % 2
            pa = (ps2[0:128, 0:64], ps2[0:128, 512:576])[sl]

            @blk.tensor
            def _(t, i=i, pa=pa, w=aslot_tick[sl]):
                if i == 0:
                    t.wait_ge(ct.sems["DW"], w3_tick)
                    ct.wait(t, "V")
                if w:
                    t.wait_ge(ct.sems["V"], w)
                for kc in range(8):
                    t.matmul(pa, xnwT[:, kc, i * 128:(i + 1) * 128], cenT[:, kc],
                             start=(kc == 0), stop=(kc == 7))
                ct.incd("T", t)
            t_tick = ct.n["T"]

            @blk.vector
            def _(v, i=i, pa=pa, t_tick=t_tick):
                v.wait_ge(ct.sems["T"], t_tick)
                v.tensor_copy(aff[:, i, :], pa)
                ct.incd("V", v)
            aslot_tick[sl] = ct.n["V"]

        for i in range(NW):
            @blk.vector
            def _(v, i=i):
                if i == 0:
                    v.wait_ge(ct.sems["DX"], eb_tick)
                v.drain()
                v.tensor_tensor(tkb[:, :], aff[:, i, :], ebias[:, :], op=OP.add)
                for r in range(8):
                    v.drain()
                    v.reduce_max(tks[:, 0:1], tkb[:, :], axis=AX.X)
                    v.drain()
                    v.tensor_scalar(oha[:, r, :], tkb[:, :], tks[:, 0:1], None, op0=OP.is_equal)
                    v.drain()
                    v.tensor_tensor(ohs[:, :], aff[:, i, :], oha[:, r, :], op=OP.mult)
                    v.drain()
                    v.reduce_sum(tks[:, 8 + r:9 + r], ohs[:, :], axis=AX.X)
                    v.drain()
                    v.scalar_tensor_tensor(out=tkb[:, :], in0=oha[:, r, :], scalar=-1e30,
                                           in1=tkb[:, :], op0=OP.mult, op1=OP.add)
                ct.incd("V", v)
            sel_tick = ct.n["V"]

            @blk.scalar
            def _(s, sel_tick=sel_tick):
                s.wait_ge(ct.sems["V"], sel_tick)
                s.activation(tks[:, 16:24], tks[:, 8:16], AF.Sigmoid)
                ct.incd("S", s)
            sig_tick = ct.n["S"]

            @blk.vector
            def _(v, i=i, sig_tick=sig_tick):
                v.wait_ge(ct.sems["S"], sig_tick)
                v.tensor_reduce(tks[:, 24:25], tks[:, 16:24], axis=AX.X, op=OP.add)
                v.drain()
                v.tensor_scalar_add(tks[:, 24:25], tks[:, 24:25], 1e-8)
                v.drain()
                v.reciprocal(tks[:, 25:26], tks[:, 24:25])
                v.drain()
                v.tensor_scalar(tks[:, 16:24], tks[:, 16:24], tks[:, 25:26], None, op0=OP.mult)
                v.drain()
                v.tensor_scalar_mul(cw[:, i, :], oha[:, 0, :], tks[:, 16:17])
                for r in range(1, 8):
                    v.drain()
                    v.scalar_tensor_tensor(out=cw[:, i, :], in0=oha[:, r, :],
                                           scalar=tks[:, 16 + r:17 + r], in1=cw[:, i, :],
                                           op0=OP.mult, op1=OP.add)
                ct.incd("V", v)
        cw_tick = ct.n["V"]

        hs_tick = 0
        yv_tick = 0
        yT_tick = 0
        wdma_tick = [0, 0]
        pyb = (ps2[:, 0:512], ps2[:, 512:1024])
        for eidx in range(-1, E):
            par = eidx % 2
            if eidx >= 0:
                @blk.gpsimd
                def _(gp, eidx=eidx, par=par, w=wdma_tick[par]):
                    if w:
                        gp.wait_ge(ct.sems["T"], w)
                    ct.dmaq("DW", gp, 
                        out=wu[par][:],
                        in_=dram["W_up"][eidx].rearrange("(kc kp) f -> kp kc f", kp=128))
                    ct.dmaq("DW", gp, 
                        out=wd[par][:],
                        in_=dram["W_down"][eidx].rearrange("(fc fp) d -> fp fc d", fp=128))
                wdma = ct.n["DW"]
                wub, wdb = wu[par], wd[par]
            else:
                wdma = w3_tick
                wub, wdb = wsup, wsdn

            off = 0
            for nt, ntok in enumerate(NT):
                @blk.tensor
                def _(t, eidx=eidx, nt=nt, off=off, ntok=ntok, wdma=wdma,
                      hs_tick=hs_tick):
                    if nt == 0:
                        t.wait_ge(ct.sems["DW"], wdma)
                        if eidx == -1:
                            ct.wait(t, "V")
                    if hs_tick:
                        t.wait_ge(ct.sems["S"], hs_tick)
                    for m in range(4):
                        pb = (ps0, ps1)[m // 2][:, 512 * (m % 2):512 * (m % 2) + 512]
                        for kc in range(8):
                            t.matmul(pb[:, 0:ntok], wub[:, kc, 128 * m:128 * m + 128],
                                     xnwT[:, kc, off:off + ntok],
                                     start=(kc == 0), stop=(kc == 7))
                    ct.incd("T", t)
                h_tick = ct.n["T"]

                @blk.scalar
                def _(s, nt=nt, off=off, ntok=ntok, h_tick=h_tick):
                    s.wait_ge(ct.sems["T"], h_tick)
                    for m in range(4):
                        pb = (ps0, ps1)[m // 2][:, 512 * (m % 2):512 * (m % 2) + 512]
                        s.activation(hT[:, m, off:off + ntok], pb[:, 0:ntok], AF.Silu)
                    ct.incd("S", s)
                hs_tick = ct.n["S"]
                off += ntok

            for tt in range(NW):
                @blk.tensor
                def _(t, tt=tt, hs_tick=hs_tick, yv_tick=yv_tick):
                    if tt == 0:
                        t.wait_ge(ct.sems["S"], hs_tick)
                    if yv_tick:
                        t.wait_ge(ct.sems["V"], yv_tick)
                    for half in range(2):
                        for m in range(4):
                            t.matmul(pyb[half], hT[:, m, 128 * tt:128 * tt + 128],
                                     wdb[:, m, 512 * half:512 * half + 512],
                                     start=(m == 0), stop=(m == 3))
                    ct.incd("T", t)
                yT_tick = ct.n["T"]

                @blk.vector
                def _(v, eidx=eidx, tt=tt, yT_tick=yT_tick):
                    v.wait_ge(ct.sems["T"], yT_tick)
                    if eidx == -1 and tt == 0:
                        v.wait_ge(ct.sems["V"], cw_tick)
                    for half in range(2):
                        dst = x1acc[:, tt, 512 * half:512 * half + 512]
                        if eidx < 0:
                            v.scalar_tensor_tensor(out=dst, in0=pyb[half], scalar=1.0,
                                                   in1=dst, op0=OP.mult, op1=OP.add)
                        else:
                            v.scalar_tensor_tensor(out=dst, in0=pyb[half],
                                                   scalar=cw[:, tt, eidx:eidx + 1],
                                                   in1=dst, op0=OP.mult, op1=OP.add)
                    ct.incd("V", v)
                yv_tick = ct.n["V"]
            if eidx >= 0:
                wdma_tick[par] = yT_tick


def _emit_phase45(P, y, stag, hbuf):
    nc, blk, ct, sb, dram = P.nc, P.blk, P.ct, P.sb, P.dram
    ps0, ps1, ps2, psT = P.ps0, P.ps1, P.ps2, P.psT
    x1acc = sb["x1acc"]
    onesc = sb["onesc"]

    with contextlib.ExitStack() as p4:
        ln4 = p4.enter_context(nc.sbuf_tensor("ln4", [128, 32], F32))
        wsmstk = contextlib.ExitStack()
        wsm = wsmstk.enter_context(nc.sbuf_tensor("wsm", [128, 8, 16, 31], BF16))
        xq1stk = contextlib.ExitStack()
        xq1T = xq1stk.enter_context(nc.sbuf_tensor("xq1T", [128, 8, 2048], BF16, side="right"))
        xn1stk = contextlib.ExitStack()
        xn1 = xn1stk.enter_context(nc.sbuf_tensor("xn1", [128, NW, D], BF16, side="right"))

        with contextlib.ExitStack() as p4a:
            xn1T = p4a.enter_context(nc.sbuf_tensor("xn1T", [128, 8, 2048], BF16))
            wlt = p4a.enter_context(nc.sbuf_tensor("wlt", [128, 8, 496], BF16))
            wr = p4a.enter_context(nc.sbuf_tensor("wr", [128, 1, 496], F32))
            zer = p4a.enter_context(nc.sbuf_tensor("zer", [128, 1272], BF16))

            @blk.vector
            def _(v):
                v.memset(zer[:], 0.0)
                ct.incd("V", v)
            z_tick = ct.n["V"]

            @blk.scalar
            def _(s):
                s.wait_ge(ct.sems["V"], z_tick)
                half = 16 * 128 * 159 // 2
                for h in range(H):
                    for q in range(2):
                        ct.dmaq("DS", s, 
                            out=stag[h, q * half:(q + 1) * half].rearrange(
                                "(p f) -> p f", p=128),
                            in_=zer[:, :])
            zf_tick = ct.n["DS"]

            @blk.gpsimd
            def _(gp):
                _phase_barrier(P, gp)
                ct.dmaq("DW", gp, 
                    out=wlt[:], in_=dram["w_linT"].rearrange("(kc kp) f -> kp kc f", kp=128))
            wl_tick = ct.n["DW"]

            for i in range(NW):
                ln_tick = _emit_ln_tile(P, x1acc[:, i, :], xn1[:, i, :], ln4, pre=None)
                if i >= 1:
                    P.tr_seq([(xn1[:, i, m * 128:(m + 1) * 128],
                               xn1T[:, m, (i - 1) * 128:i * 128], None, None)
                              for m in range(8)], pre=("V", ln_tick))
            xn1_tick = ct.n["V"]
            xn1T_tick = ct.n["S"]

            wsl_tick = [0, 0]
            for bi in range(16):
                sl = 0
                pw = (ps2[0:128, 0:496], ps2[0:128, 512:1008])[bi % 2]

                @blk.tensor
                def _(t, bi=bi, pw=pw, w=wsl_tick[bi % 2]):
                    if bi == 0:
                        t.wait_ge(ct.sems["S"], xn1T_tick)
                        t.wait_ge(ct.sems["DW"], wl_tick)
                        ct.wait(t, "V")
                    if w:
                        t.wait_ge(ct.sems["V"], w)
                    t.wait_ge(ct.sems["V"], wsl_tick[0]) if False else None
                    tok0 = 128 * bi
                    for kc in range(8):
                        t.matmul(pw, xn1T[:, kc, tok0:tok0 + 128], wlt[:, kc],
                                 start=(kc == 0), stop=(kc == 7))
                    ct.incd("T", t)
                wp_tick = ct.n["T"]

                @blk.vector
                def _(v, pw=pw, wp_tick=wp_tick):
                    v.wait_ge(ct.sems["T"], wp_tick)
                    ct.wait(v, "S")
                    v.drain()
                    v.tensor_reduce(ln4[:, 0:16], split3(pw, 16, 31), axis=AX.X, op=OP.max)
                    v.drain()
                    v.tensor_tensor(split3(wr[:, 0, :], 16, 31), split3(pw, 16, 31),
                                    rep_ap(ln4[:, 0:16], 31, 2), op=OP.subtract)
                    ct.incd("V", v)
                sub_tick = ct.n["V"]

                @blk.scalar
                def _(s, sub_tick=sub_tick):
                    s.wait_ge(ct.sems["V"], sub_tick)
                    s.activation(wr[:, 0, :], wr[:, 0, :], AF.Exp)
                    ct.incd("S", s)
                ex_tick = ct.n["S"]

                @blk.vector
                def _(v, bi=bi, ex_tick=ex_tick):
                    v.wait_ge(ct.sems["S"], ex_tick)
                    if bi == 8:
                        v.wait_ge(ct.sems["DX"], stw1_tick)
                    w3 = split3(wr[:, 0, :], 16, 31)
                    v.drain()
                    v.tensor_reduce(ln4[:, 0:16], w3, axis=AX.X, op=OP.add)
                    v.drain()
                    v.reciprocal(ln4[:, 0:16], ln4[:, 0:16])
                    v.drain()
                    v.tensor_tensor(wsm[:, bi % 8], w3, rep_ap(ln4[:, 0:16], 31, 2),
                                    op=OP.mult)
                    ct.incd("V", v)
                wsl_tick[bi % 2] = ct.n["V"]
                if bi == 7:
                    sm_half = ct.n["V"]

                    @blk.sync
                    def _(sy, sm_half=sm_half):
                        sy.wait_ge(ct.sems["V"], sm_half)
                        sy.wait_ge(ct.sems["DS"], zf_tick)
                        dst = stag.rearrange("h (b t x) -> t b h x", b=16, t=128, x=159)
                        for hh in range(H):
                            ct.dmaq("DX", sy, out=dst[:, 0:8, hh, 0:31],
                                                      in_=wsm[:, :, hh, :])
                    stw1_tick = ct.n["DX"]
            sm_tick = ct.n["V"]

            @blk.sync
            def _(sy):
                sy.wait_ge(ct.sems["V"], sm_tick)
                dst = stag.rearrange("h (b t x) -> t b h x", b=16, t=128, x=159)
                for hh in range(H):
                    ct.dmaq("DX", sy, out=dst[:, 8:16, hh, 0:31],
                                              in_=wsm[:, :, hh, :])
            stw_tick = ct.n["DX"]

        with contextlib.ExitStack() as p4b:
            shift = p4b.enter_context(nc.sbuf_tensor("shift", [128, NW, D], BF16))
            weAt = p4b.enter_context(nc.sbuf_tensor("weAt", [128, 16, 158], BF16))
            weA = p4b.enter_context(nc.sbuf_tensor("weA", [128, 16, 128], BF16))
            weB = p4b.enter_context(nc.sbuf_tensor("weB", [32, 16, 128], BF16))

            @blk.scalar
            def _(s):
                _phase_barrier(P, s)
                s.wait_ge(ct.sems["V"], xn1_tick)
                for i in range(NW):
                    ct.dmaq("DS", s, out=shift[0:30, i, :], in_=xn1[98:128, i, :])
                    if i < NW - 1:
                        ct.dmaq("DS", s, out=shift[30:128, i, :],
                                                 in_=xn1[0:98, i + 1, :])
            sh_tick = ct.n["DS"]

            pslots = [ps0[:, 0:64], ps0[:, 512:576], ps1[:, 0:64], ps1[:, 512:576]]
            slot_v = [0, 0, 0, 0]
            web_tick = 0
            for h in range(H):
                @blk.sync
                def _(sy, h=h, w=web_tick):
                    if h == 0:
                        sy.wait_ge(ct.sems["DX"], stw_tick)
                        sy.wait_ge(ct.sems["DS"], sh_tick)
                        _phase_barrier(P, sy)
                    if w:
                        ct.wait(sy, "T", w)
                    srcT = bass.AP(stag.tensor, h * (16 * 128 * 159),
                                   [[158, 128], [20352, 16], [1, 158]])
                    ct.dmaq("DX", sy, out=weAt[:, :, :], in_=srcT)
                rb_tick = ct.n["DX"]

                @blk.tensor
                def _(t, rb_tick=rb_tick):
                    t.wait_ge(ct.sems["DX"], rb_tick)
                    ct.wait(t, "S")
                    for b_ in range(16):
                        t.transpose(psT[0:128, 128 * b_:128 * b_ + 128],
                                    weAt[:, b_, 0:128], sb["ident"][:, :])
                    ct.incd("T", t)
                trA_tick = ct.n["T"]

                @blk.scalar
                def _(s, trA_tick=trA_tick):
                    s.wait_ge(ct.sems["T"], trA_tick)
                    s.copy(weA[:, :, :], split3(psT[0:128, 0:2048], 16, 128))
                    ct.incd("S", s)
                evA_tick = ct.n["S"]

                @blk.tensor
                def _(t, evA_tick=evA_tick):
                    t.wait_ge(ct.sems["S"], evA_tick)
                    for b_ in range(16):
                        t.transpose(psT[0:30, 128 * b_:128 * b_ + 128],
                                    weAt[:, b_, 128:158], sb["ident"][:, :])
                    ct.incd("T", t)
                trB_tick = ct.n["T"]

                @blk.scalar
                def _(s, trB_tick=trB_tick):
                    s.wait_ge(ct.sems["T"], trB_tick)
                    s.copy(weB[0:30, :, :],
                           bass.AP(psT[0:30, 0:2048].tensor, psT[0:30, 0:2048].offset,
                                   [list(psT[0:30, 0:2048].ap[0]), [128, 16], [1, 128]]))
                    ct.incd("S", s)
                rb_tick = ct.n["S"]

                for bi in range(16):
                    k = h * 16 + bi
                    sl = k % 4
                    pband = pslots[sl]

                    @blk.tensor
                    def _(t, h=h, bi=bi, sl=sl, pband=pband, rb_tick=rb_tick, w=slot_v[sl]):
                        if bi == 0:
                            t.wait_ge(ct.sems["S"], rb_tick)
                        if w:
                            t.wait_ge(ct.sems["V"], w)
                        t.matmul(pband, weA[:, bi, :], shift[:, bi, 64 * h:64 * h + 64],
                                 start=True, stop=False)
                        t.matmul(pband, weB[0:30, bi, :], shift[0:30, bi + 1, 64 * h:64 * h + 64],
                                 start=False, stop=True)
                        ct.incd("T", t)
                    bT_tick = ct.n["T"]

                    @blk.vector
                    def _(v, h=h, bi=bi, pband=pband, bT_tick=bT_tick):
                        v.wait_ge(ct.sems["T"], bT_tick)
                        dst = x1acc[:, 1 + bi, 64 * h:64 * h + 64]
                        v.scalar_tensor_tensor(out=dst, in0=pband, scalar=1.0, in1=dst,
                                               op0=OP.mult, op1=OP.add)
                        ct.incd("V", v)
                    slot_v[sl] = ct.n["V"]
                web_tick = ct.n["T"]

            if "tap_p4" in P.dbg_out:
                @blk.sync
                def _(sy):
                    ct.wait(sy, "V")
                    ct.dmaq("DX", sy, 
                        out=P.dbg_out["tap_p4"].rearrange("(i p) d -> p i d", p=128),
                        in_=x1acc[:])
                p4tap = ct.n["DX"]

                @blk.vector
                def _(v):
                    v.wait_ge(ct.sems["DX"], p4tap)
                    ct.incd("V", v)

        with contextlib.ExitStack() as p4c:
            qf = p4c.enter_context(nc.sbuf_tensor("qf", [128, D], F32))
            qx = p4c.enter_context(nc.sbuf_tensor("qx", [128, 2, D], BF16))
            for ii in range(16):
                i = 1 + ii
                sl = ii % 2

                c0 = 3 * sl

                @blk.vector
                def _(v, i=i, sl=sl, c0=c0):
                    if i == 1:
                        ct.wait(v, "T")
                    else:
                        ct.wait(v, "S")
                    v.drain()
                    v.reduce_max(ln4[:, c0:c0 + 1], xn1[:, i, :], axis=AX.X,
                                 apply_absolute_value=True)
                    v.drain()
                    v.tensor_scalar(ln4[:, c0 + 1:c0 + 2], ln4[:, c0:c0 + 1], 1e-5, 1.0 / QB,
                                    op0=OP.max, op1=OP.mult)
                    v.drain()
                    v.reciprocal(ln4[:, c0 + 2:c0 + 3], ln4[:, c0 + 1:c0 + 2])
                    ct.incd("V", v)
                q1_tick = ct.n["V"]

                @blk.scalar
                def _(s, i=i, q1_tick=q1_tick, c0=c0):
                    s.wait_ge(ct.sems["V"], q1_tick)
                    ct.wait(s, "T")
                    s.activation(qf[:, :], xn1[:, i, :], AF.Identity,
                                 scale=ln4[:, c0 + 2:c0 + 3], bias=sb["magic"][:, :])
                    ct.incd("S", s)
                q2_tick = ct.n["S"]

                @blk.vector
                def _(v, sl=sl, q2_tick=q2_tick, c0=c0):
                    v.wait_ge(ct.sems["S"], q2_tick)
                    v.tensor_scalar(qf[:, :], qf[:, :], MAGIC, QB, op0=OP.subtract, op1=OP.min)
                    v.drain()
                    v.tensor_scalar(qx[:, sl, :], qf[:, :], -QB - 1.0, ln4[:, c0 + 1:c0 + 2],
                                    op0=OP.max, op1=OP.mult)
                    ct.incd("V", v)
                q3_tick = ct.n["V"]

                P.tr_seq([(qx[:, sl, m * 128:(m + 1) * 128],
                           xq1T[:, m, ii * 128:(ii + 1) * 128], None, None) for m in range(8)],
                         pre=("V", q3_tick))
        xn1stk.close()
        wsmstk.close()

        with contextlib.ExitStack() as p5:
            b1s = p5.enter_context(nc.sbuf_tensor("b1s", [1, HFF], BF16))
            b2s = p5.enter_context(nc.sbuf_tensor("b2s", [1, D], BF16))
            onesb = p5.enter_context(nc.sbuf_tensor("onesb", [1, 128], BF16))
            o128 = p5.enter_context(nc.sbuf_tensor("o128", [128, 1], F32))
            wsc = p5.enter_context(nc.sbuf_tensor("wsc", [1, 1], F32))

            @blk.gpsimd
            def _(gp):
                _phase_barrier(P, gp)
                ct.dmaq("DW", gp, out=b1s[:], in_=dram["b1"][:])
                ct.dmaq("DW", gp, out=b2s[:], in_=dram["b2"][:])
                ct.dmaq("DW", gp, out=onesb[:], in_=dram["ones_col"][:])
            bias_tick = ct.n["DW"]

            @blk.sync
            def _(sy):
                _phase_barrier(P, sy)
                ct.dmaq("DX", sy, out=o128[:], in_=dram["ones_col"].rearrange(
                    "o p -> p o"))
            o128_tick = ct.n["DX"]

            if "tap_b1" in P.dbg_out:
                @blk.sync
                def _(sy):
                    sy.wait_ge(ct.sems["DW"], bias_tick)
                    ct.dmaq("DX", sy, out=P.dbg_out["tap_b1"], in_=b1s[:])
                btap = ct.n["DX"]

                @blk.vector
                def _(v):
                    v.wait_ge(ct.sems["DX"], btap)
                    ct.incd("V", v)

            def quantize_W(name, nchunks, cw_, dst, wstg):
                tot = float(nchunks * 128 * cw_)
                wld = 0
                for ck in range(nchunks):
                    @blk.sync
                    def _(sy, ck=ck, wld=wld):
                        if ck == 0:
                            _phase_barrier(P, sy)
                            ct.wait(sy, "DS")
                        if wld:
                            ct.wait(sy, "V", wld)
                        ct.dmaq("DX", sy, out=wstg[:, 0:cw_],
                                                  in_=dram[name][128 * ck:128 * ck + 128, :])
                    d_tick = ct.n["DX"]

                    @blk.vector
                    def _(v, ck=ck, d_tick=d_tick):
                        v.wait_ge(ct.sems["DX"], d_tick)
                        v.drain()
                        if ck < 16:
                            v.reduce_sum(ln4[:, 8 + ck:9 + ck],
                                         wstg[:, 0:cw_], axis=AX.X, apply_absolute_value=True)
                        else:
                            v.reduce_sum(ln4[:, 24:25], wstg[:, 0:cw_], axis=AX.X,
                                         apply_absolute_value=True)
                            v.drain()
                            v.tensor_tensor(ln4[:, 8 + ck - 16:9 + ck - 16],
                                            ln4[:, 8 + ck - 16:9 + ck - 16],
                                            ln4[:, 24:25], op=OP.add)
                        ct.incd("V", v)
                    wld = ct.n["V"]
                ncol = 8 if nchunks <= 8 else 16

                @blk.tensor
                def _(t, wld=wld, ncol=ncol):
                    t.wait_ge(ct.sems["V"], wld)
                    t.wait_ge(ct.sems["DX"], o128_tick)
                    ct.wait(t, "V")
                    t.matmul(ps2[0:1, 0:ncol], o128[:, :], ln4[:, 8:8 + ncol],
                             start=True, stop=True)
                    ct.incd("T", t)
                s1_tick = ct.n["T"]

                @blk.vector
                def _(v, s1_tick=s1_tick, tot=tot, ncol=ncol):
                    v.wait_ge(ct.sems["T"], s1_tick)
                    v.tensor_reduce(wsc[0:1, 0:1], ps2[0:1, 0:ncol], axis=AX.X, op=OP.add)
                    v.drain()
                    v.tensor_scalar(wsc[0:1, 0:1], wsc[0:1, 0:1], 1.0 / tot, 1e-5,
                                    op0=OP.mult, op1=OP.max)
                    ct.incd("V", v)
                ws_tick = ct.n["V"]

                @blk.tensor
                def _(t, ws_tick=ws_tick):
                    t.wait_ge(ct.sems["V"], ws_tick)
                    t.matmul(ps2[0:128, 64:65], onesc[:, :], wsc[:, :],
                             start=True, stop=True)
                    ct.incd("T", t)
                bc_tick = ct.n["T"]

                @blk.vector
                def _(v, bc_tick=bc_tick):
                    v.wait_ge(ct.sems["T"], bc_tick)
                    v.tensor_copy(ln4[:, 0:1], ps2[0:128, 64:65])
                    v.drain()
                    v.reciprocal(ln4[:, 1:2], ln4[:, 0:1])
                    ct.incd("V", v)
                wsb_tick = ct.n["V"]

                qdone = 0
                for ck in range(nchunks):
                    @blk.sync
                    def _(sy, ck=ck, qdone=qdone, wsb_tick=wsb_tick):
                        if ck == 0:
                            sy.wait_ge(ct.sems["V"], wsb_tick)
                        if qdone:
                            sy.wait_ge(ct.sems["V"], qdone)
                        ct.dmaq("DX", sy, out=wstg[:, 0:cw_],
                                                  in_=dram[name][128 * ck:128 * ck + 128, :])
                    d_tick = ct.n["DX"]

                    @blk.scalar
                    def _(s, d_tick=d_tick):
                        s.wait_ge(ct.sems["DX"], d_tick)
                        ct.wait(s, "V")
                        s.activation(wstg[:, 0:cw_], wstg[:, 0:cw_], AF.Identity,
                                     scale=ln4[:, 1:2], bias=sb["magic"][:, :])
                        ct.incd("S", s)
                    qdone = ct.n["S"]

                    @blk.vector
                    def _(v, ck=ck, qdone=qdone):
                        v.wait_ge(ct.sems["S"], qdone)
                        v.tensor_scalar(wstg[:, 0:cw_], wstg[:, 0:cw_], MAGIC, 1.0,
                                        op0=OP.subtract, op1=OP.min)
                        v.drain()
                        v.tensor_scalar(dst[:, ck, :], wstg[:, 0:cw_], -1.0, ln4[:, 0:1],
                                        op0=OP.max, op1=OP.mult)
                        ct.incd("V", v)
                    qdone = ct.n["V"]
                return qdone

            with contextlib.ExitStack() as p5a:
                W1q = p5a.enter_context(nc.sbuf_tensor("W1q", [128, 8, HFF], BF16))
                with contextlib.ExitStack() as p5w:
                    wst1 = p5w.enter_context(nc.sbuf_tensor("wst1", [128, HFF], F32))
                    w1q_tick = quantize_W("W1T", 8, HFF, W1q, wst1)
                hbt = p5a.enter_context(nc.sbuf_tensor("hbt", [128, HFF], BF16))


                pslots5 = [ps0[:, 0:512], ps0[:, 512:1024], ps1[:, 0:512], ps1[:, 512:1024]]
                sl_tick = [0, 0, 0, 0]
                for tt in range(16):
                    hb_tick = 0
                    for ns in range(8):
                        k = tt * 8 + ns
                        sl = k % 4
                        pb = pslots5[sl]

                        @blk.tensor
                        def _(t, tt=tt, ns=ns, pb=pb, w=sl_tick[sl], w1q_tick=w1q_tick):
                            if tt == 0 and ns == 0:
                                t.wait_ge(ct.sems["V"], w1q_tick)
                                t.wait_ge(ct.sems["DW"], bias_tick)
                                ct.wait(t, "S")
                            if w:
                                t.wait_ge(ct.sems["S"], w)
                            import os
                            _nb = os.environ.get("KNB", "0") == "1"
                            for kc in range(8):
                                t.matmul(pb, xq1T[:, kc, 128 * tt:128 * tt + 128],
                                         W1q[:, kc, 512 * ns:512 * ns + 512],
                                         start=(kc == 0), stop=(_nb and kc == 7))
                            if not _nb:
                                t.matmul(pb, onesb[0:1, :], b1s[0:1, 512 * ns:512 * ns + 512],
                                         start=False, stop=True)
                            ct.incd("T", t)
                        m_tick = ct.n["T"]

                        @blk.scalar
                        def _(s, ns=ns, pb=pb, m_tick=m_tick, hb_tick=hb_tick):
                            import os
                            s.wait_ge(ct.sems["T"], m_tick)
                            if ns == 0 and hb_tick == 0:
                                ct.wait(s, "DS")
                            fn = AF.Identity if os.environ.get("KID", "0") == "1" else AF.Gelu
                            s.activation(hbt[:, 512 * ns:512 * ns + 512], pb, fn)
                            ct.incd("S", s)
                        sl_tick[sl] = ct.n["S"]
                    ge_tick = ct.n["S"]

                    @blk.scalar
                    def _(s, tt=tt, ge_tick=ge_tick):
                        s.wait_ge(ct.sems["S"], ge_tick)
                        s.drain()
                        ct.dmaq("DS", s, out=hbuf[128 * tt:128 * tt + 128, :],
                                                 in_=hbt[:, :])

                if "tap_w1q" in P.dbg_out:
                    @blk.sync
                    def _(sy):
                        ct.wait(sy, "S")
                        ct.dmaq("DX", sy, out=P.dbg_out["tap_w1q"], in_=W1q[:])
                    wtap2 = ct.n["DX"]

                    @blk.vector
                    def _(v):
                        v.wait_ge(ct.sems["DX"], wtap2)
                        ct.incd("V", v)
                if "tap_xq1" in P.dbg_out:
                    @blk.sync
                    def _(sy):
                        ct.wait(sy, "S")
                        ct.dmaq("DX", sy, out=P.dbg_out["tap_xq1"], in_=xq1T[:])
                    xtap2 = ct.n["DX"]

                    @blk.vector
                    def _(v):
                        v.wait_ge(ct.sems["DX"], xtap2)
                        ct.incd("V", v)
            xq1stk.close()
            with contextlib.ExitStack() as p5b:
                wst2 = p5b.enter_context(nc.sbuf_tensor("wst2", [128, D], F32))
                W2q = p5b.enter_context(nc.sbuf_tensor("W2q", [128, 32, D], BF16))
                hbt2 = p5b.enter_context(nc.sbuf_tensor("hbt2", [128, HFF], BF16))
                qf2 = p5b.enter_context(nc.sbuf_tensor("qf2", [128, HFF], F32))
                qx2 = p5b.enter_context(nc.sbuf_tensor("qx2", [128, HFF], BF16))
                xq2T = p5b.enter_context(nc.sbuf_tensor("xq2T", [128, 32, 128], BF16))
                w2q_tick = quantize_W("W2T", 32, D, W2q, wst2)

                for tt in range(16):
                    @blk.sync
                    def _(sy, tt=tt):
                        if tt == 0:
                            ct.wait(sy, "DS")
                        else:
                            ct.wait(sy, "S")
                        ct.dmaq("DX", sy, out=hbt2[:, :],
                                                  in_=hbuf[128 * tt:128 * tt + 128, :])
                    hld_tick = ct.n["DX"]

                    @blk.vector
                    def _(v, hld_tick=hld_tick):
                        v.wait_ge(ct.sems["DX"], hld_tick)
                        v.drain()
                        v.reduce_max(ln4[:, 4:5], hbt2[:, :], axis=AX.X, apply_absolute_value=True)
                        v.drain()
                        v.tensor_scalar(ln4[:, 5:6], ln4[:, 4:5], 1e-5, 1.0 / QB,
                                        op0=OP.max, op1=OP.mult)
                        v.drain()
                        v.reciprocal(ln4[:, 6:7], ln4[:, 5:6])
                        ct.incd("V", v)
                    qq1 = ct.n["V"]

                    @blk.scalar
                    def _(s, qq1=qq1, tt=tt):
                        s.wait_ge(ct.sems["V"], qq1)
                        if tt > 0:
                            ct.wait(s, "T")
                        s.activation(qf2[:, :], hbt2[:, :], AF.Identity,
                                     scale=ln4[:, 6:7], bias=sb["magic"][:, :])
                        ct.incd("S", s)
                    qq2 = ct.n["S"]

                    @blk.vector
                    def _(v, qq2=qq2):
                        v.wait_ge(ct.sems["S"], qq2)
                        v.tensor_scalar(qf2[:, :], qf2[:, :], MAGIC, QB,
                                        op0=OP.subtract, op1=OP.min)
                        v.drain()
                        v.tensor_scalar(qx2[:, :], qf2[:, :], -QB - 1.0, ln4[:, 5:6],
                                        op0=OP.max, op1=OP.mult)
                        ct.incd("V", v)
                    qq3 = ct.n["V"]

                    P.tr_seq([(qx2[:, m * 128:(m + 1) * 128], xq2T[:, m, :], None, None)
                              for m in range(32)], pre=("V", qq3))
                    x2_tick = ct.n["S"]

                    @blk.tensor
                    def _(t, tt=tt, x2_tick=x2_tick, w2q_tick=w2q_tick):
                        t.wait_ge(ct.sems["S"], x2_tick)
                        if tt == 0:
                            t.wait_ge(ct.sems["V"], w2q_tick)
                        else:
                            ct.wait(t, "V")
                        for half in range(2):
                            for kc in range(32):
                                t.matmul(ps2[:, 512 * half:512 * half + 512],
                                         xq2T[:, kc, :],
                                         W2q[:, kc, 512 * half:512 * half + 512],
                                         start=(kc == 0), stop=False)
                            t.matmul(ps2[:, 512 * half:512 * half + 512], onesb[0:1, :],
                                     b2s[0:1, 512 * half:512 * half + 512],
                                     start=False, stop=True)
                        ct.incd("T", t)
                    mm2_tick = ct.n["T"]

                    @blk.vector
                    def _(v, tt=tt, mm2_tick=mm2_tick):
                        v.wait_ge(ct.sems["T"], mm2_tick)
                        for half in range(2):
                            dst = x1acc[:, 1 + tt, 512 * half:512 * half + 512]
                            v.scalar_tensor_tensor(out=dst, in0=ps2[:, 512 * half:512 * half + 512],
                                                   scalar=1.0, in1=dst, op0=OP.mult, op1=OP.add)
                        ct.incd("V", v)

        @blk.sync
        def _(sy):
            ct.wait(sy, "V")
            ct.dmaq("DO", sy, 
                out=y.rearrange("(i p) d -> p i d", p=128),
                in_=x1acc[:, 1:17, :])
            ct.wait(sy, "DO")


def kernel(**inputs):
    in_maps = _host_prep(inputs)
    nc = build_program()
    res = run_bass_kernel_spmd(nc, in_maps, core_ids=list(range(8)))
    out = np.empty((B, T, D), np.float32)
    for core in range(8):
        b, g = core // 2, core % 2
        out[b, g * 2048:(g + 1) * 2048, :] = res.results[core]["y"]
    return out


# version 12 (final)
# speedup vs baseline: 1.0812x; 1.0525x over previous
import contextlib
import numpy as np
import ml_dtypes
import concourse.bass as bass
import concourse.mybir as mybir
from concourse.bass_utils import run_bass_kernel_spmd

F32 = mybir.dt.float32
F32R = mybir.dt.float32r
BF16 = mybir.dt.bfloat16
AF = mybir.ActivationFunctionType
OP = mybir.AluOpType
AX = mybir.AxisListType

bfd = ml_dtypes.bfloat16

B, T, D = 4, 4096, 1024
H, HD = 16, 64
C = 128
NCH = T // C
SEG = 256
NSEG = T // SEG
CPS = SEG // C
WIN = 2176
NW = WIN // 128
E, FE = 64, 512
HFF = 4096
QB = 127.0
MAGIC = 1.5 * 2.0 ** 23
NLVL = 7


def r32(ap):
    return ap.bitcast(F32R)


def rep_ap(ap, n, pos):
    lay = [list(x) for x in ap.ap]
    lay.insert(pos, [0, n])
    return bass.AP(ap.tensor, ap.offset, lay)


def split3(ap, a, b):
    lay = [list(ap.ap[0]), [b, a], [1, b]]
    return bass.AP(ap.tensor, ap.offset, lay)


_LAST_T = [None]
_LAST_S = [None]


def _wrap_tensor_engine():
    if getattr(bass.BassTensorEngine, "_mk_wrapped", False):
        return
    _omm = bass.BassTensorEngine.matmul
    _otr = bass.BassTensorEngine.transpose

    def _mm(self, *args, **kw):
        r = _omm(self, *args, **kw)
        _LAST_T[0] = r
        return r

    def _tr(self, *args, **kw):
        r = _otr(self, *args, **kw)
        _LAST_T[0] = r
        return r

    bass.BassTensorEngine.matmul = _mm
    bass.BassTensorEngine.transpose = _tr
    bass.BassTensorEngine._mk_wrapped = True
    _oact = bass.BassScalarEngine.activation
    _ocp = bass.BassScalarEngine.copy

    def _act(self, *args, **kw):
        r = _oact(self, *args, **kw)
        _LAST_S[0] = r
        return r

    def _cp(self, *args, **kw):
        r = _ocp(self, *args, **kw)
        _LAST_S[0] = r
        return r

    bass.BassScalarEngine.activation = _act
    bass.BassScalarEngine.copy = _cp


class Ctr:
    def __init__(self, nc):
        self.nc = nc
        self.sems = {}
        self.n = {}

    def make(self, stack, names):
        for nm in names:
            self.sems[nm] = stack.enter_context(self.nc.semaphore(nm))
            self.n[nm] = 0

    def inc(self, nm, instr, amt=1):
        instr.then_inc(self.sems[nm], amt)
        self.n[nm] += amt
        return self.n[nm]

    def wait(self, eng, nm, v=None):
        eng.wait_ge(self.sems[nm], self.n[nm] if v is None else v)

    def incd(self, nm, eng, amt=1):
        if nm == "T" and _LAST_T[0] is not None:
            ins = _LAST_T[0]
            _LAST_T[0] = None
            return self.inc(nm, ins, amt)
        if nm == "S" and _LAST_S[0] is not None:
            ins = _LAST_S[0]
            _LAST_S[0] = None
            return self.inc(nm, ins, amt)
        eng.drain()
        return self.inc(nm, eng.nop(), amt)

    def dmaq(self, nm, eng, *args, **kw):
        eng.wait_ge(self.sems[nm], self.n[nm])
        return self.inc(nm, eng.dma_start(*args, **kw), 16)


def _host_prep(inp):
    f32 = np.float32
    x = np.asarray(inp["L0_attn_x"], f32)

    com = {}
    for nm, src in (("Wq", "L0_attn_Wq"), ("Wk", "L0_attn_Wk"), ("Wv", "L0_attn_Wv"),
                    ("Wg", "L0_attn_Wg"), ("Wo", "L0_attn_Wo"), ("Wb", "L0_attn_Wb"),
                    ("Ws_up", "L0_ffn_Ws_up"), ("Ws_down", "L0_ffn_Ws_down"),
                    ("W_up", "L0_ffn_W_up"), ("W_down", "L0_ffn_W_down")):
        com[nm] = np.ascontiguousarray(np.asarray(inp[src], f32))
    for nm, src in (("cqw", "L0_attn_cq_w"), ("ckw", "L0_attn_ck_w"), ("cvw", "L0_attn_cv_w")):
        com[nm] = np.ascontiguousarray(
            np.asarray(inp[src], f32).reshape(8, 128, 4).transpose(1, 0, 2))
    for nm, src in (("cqb", "L0_attn_cq_b"), ("ckb", "L0_attn_ck_b"), ("cvb", "L0_attn_cv_b")):
        com[nm] = np.ascontiguousarray(np.asarray(inp[src], f32).reshape(8, 128).T)
    com["nw_rep"] = np.ascontiguousarray(
        np.broadcast_to(np.tile(np.asarray(inp["L0_attn_nw"], f32), H), (128, D)))
    com["centroidsT"] = np.ascontiguousarray(np.asarray(inp["L0_ffn_centroids"], f32).T)
    com["ebias_rep"] = np.ascontiguousarray(
        np.broadcast_to(np.asarray(inp["L0_ffn_expert_biases"], f32), (128, E)))
    com["w_linT"] = np.ascontiguousarray(np.asarray(inp["L1_attn_w_lin"], f32).T)
    com["W1T"] = np.ascontiguousarray(np.asarray(inp["L1_ffn_W1"], f32).T)
    com["W2T"] = np.ascontiguousarray(np.asarray(inp["L1_ffn_W2"], f32).T)
    com["b1"] = np.asarray(inp["L1_ffn_b1"], f32).reshape(1, HFF)
    com["b2"] = np.asarray(inp["L1_ffn_b2"], f32).reshape(1, D)

    ti = np.arange(128)
    com["ident128"] = np.eye(128, dtype=bfd)
    com["identf"] = np.eye(128, dtype=f32)
    com["ltmask"] = (ti[:, None] > ti[None, :]).astype(bfd)
    com["utmask"] = (ti[:, None] < ti[None, :]).astype(bfd)
    com["utimask"] = (ti[:, None] <= ti[None, :]).astype(bfd)
    com["ones_col"] = np.ones((1, 128), f32)
    Ec = np.zeros((8, 16, 128), bfd)
    for c in range(8):
        Ec[c, 2 * c, 0:64] = 1
        Ec[c, 2 * c + 1, 64:128] = 1
    com["Ec"] = np.ascontiguousarray(Ec.transpose(1, 0, 2))
    com["Esq"] = np.ascontiguousarray(Ec.transpose(2, 0, 1))

    in_maps = []
    for core in range(8):
        b, g = core // 2, core % 2
        m = dict(com)
        m["x_full"] = np.ascontiguousarray(x[b])
        start = g * 2048
        lo = start - 128
        xw = np.zeros((WIN, D), f32)
        s0 = max(lo, 0)
        xw[s0 - lo:, :] = x[b, s0:start + 2048, :]
        m["x_win"] = xw
        mo = np.zeros(NCH, f32)
        mo[g * 16:(g + 1) * 16] = 1.0
        mh = np.zeros(NCH, f32)
        if g == 1:
            mh[15] = 1.0
        m["maskO"] = np.ascontiguousarray(np.broadcast_to(mo, (128, NCH)))
        m["maskH"] = np.ascontiguousarray(np.broadcast_to(mh, (128, NCH)))
        in_maps.append(m)
    return in_maps


class Prog:

    def __init__(self, nc, blk, ct, sb, ps, dram, dbg_out):
        self.nc, self.blk, self.ct, self.sb, self.dram = nc, blk, ct, sb, dram
        self.ps0, self.ps1, self.ps2, self.psT = ps
        self.dbg_out = dbg_out
        self.trtick = {"bf": [0, 0], "f32": [0, 0]}

    def tr_seq(self, items, ident_ap=None, f32psum=False, pre=None):
        blk, ct = self.blk, self.ct
        ident = ident_ap if ident_ap is not None else self.sb["ident"][:, :]
        slot_tick = self.trtick["f32" if f32psum else "bf"]
        for idx, (src, dst, fn, bias) in enumerate(items):
            sl = idx % 2
            p_in = src.ap[0][1]
            f_in = src.ap[-1][1]
            if f32psum:
                slot = (self.ps2[0:f_in, 0:p_in], self.ps2[0:f_in, 512:512 + p_in])[sl]
            else:
                slot = (self.psT[0:f_in, 0:p_in], self.psT[0:f_in, 1024:1024 + p_in])[sl]

            @blk.tensor
            def _(t, src=src, slot=slot, sl=sl, idx=idx):
                if idx == 0 and pre is not None:
                    t.wait_ge(ct.sems[pre[0]], pre[1])
                if slot_tick[sl]:
                    t.wait_ge(ct.sems["S"], slot_tick[sl])
                t.transpose(slot, src, ident)
                ct.incd("T", t)
            t_tick = ct.n["T"]

            @blk.scalar
            def _(s, dst=dst, fn=fn, bias=bias, slot=slot, t_tick=t_tick):
                s.wait_ge(ct.sems["T"], t_tick)
                if fn is None:
                    s.copy(dst, slot)
                elif bias is None:
                    s.activation(dst, slot, fn)
                else:
                    s.activation(dst, slot, fn, bias=bias)
                ct.incd("S", s)
            slot_tick[sl] = ct.n["S"]


def _emit_segment(P, seg, ticks):
    STG = getattr(P, "stage", 99)
    nc, blk, ct, sb, dram = P.nc, P.blk, P.ct, P.sb, P.dram
    ps0, ps1, ps2, psT = P.ps0, P.ps1, P.ps2, P.psT
    xseg, lnst, xnT = sb["xseg"], sb["lnst"], sb["xnT"]
    qpT, kpT, vpT = sb["qpT"], sb["kpT"], sb["vpT"]
    sqb, sqb2 = sb["sqb"], sb["sqb2"]
    qnT, knT, kbnT, vbT = sb["qnT"], sb["knT"], sb["kbnT"], sb["vbT"]
    vtok, ktok = sb["vtok"], sb["ktok"]
    beta, betaT, btb = sb["beta"], sb["betaT"], sb["btb"]
    ssq, rsn = sb["ssq"], sb["rsn"]
    cw_conv, cb_conv = sb["cw_conv"], sb["cb_conv"]
    ec, esq = sb["ec"], sb["esq"]
    wqkvg, wb_s = sb["wqkvg"], sb["wb_s"]
    xn_st = sb["xn_st"]

    @blk.gpsimd
    def _(gp):
        if seg == 0:
            gp.wait_ge(ct.sems["DX"], ticks["const"])
            gp.wait_ge(ct.sems["V"], ticks["zinit"])
        else:
            ct.wait(gp, "V")
        src = dram["x_full"][seg * SEG:(seg + 1) * SEG, :].rearrange("(i p) d -> p i d", p=128)
        ct.dmaq("DW", gp, out=xseg[:], in_=src)
    x_tick = ct.n["DW"]

    @blk.vector
    def _(v):
        v.wait_ge(ct.sems["DW"], x_tick)
        if seg > 0:
            ct.wait(v, "T")
        for i in range(CPS):
            xt = xseg[:, i]
            v.bn_stats(lnst[:, 12 * i:12 * i + 6], xt[:, 0:512])
            v.bn_stats(lnst[:, 12 * i + 6:12 * i + 12], xt[:, 512:1024])
        v.drain()
        for i in range(CPS):
            v.bn_aggr(lnst[:, 24 + 2 * i:26 + 2 * i], lnst[:, 12 * i:12 * i + 12])
        v.drain()
        for i in range(CPS):
            v.tensor_scalar_add(lnst[:, 28 + i:29 + i], lnst[:, 25 + 2 * i:26 + 2 * i], 1e-5)
        v.drain()
        for i in range(CPS):
            v.reciprocal(lnst[:, 30 + i:31 + i], lnst[:, 28 + i:29 + i])
        ct.incd("V", v)
    lnv_tick = ct.n["V"]

    @blk.scalar
    def _(s):
        s.wait_ge(ct.sems["V"], lnv_tick)
        for i in range(CPS):
            s.activation(lnst[:, 28 + i:29 + i], lnst[:, 30 + i:31 + i], AF.Sqrt)
        ct.incd("S", s)
    rstd_tick = ct.n["S"]

    @blk.vector
    def _(v):
        v.wait_ge(ct.sems["S"], rstd_tick)
        for i in range(CPS):
            v.tensor_scalar(xn_st[:, i], xseg[:, i],
                            lnst[:, 24 + 2 * i:25 + 2 * i], lnst[:, 28 + i:29 + i],
                            op0=OP.subtract, op1=OP.mult)
        ct.incd("V", v)
    xn_tick = ct.n["V"]

    if STG < 2:
        return
    @blk.tensor
    def _(t):
        if seg == 0:
            t.wait_ge(ct.sems["DW"], ticks["l0w"])
        t.nop()
    P.tr_seq([(xn_st[:, i, m * 128:(m + 1) * 128], xnT[:, m, i * 128:(i + 1) * 128], None, None)
              for i in range(CPS) for m in range(8)], pre=("V", xn_tick))
    xnT_tick = ct.n["S"]

    if STG < 3:
        return
    slots = [ps0[:, 0:256], ps0[:, 512:768], ps1[:, 0:256], ps1[:, 512:768]]
    slot_tick = [0, 0, 0, 0]
    kproj = 0
    for mi in range(3):
        for m in range(8):
            sl = kproj % 4
            kproj += 1
            pst = slots[sl]

            @blk.tensor
            def _(t, mi=mi, m=m, pst=pst, sl=sl, first=(mi == 0 and m == 0)):
                if first:
                    t.wait_ge(ct.sems["S"], xnT_tick)
                    ct.wait(t, "V")
                if slot_tick[sl]:
                    t.wait_ge(ct.sems["S"], slot_tick[sl])
                for kc in range(8):
                    t.matmul(pst, wqkvg[:, mi, kc, m * 128:(m + 1) * 128], xnT[:, kc, :],
                             start=(kc == 0), stop=(kc == 7))
                ct.incd("T", t)
            t_tick = ct.n["T"]

            @blk.scalar
            def _(s, mi=mi, m=m, pst=pst, t_tick=t_tick):
                s.wait_ge(ct.sems["T"], t_tick)
                s.copy((qpT, kpT, vpT)[mi][:, m, 3:3 + SEG], pst)
                ct.incd("S", s)
            slot_tick[sl] = ct.n["S"]
    proj_tick = ct.n["S"]

    if STG < 4:
        return
    bslot_tick = 0
    for i in range(CPS):
        @blk.tensor
        def _(t, i=i):
            if bslot_tick:
                t.wait_ge(ct.sems["S"], bslot_tick)
            else:
                t.wait_ge(ct.sems["S"], xnT_tick)
            for kc in range(8):
                t.matmul(ps2[:, 0:16], xnT[:, kc, i * 128:(i + 1) * 128], wb_s[:, kc],
                         start=(kc == 0), stop=(kc == 7))
            ct.incd("T", t)
        t_tick = ct.n["T"]

        @blk.scalar
        def _(s, i=i, t_tick=t_tick):
            s.wait_ge(ct.sems["T"], t_tick)
            s.activation(beta[:, i], ps2[:, 0:16], AF.Sigmoid)
            ct.incd("S", s)
        bslot_tick = ct.n["S"]

    if STG < 5:
        return
    P.tr_seq([(beta[:, i], betaT[:, i * 128:(i + 1) * 128], None, None) for i in range(CPS)],
             ident_ap=sb["identf_s"][:, :], f32psum=True, pre=("S", bslot_tick))
    betaT_tick = ct.n["S"]

    @blk.vector
    def _(v):
        v.wait_ge(ct.sems["S"], betaT_tick)
        v.tensor_copy(btb[:, :], betaT[:, :])
        ct.incd("V", v)
    btb_tick = ct.n["V"]

    if STG < 6:
        return
    cacc8 = sb["cacc8"]
    sil_tick = 0
    for mi, dst in ((0, qnT), (1, knT), (2, vbT)):
        src = (qpT, kpT, vpT)[mi]

        @blk.vector
        def _(v, mi=mi, src=src, sil_tick=sil_tick, first=(mi == 0)):
            if first:
                v.wait_ge(ct.sems["S"], proj_tick)
            if sil_tick:
                v.wait_ge(ct.sems["S"], sil_tick)
            for m in range(8):
                v.tensor_scalar_mul(cacc8[:, m], src[:, m, 0:SEG], cw_conv[:, mi, m, 0:1])
            for j in (1, 2, 3):
                v.drain()
                for m in range(8):
                    v.scalar_tensor_tensor(out=cacc8[:, m], in0=src[:, m, j:j + SEG],
                                           scalar=cw_conv[:, mi, m, j:j + 1],
                                           in1=cacc8[:, m], op0=OP.mult, op1=OP.add)
            ct.incd("V", v)
        v_tick = ct.n["V"]

        @blk.scalar
        def _(s, mi=mi, dst=dst, v_tick=v_tick):
            s.wait_ge(ct.sems["V"], v_tick)
            for m in range(8):
                s.activation(dst[:, m, :], cacc8[:, m], AF.Silu, bias=cb_conv[:, mi, m:m + 1])
            ct.incd("S", s)
        sil_tick = ct.n["S"]
    conv_tick = ct.n["S"]

    if STG < 7:
        return
    sq_slot = [0, 0]
    for qi, srcb in ((0, qnT), (1, knT)):
        pdst = ps2[0:16, 0:256] if qi == 0 else ps2[0:16, 512:768]
        for m in range(8):
            @blk.scalar
            def _(s, qi=qi, m=m, srcb=srcb, w=sq_slot[m % 2], first=(qi == 0 and m == 0)):
                if first:
                    s.drain()
                if w:
                    s.wait_ge(ct.sems["T"], w)
                sq = (sqb, sqb2)[m % 2]
                s.activation(sq[:, :], srcb[:, m, :], AF.Square)
                ct.incd("S", s)
            s_tick = ct.n["S"]

            @blk.tensor
            def _(t, qi=qi, m=m, pdst=pdst, s_tick=s_tick):
                t.wait_ge(ct.sems["S"], s_tick)
                sq = (sqb, sqb2)[m % 2]
                t.matmul(pdst, esq[:, m], sq[:, :], start=(m == 0), stop=(m == 7))
                ct.incd("T", t)
            sq_slot[m % 2] = ct.n["T"]
    ssq_mm_tick = ct.n["T"]

    @blk.vector
    def _(v):
        v.wait_ge(ct.sems["T"], ssq_mm_tick)
        v.tensor_scalar_add(ssq[:, 0, :], ps2[0:16, 0:256], 1e-6)
        v.tensor_scalar_add(ssq[:, 1, :], ps2[0:16, 512:768], 1e-6)
        v.drain()
        v.reciprocal(ssq[:, 0, :], ssq[:, 0, :])
        v.reciprocal(ssq[:, 1, :], ssq[:, 1, :])
        ct.incd("V", v)
    rs_tick = ct.n["V"]

    @blk.scalar
    def _(s):
        s.wait_ge(ct.sems["V"], rs_tick)
        s.activation(rsn[:, 0, :], ssq[:, 0, :], AF.Sqrt, scale=1.0 / 64.0)
        s.activation(rsn[:, 1, :], ssq[:, 1, :], AF.Sqrt)
        ct.incd("S", s)
    rsn_tick = ct.n["S"]

    if STG < 8:
        return
    rep_tick = [0, 0]
    for m in range(8):
        pq = (ps0, ps1)[m % 2]

        @blk.tensor
        def _(t, m=m, pq=pq):
            if m == 0:
                t.wait_ge(ct.sems["S"], rsn_tick)
                ct.wait(t, "V")
            if rep_tick[m % 2]:
                t.wait_ge(ct.sems["V"], rep_tick[m % 2])
            t.matmul(pq[:, 0:256], ec[:, m], rsn[:, 0, :], start=True, stop=True)
            t.matmul(pq[:, 256:512], ec[:, m], rsn[:, 1, :], start=True, stop=True)
            t.matmul(pq[:, 512:768], ec[:, m], btb[:, :], start=True, stop=True)
            ct.incd("T", t)
        t_tick = ct.n["T"]

        @blk.vector
        def _(v, m=m, pq=pq, t_tick=t_tick):
            v.wait_ge(ct.sems["T"], t_tick)
            v.tensor_tensor(qnT[:, m, :], qnT[:, m, :], pq[:, 0:256], op=OP.mult)
            v.tensor_tensor(knT[:, m, :], knT[:, m, :], pq[:, 256:512], op=OP.mult)
            v.tensor_tensor(vbT[:, m, :], vbT[:, m, :], pq[:, 512:768], op=OP.mult)
            v.drain()
            v.scalar_tensor_tensor(out=kbnT[:, m, :], in0=knT[:, m, :], scalar=-1.0,
                                   in1=pq[:, 512:768], op0=OP.mult, op1=OP.mult)
            ct.incd("V", v)
        rep_tick[m % 2] = ct.n["V"]
    pres_tick = ct.n["V"]

    if STG < 9:
        return
    @blk.sync
    def _(sy):
        sy.wait_ge(ct.sems["V"], pres_tick)
        for nm in ("qnT", "knT", "kbnT"):
            ct.dmaq("DX", sy, out=sb[nm + "0"][:, :, :], in_=sb[nm][64:128, :, :])
    h0_tick = ct.n["DX"]

    P.tr_seq([(vbT[:, m, i * 128:(i + 1) * 128], vtok[:, i, m * 128:(m + 1) * 128], None, None)
              for i in range(CPS) for m in range(8)] +
             [(knT[:, m, i * 128:(i + 1) * 128], ktok[:, i, m * 128:(m + 1) * 128], None, None)
              for i in range(CPS) for m in range(8)], pre=("V", pres_tick))
    vk_tick = ct.n["S"]

    if STG < 10:
        return
    if P.do_chunks:
        for i in range(CPS):
            _emit_chunk(P, seg * CPS + i, i, vk_tick, h0_tick)

    @blk.vector
    def _(v):
        ct.wait(v, "T")
        for srcb in (qpT, kpT, vpT):
            v.tensor_copy(srcb[:, :, 0:3], srcb[:, :, SEG:SEG + 3])
        ct.incd("V", v)


def _emit_chunk(P, ci, i, vk_tick, h0_tick):
    CS = getattr(P, "cstage", 99)
    nc, blk, ct, sb = P.nc, P.blk, P.ct, P.sb
    ps0, ps1, ps2, psT = P.ps0, P.ps1, P.ps2, P.psT
    qnT, knT, kbnT = sb["qnT"], sb["knT"], sb["kbnT"]
    qnT0, knT0, kbnT0 = sb["qnT0"], sb["knT0"], sb["kbnT0"]
    vtok, ktok = sb["vtok"], sb["ktok"]
    S_f32, S_bf = sb["S_f32"], sb["S_bf"]
    Xs = sb["Xs"]
    XTb = [sb["XT0"], sb["XT1"]]
    Yv = [sb["Ya"], sb["Yb"]]
    cb_ = sb["cacc8"][:, 4:6, :]
    Ubf = bass.AP(cb_.tensor, cb_.offset, [list(cb_.ap[0]), [1, 512]]).bitcast(BF16)
    MTb, rsc, owin = sb["MTb"], sb["rsc"], sb["owin"]
    ca = sb["cacc8"][:, 0:4, :]
    onrm = bass.AP(ca.tensor, ca.offset, [list(ca.ap[0]), [1, 1024]])
    ltm, utm, utim = sb["ltm"], sb["utm"], sb["utim"]
    nwrep, maskO, maskH = sb["nwrep"], sb["maskO_s"], sb["maskH_s"]

    pairs = {id(qnT): qnT0, id(knT): knT0, id(kbnT): kbnT0}

    def hsl(buf, h):
        if h % 2 == 0:
            return buf[0:64, h // 2, i * 128:(i + 1) * 128]
        return pairs[id(buf)][0:64, h // 2, i * 128:(i + 1) * 128]

    def Ssl(buf, h):
        return buf[0:64, h, :]

    def xsl(buf, h):
        return buf[:, 128 * h:128 * h + 128]

    def ysl(buf, h):
        if isinstance(buf, bass.AP):
            return bass.AP(buf.tensor, buf.offset + 64 * h * buf.ap[-1][0],
                           [list(buf.ap[0]), [buf.ap[-1][0], 64]])
        return buf[:, 64 * h:64 * h + 64]

    @blk.tensor
    def _(t):
        t.wait_ge(ct.sems["S"], vk_tick)
        t.wait_ge(ct.sems["DX"], h0_tick)
        ct.wait(t, "V")
        for h in range(H):
            t.matmul(ps0[:, 64 * h:64 * h + 64], hsl(kbnT, h), Ssl(S_bf, h),
                     start=True, stop=True)
        ct.incd("T", t)
    r_tick = ct.n["T"]

    Y0MODE = getattr(P, "y0mode", 2)
    @blk.vector
    def _(v):
        v.wait_ge(ct.sems["T"], r_tick)
        if Y0MODE == 2:
            v.scalar_tensor_tensor(out=r32(Yv[0][:, :]), in0=ps0[:, :], scalar=1.0,
                                   in1=vtok[:, i, :], op0=OP.mult, op1=OP.add)
        elif Y0MODE == 1:
            v.scalar_tensor_tensor(out=Yv[0][:, :], in0=ps0[:, :], scalar=1.0,
                                   in1=vtok[:, i, :], op0=OP.mult, op1=OP.add)
        ct.incd("V", v)
    y0_tick = ct.n["V"]

    if CS < 2:
        return
    @blk.tensor
    def _(t):
        ct.wait(t, "V")
        for h in range(H):
            pst = (ps1, ps2)[h // 8]
            t.matmul(pst[:, 128 * (h % 8):128 * (h % 8) + 128], hsl(kbnT, h), hsl(knT, h),
                     start=True, stop=True)
        ct.incd("T", t)
    xb_tick = ct.n["T"]

    @blk.vector
    def _(v):
        v.wait_ge(ct.sems["T"], xb_tick)
        for pi, pst in enumerate((ps1, ps2)):
            v.tensor_tensor(r32(split3(Xs[:, 1024 * pi:1024 * (pi + 1)], 8, 128)),
                            split3(pst[:, :], 8, 128),
                            rep_ap(ltm[:, :], 8, 1), op=OP.mult)
        ct.incd("V", v)
    xev_tick = ct.n["V"]

    if CS < 3:
        return
    @blk.tensor
    def _(t):
        t.wait_ge(ct.sems["V"], xev_tick)
        for h in range(H):
            pst = (ps1, ps2)[h // 8]
            t.matmul(pst[:, 128 * (h % 8):128 * (h % 8) + 128], hsl(knT, h), hsl(kbnT, h),
                     start=True, stop=True)
        ct.incd("T", t)
    xtb_tick = ct.n["T"]

    @blk.vector
    def _(v):
        v.wait_ge(ct.sems["T"], xtb_tick)
        for pi, pst in enumerate((ps1, ps2)):
            v.tensor_tensor(r32(split3(XTb[0][:, 1024 * pi:1024 * (pi + 1)], 8, 128)),
                            split3(pst[:, :], 8, 128),
                            rep_ap(utm[:, :], 8, 1), op=OP.mult)
        ct.incd("V", v)
    xtev_tick = ct.n["V"]

    if CS < 4:
        return
    ycur = 0
    xtc = 0
    for j in range(min(NLVL, CS - 3) if CS < 99 else NLVL):
        @blk.tensor
        def _(t, xtc=xtc, ycur=ycur, j=j):
            ct.wait(t, "V")
            if j > 0:
                ct.wait(t, "S")
            for h in range(H):
                t.matmul(ps0[:, 64 * h:64 * h + 64], r32(xsl(XTb[xtc], h)),
                         r32(ysl(Yv[ycur], h)), start=True, stop=True)
            ct.incd("T", t)
        sv_tick = ct.n["T"]

        @blk.vector
        def _(v, ycur=ycur, sv_tick=sv_tick):
            v.wait_ge(ct.sems["T"], sv_tick)
            v.drain()
            v.scalar_tensor_tensor(out=r32(Yv[1 - ycur][:, :]), in0=ps0[:, :], scalar=1.0,
                                   in1=Yv[ycur][:, :], op0=OP.mult, op1=OP.add)
            ct.incd("V", v)
        ycur = 1 - ycur

        if j < 6:
            @blk.tensor
            def _(t, xtc=xtc):
                ct.wait(t, "V")
                ct.wait(t, "S")
                for h in range(H):
                    pst = (ps1, ps2)[h // 8]
                    t.matmul(pst[:, 128 * (h % 8):128 * (h % 8) + 128],
                             r32(xsl(Xs, h)), r32(xsl(XTb[xtc], h)),
                             start=True, stop=True)
                ct.incd("T", t)
            xt2_tick = ct.n["T"]

            if j < 5:
                @blk.scalar
                def _(s, xtc=xtc, xt2_tick=xt2_tick):
                    s.wait_ge(ct.sems["T"], xt2_tick)
                    s.copy(r32(XTb[1 - xtc][:, 0:1024]), ps1[:, :])
                    s.copy(r32(XTb[1 - xtc][:, 1024:2048]), ps2[:, :])
                    ct.incd("S", s)
                xtn_tick = ct.n["S"]

                @blk.tensor
                def _(t, xtc=xtc, xtn_tick=xtn_tick):
                    t.wait_ge(ct.sems["S"], xtn_tick)
                    for h in range(H):
                        pst = (ps1, ps2)[h // 8]
                        t.matmul(pst[:, 128 * (h % 8):128 * (h % 8) + 128],
                                 r32(xsl(XTb[xtc], h)), r32(xsl(Xs, h)),
                                 start=True, stop=True)
                    ct.incd("T", t)
                x2_tick = ct.n["T"]

                @blk.scalar
                def _(s, x2_tick=x2_tick):
                    s.wait_ge(ct.sems["T"], x2_tick)
                    s.copy(r32(Xs[:, 0:1024]), ps1[:, :])
                    s.copy(r32(Xs[:, 1024:2048]), ps2[:, :])
                    ct.incd("S", s)
            else:
                @blk.scalar
                def _(s, xtc=xtc, xt2_tick=xt2_tick):
                    s.wait_ge(ct.sems["T"], xt2_tick)
                    s.copy(r32(XTb[1 - xtc][:, 0:1024]), ps1[:, :])
                    s.copy(r32(XTb[1 - xtc][:, 1024:2048]), ps2[:, :])
                    ct.incd("S", s)
            xtc = 1 - xtc

    if CS < 11:
        return
    yfin = ycur

    @blk.vector
    def _(v):
        v.drain()
        v.tensor_copy(Ubf, Yv[yfin][:, :])
        ct.incd("V", v)
    ubf_tick = ct.n["V"]

    if CS < 12:
        return
    @blk.tensor
    def _(t):
        ct.wait(t, "S")
        ct.wait(t, "V")
        for h in range(H):
            pst = (ps1, ps2)[h // 8]
            t.matmul(pst[:, 128 * (h % 8):128 * (h % 8) + 128], hsl(knT, h), hsl(qnT, h),
                     start=True, stop=True)
        ct.incd("T", t)
    mt_tick = ct.n["T"]

    @blk.vector
    def _(v):
        v.wait_ge(ct.sems["T"], mt_tick)
        for pi, pst in enumerate((ps1, ps2)):
            v.tensor_tensor(split3(MTb[:, 1024 * pi:1024 * (pi + 1)], 8, 128),
                            split3(pst[:, :], 8, 128),
                            rep_ap(utim[:, :], 8, 1), op=OP.mult)
        ct.incd("V", v)
    mtev_tick = ct.n["V"]

    if CS < 13:
        return
    @blk.tensor
    def _(t):
        t.wait_ge(ct.sems["V"], mtev_tick)
        for h in range(H):
            t.matmul(ps0[:, 64 * h:64 * h + 64], MTb[:, 128 * h:128 * h + 128],
                     ysl(Ubf, h), start=True, stop=False)
            t.matmul(ps0[:, 64 * h:64 * h + 64], hsl(qnT, h), Ssl(S_bf, h),
                     start=False, stop=True)
        ct.incd("T", t)
    o_tick = ct.n["T"]

    if CS < 14:
        return
    @blk.tensor
    def _(t):
        for h in range(H):
            t.matmul(ps1[0:64, 64 * h:64 * h + 64], ktok[:, i, 64 * h:64 * h + 64],
                     ysl(Ubf, h), start=True, stop=True)
        ct.incd("T", t)
    su_tick = ct.n["T"]

    @blk.vector
    def _(v):
        v.wait_ge(ct.sems["T"], su_tick)
        v.scalar_tensor_tensor(out=S_f32[:, :, :], in0=split3(ps1[0:64, :], H, HD),
                               scalar=1.0, in1=S_f32[:, :, :], op0=OP.mult, op1=OP.add)
        v.drain()
        v.tensor_copy(S_bf[:, :, :], S_f32[:, :, :])
        ct.incd("V", v)

    if CS < 15:
        return
    @blk.scalar
    def _(s):
        s.wait_ge(ct.sems["T"], o_tick)
        s.activation(onrm, ps0[:, :], AF.Square)
        ct.incd("S", s)
    osq_tick = ct.n["S"]

    @blk.vector
    def _(v):
        v.wait_ge(ct.sems["S"], osq_tick)
        v.tensor_reduce(rsc[:, 0:16], split3(onrm, 16, 64), axis=AX.X, op=OP.add)
        v.drain()
        v.tensor_scalar(rsc[:, 16:32], rsc[:, 0:16], 1.0 / 64.0, 1e-5,
                        op0=OP.mult, op1=OP.add)
        v.drain()
        v.reciprocal(rsc[:, 16:32], rsc[:, 16:32])
        ct.incd("V", v)
    rms1_tick = ct.n["V"]

    @blk.scalar
    def _(s):
        s.wait_ge(ct.sems["V"], rms1_tick)
        s.activation(rsc[:, 32:48], rsc[:, 16:32], AF.Sqrt)
        ct.incd("S", s)
    rms2_tick = ct.n["S"]

    @blk.vector
    def _(v):
        v.wait_ge(ct.sems["S"], rms2_tick)
        v.tensor_tensor(split3(onrm, 16, 64), split3(ps0[:, :], 16, 64),
                        rep_ap(rsc[:, 32:48], 64, 2), op=OP.mult)
        v.drain()
        v.tensor_tensor(onrm, onrm, nwrep[:, :], op=OP.mult)
        v.drain()
        slot = 1 + (ci % 16)
        if ci < 16:
            v.tensor_scalar_mul(owin[:, slot, :], onrm, maskO[:, ci:ci + 1])
        else:
            v.scalar_tensor_tensor(out=owin[:, slot, :], in0=onrm,
                                   scalar=maskO[:, ci:ci + 1], in1=owin[:, slot, :],
                                   op0=OP.mult, op1=OP.add)
        if ci == 15:
            v.tensor_scalar_mul(owin[:, 0, :], onrm, maskH[:, ci:ci + 1])
        ct.incd("V", v)


def build_program(dbg=(), n_seg=NSEG, do_chunks=True, stage=99, cstage=99, y0mode=2, rh=None,
                  p2=True, p3=True, p45=True):
    _wrap_tensor_engine()
    _LAST_T[0] = None
    _LAST_S[0] = None
    nc = bass.Bass("TRN2", target_bir_lowering=False, debug=False)
    dram = {}

    def din(name, shape, dtype=F32):
        dram[name] = nc.dram_tensor(name, list(shape), dtype, kind="ExternalInput").ap()
        return dram[name]

    din("x_full", (T, D)); din("x_win", (WIN, D))
    din("maskO", (128, NCH)); din("maskH", (128, NCH))
    for nm in ("Wq", "Wk", "Wv", "Wg", "Wo"):
        din(nm, (D, D))
    din("Wb", (D, H))
    for nm in ("cqw", "ckw", "cvw"):
        din(nm, (128, 8, 4))
    for nm in ("cqb", "ckb", "cvb"):
        din(nm, (128, 8))
    din("nw_rep", (128, D))
    din("centroidsT", (D, E)); din("ebias_rep", (128, E))
    din("Ws_up", (D, FE)); din("Ws_down", (FE, D))
    din("W_up", (E, D, FE)); din("W_down", (E, FE, D))
    din("w_linT", (D, 496)); din("W1T", (D, HFF)); din("W2T", (HFF, D))
    din("b1", (1, HFF)); din("b2", (1, D))
    din("ident128", (128, 128), BF16); din("identf", (128, 128))
    din("ltmask", (128, 128), BF16); din("utmask", (128, 128), BF16)
    din("utimask", (128, 128), BF16)
    din("ones_col", (1, 128))
    din("Ec", (16, 8, 128), BF16); din("Esq", (128, 8, 16), BF16)

    y = nc.dram_tensor("y", [2048, D], F32, kind="ExternalOutput").ap()
    stag = nc.dram_tensor("stag", [H, 16 * 128 * 159], BF16, kind="Internal").ap()
    hbuf = nc.dram_tensor("hbuf", [2048, HFF], BF16,
                          kind="ExternalOutput" if any(d[0] == "tap_h" for d in dbg)
                          else "Internal").ap()
    dbg_out = {}
    for (nm, shape, dt) in dbg:
        if shape is not None:
            dbg_out[nm] = nc.dram_tensor(nm, list(shape), dt, kind="ExternalOutput").ap()

    ct = Ctr(nc)
    stk = contextlib.ExitStack()
    sb = {}

    def sbuf(name, shape, dtype, st=None):
        tlocal = (st or stk).enter_context(nc.sbuf_tensor(name, list(shape), dtype))
        sb[name] = tlocal
        return tlocal

    with stk:
        ct.make(stk, ["T", "V", "S", "G", "DX", "DW", "DS", "DO"])
        sbuf("ident", (128, 128), BF16)
        sbuf("identf_s", (128, 128), F32)
        sbuf("ltm", (128, 128), BF16)
        sbuf("utm", (128, 128), BF16)
        sbuf("utim", (128, 128), BF16)
        sbuf("onesc", (1, 128), F32)
        sbuf("ec", (16, 8, 128), BF16)
        sbuf("esq", (128, 8, 16), BF16)
        sbuf("nwrep", (128, D), F32)
        sbuf("maskO_s", (128, NCH), F32)
        sbuf("maskH_s", (128, NCH), F32)
        sbuf("magic", (128, 1), F32)
        sbuf("S_f32", (64, H, HD), F32)
        sbuf("S_bf", (64, H, HD), BF16)
        sbuf("cw_conv", (128, 3, 8, 4), F32)
        sbuf("cb_conv", (128, 3, 8), F32)

        ps0 = stk.enter_context(nc.psum_tensor("ps0", [128, 1024], F32))
        ps1 = stk.enter_context(nc.psum_tensor("ps1", [128, 1024], F32))
        ps2 = stk.enter_context(nc.psum_tensor("ps2", [128, 1024], F32))
        psT = stk.enter_context(nc.psum_tensor("psT", [128, 2048], BF16))

        blk = stk.enter_context(nc.Block())
        P = Prog(nc, blk, ct, sb, (ps0, ps1, ps2, psT), dram, dbg_out)
        P.do_chunks = do_chunks
        P.stage = stage
        P.cstage = cstage
        P.y0mode = y0mode
        if rh is not None:
            P.rh = rh

        ow = contextlib.ExitStack()
        sb["owin"] = ow.enter_context(nc.sbuf_tensor("owin", [128, NW, D], BF16, side="right"))

        @blk.sync
        def _(sy):
            for dst, src in (("ident", "ident128"), ("identf_s", "identf"),
                             ("ltm", "ltmask"), ("utm", "utmask"), ("utim", "utimask"),
                             ("onesc", "ones_col"), ("ec", "Ec"), ("esq", "Esq"),
                             ("nwrep", "nw_rep"), ("maskO_s", "maskO"), ("maskH_s", "maskH")):
                ct.dmaq("DX", sy, out=sb[dst][:], in_=dram[src][:])
            for i, nm in enumerate(("cqw", "ckw", "cvw")):
                ct.dmaq("DX", sy, out=sb["cw_conv"][:, i], in_=dram[nm][:])
            for i, nm in enumerate(("cqb", "ckb", "cvb")):
                ct.dmaq("DX", sy, out=sb["cb_conv"][:, i], in_=dram[nm][:])
        ticks = {"const": ct.n["DX"]}

        with contextlib.ExitStack() as p1:
            sbuf("wqkvg", (128, 3, 8, D), BF16, p1)
            sbuf("wb_s", (128, 8, H), BF16, p1)
            sbuf("xseg", (128, CPS, D), BF16, p1)
            sbuf("lnst", (128, 32), F32, p1)
            sbuf("xn_st", (128, CPS, D), BF16, p1)
            sbuf("xnT", (128, 8, SEG), BF16, p1)
            sbuf("qpT", (128, 8, 3 + SEG), BF16, p1)
            sbuf("kpT", (128, 8, 3 + SEG), BF16, p1)
            sbuf("vpT", (128, 8, 3 + SEG), BF16, p1)
            sbuf("cacc8", (128, 8, SEG), F32, p1)
            sbuf("sqb", (128, SEG), BF16, p1)
            sbuf("sqb2", (128, SEG), BF16, p1)
            sbuf("qnT", (128, 8, SEG), BF16, p1)
            sbuf("knT", (128, 8, SEG), BF16, p1)
            sbuf("kbnT", (128, 8, SEG), BF16, p1)
            sbuf("qnT0", (64, 8, SEG), BF16, p1)
            sbuf("knT0", (64, 8, SEG), BF16, p1)
            sbuf("kbnT0", (64, 8, SEG), BF16, p1)
            sbuf("vbT", (128, 8, SEG), BF16, p1)
            sbuf("vtok", (128, CPS, D), BF16, p1)
            sbuf("ktok", (128, CPS, D), BF16, p1)
            sbuf("beta", (128, CPS, H), F32, p1)
            sbuf("betaT", (16, SEG), F32, p1)
            sbuf("btb", (16, SEG), BF16, p1)
            sbuf("ssq", (16, 2, SEG), F32, p1)
            sbuf("rsn", (16, 2, SEG), BF16, p1)
            sbuf("Xs", (128, H * 128), F32, p1)
            sbuf("XT0", (128, H * 128), F32, p1)
            sbuf("XT1", (128, H * 128), F32, p1)
            sbuf("MTb", (128, H * 128), BF16, p1)
            sbuf("Ya", (128, H * HD), F32, p1)
            sbuf("Yb", (128, H * HD), F32, p1)
            sbuf("rsc", (128, 48), F32, p1)

            @blk.gpsimd
            def _(gp):
                for mi, nm in enumerate(("Wq", "Wk", "Wv")):
                    src = dram[nm].rearrange("(kc kp) d -> kp kc d", kp=128)
                    ct.dmaq("DW", gp, out=sb["wqkvg"][:, mi], in_=src)
                ct.dmaq("DW", gp, 
                    out=sb["wb_s"][:], in_=dram["Wb"].rearrange("(kc kp) h -> kp kc h", kp=128))
            ticks["l0w"] = ct.n["DW"]

            @blk.vector
            def _(v):
                v.memset(sb["magic"][:], MAGIC)
                v.memset(sb["S_f32"][:], 0.0)
                v.memset(sb["S_bf"][:], 0.0)
                v.memset(sb["owin"][:], 0.0)
                v.memset(sb["qpT"][:, :, 0:3], 0.0)
                v.memset(sb["kpT"][:, :, 0:3], 0.0)
                v.memset(sb["vpT"][:, :, 0:3], 0.0)
                ct.incd("V", v)
            ticks["zinit"] = ct.n["V"]

            for seg in range(n_seg):
                _emit_segment(P, seg, ticks)

            if "tap_owin" in dbg_out:
                @blk.sync
                def _(sy):
                    ct.wait(sy, "V")
                    ct.dmaq("DX", sy, 
                        out=dbg_out["tap_owin"].rearrange("(i p) d -> p i d", p=128),
                        in_=sb["owin"][:])

        sbuf("x1acc", (128, NW, D), F32)
        with contextlib.ExitStack() as p23:
            sbuf("xnwT", (128, 8, NW * 128), BF16, p23)
            if p2:
                _emit_phase2(P)
            if "tap_p2" in dbg_out:
                @blk.sync
                def _(sy):
                    ct.wait(sy, "V")
                    ct.dmaq("DX", sy, 
                        out=dbg_out["tap_p2"].rearrange("(i p) d -> p i d", p=128),
                        in_=sb["x1acc"][:])
                p2tap = ct.n["DX"]

                @blk.vector
                def _(v):
                    v.wait_ge(ct.sems["DX"], p2tap)
                    ct.incd("V", v)
            ow.close()
            if p3:
                _emit_phase3(P)
            if "tap_p3" in dbg_out:
                @blk.sync
                def _(sy):
                    ct.wait(sy, "V")
                    ct.dmaq("DX", sy, 
                        out=dbg_out["tap_p3"].rearrange("(i p) d -> p i d", p=128),
                        in_=sb["x1acc"][:])
                p3tap = ct.n["DX"]

                @blk.vector
                def _(v):
                    v.wait_ge(ct.sems["DX"], p3tap)
                    ct.incd("V", v)
        if p45:
            _emit_phase45(P, y, stag, hbuf)
        else:
            @blk.sync
            def _(sy):
                ct.wait(sy, "V")
                ct.dmaq("DO", sy, 
                    out=y.rearrange("(i p) d -> p i d", p=128),
                    in_=sb["x1acc"][:, 1:17, :])
                ct.wait(sy, "DO")

    return nc


def _emit_ln_tile(P, src_ap, dst_ap, ln, pre=None):
    blk, ct = P.blk, P.ct

    @blk.vector
    def _(v):
        if pre is not None:
            v.wait_ge(ct.sems[pre[0]], pre[1])
        v.drain()
        v.bn_stats(ln[:, 0:6], src_ap[:, 0:512])
        v.bn_stats(ln[:, 6:12], src_ap[:, 512:1024])
        v.drain()
        v.bn_aggr(ln[:, 12:14], ln[:, 0:12])
        v.drain()
        v.tensor_scalar_add(ln[:, 14:15], ln[:, 13:14], 1e-5)
        v.drain()
        v.reciprocal(ln[:, 15:16], ln[:, 14:15])
        ct.incd("V", v)
    t1 = ct.n["V"]

    @blk.scalar
    def _(s):
        s.wait_ge(ct.sems["V"], t1)
        s.activation(ln[:, 16:17], ln[:, 15:16], AF.Sqrt)
        ct.incd("S", s)
    t2 = ct.n["S"]

    @blk.vector
    def _(v):
        v.wait_ge(ct.sems["S"], t2)
        v.tensor_scalar(dst_ap, src_ap, ln[:, 12:13], ln[:, 16:17],
                        op0=OP.subtract, op1=OP.mult)
        ct.incd("V", v)
    return ct.n["V"]


def _phase_barrier(P, eng):
    ct = P.ct
    ct.wait(eng, "T")
    ct.wait(eng, "V")
    ct.wait(eng, "S")


def _emit_phase2(P):
    nc, blk, ct, sb, dram = P.nc, P.blk, P.ct, P.sb, P.dram
    ps0, ps1, ps2, psT = P.ps0, P.ps1, P.ps2, P.psT
    x1acc, xnwT, owin = sb["x1acc"], sb["xnwT"], sb["owin"]

    with contextlib.ExitStack() as p2:
        wg = p2.enter_context(nc.sbuf_tensor("wg_s", [128, 8, D], BF16))
        wo = p2.enter_context(nc.sbuf_tensor("wo_s", [128, 8, D], BF16))
        xnb = p2.enter_context(nc.sbuf_tensor("xnb", [128, 2, D], BF16))
        sgt = p2.enter_context(nc.sbuf_tensor("sgt", [128, 2, D], BF16))
        otT = p2.enter_context(nc.sbuf_tensor("otT", [128, 2, 8, 128], BF16))
        ln2 = p2.enter_context(nc.sbuf_tensor("ln2", [128, 32], F32))

        @blk.sync
        def _(sy):
            _phase_barrier(P, sy)
            ct.dmaq("DX", sy, 
                out=x1acc[:], in_=dram["x_win"].rearrange("(i p) d -> p i d", p=128))
        xw_tick = ct.n["DX"]

        @blk.gpsimd
        def _(gp):
            _phase_barrier(P, gp)
            for dst, nm in ((wg, "Wg"), (wo, "Wo")):
                ct.dmaq("DW", gp, 
                    out=dst[:], in_=dram[nm].rearrange("(kc kp) d -> kp kc d", kp=128))
        w2_tick = ct.n["DW"]

        xnb_tick = [0, 0]
        sg_tick = [0, 0]
        ot_tick = [0, 0]
        for i in range(NW):
            sl = i % 2
            pre = ("DX", xw_tick) if i == 0 else ("T", xnb_tick[sl])
            ln_tick = _emit_ln_tile(P, x1acc[:, i, :], xnb[:, sl, :], ln2, pre=pre)

            P.tr_seq([(xnb[:, sl, m * 128:(m + 1) * 128],
                       xnwT[:, m, i * 128:(i + 1) * 128], None, None) for m in range(8)],
                     pre=("V", ln_tick))
            xnb_tick[sl] = ct.n["T"]
            ev_tick = ct.n["S"]

            @blk.tensor
            def _(t, i=i, ev_tick=ev_tick, w=max(sg_tick)):
                t.wait_ge(ct.sems["S"], ev_tick)
                if i == 0:
                    t.wait_ge(ct.sems["DW"], w2_tick)
                    ct.wait(t, "V")
                if w:
                    t.wait_ge(ct.sems["S"], w)
                for half in range(2):
                    for kc in range(8):
                        t.matmul(ps0[:, 512 * half:512 * half + 512],
                                 xnwT[:, kc, i * 128:(i + 1) * 128],
                                 wg[:, kc, 512 * half:512 * half + 512],
                                 start=(kc == 0), stop=(kc == 7))
                ct.incd("T", t)
            g_tick = ct.n["T"]

            @blk.scalar
            def _(s, sl=sl, g_tick=g_tick):
                s.wait_ge(ct.sems["T"], g_tick)
                s.activation(sgt[:, sl, 0:512], ps0[:, 0:512], AF.Silu)
                s.activation(sgt[:, sl, 512:1024], ps0[:, 512:1024], AF.Silu)
                ct.incd("S", s)
            sg_tick[sl] = ct.n["S"]

            @blk.vector
            def _(v, i=i, sl=sl, w=sg_tick[sl]):
                v.wait_ge(ct.sems["S"], w)
                v.tensor_tensor(owin[:, i, :], owin[:, i, :], sgt[:, sl, :], op=OP.mult)
                ct.incd("V", v)
            om_tick = ct.n["V"]

            P.tr_seq([(owin[:, i, m * 128:(m + 1) * 128], otT[:, sl, m, :], None, None)
                      for m in range(8)], pre=("V", om_tick))
            otev_tick = ct.n["S"]

            @blk.tensor
            def _(t, sl=sl, otev_tick=otev_tick, w=ot_tick[sl]):
                t.wait_ge(ct.sems["S"], otev_tick)
                if w:
                    t.wait_ge(ct.sems["V"], w)
                for half in range(2):
                    for m in range(8):
                        t.matmul(ps1[:, 512 * half:512 * half + 512], otT[:, sl, m, :],
                                 wo[:, m, 512 * half:512 * half + 512],
                                 start=(m == 0), stop=(m == 7))
                ct.incd("T", t)
            woT_tick = ct.n["T"]

            @blk.vector
            def _(v, i=i, woT_tick=woT_tick):
                v.wait_ge(ct.sems["T"], woT_tick)
                for half in range(2):
                    v.scalar_tensor_tensor(
                        out=x1acc[:, i, 512 * half:512 * half + 512],
                        in0=ps1[:, 512 * half:512 * half + 512], scalar=1.0,
                        in1=x1acc[:, i, 512 * half:512 * half + 512],
                        op0=OP.mult, op1=OP.add)
                ct.incd("V", v)
            ot_tick[sl] = ct.n["V"]


def _emit_phase3(P):
    nc, blk, ct, sb, dram = P.nc, P.blk, P.ct, P.sb, P.dram
    ps0, ps1, ps2, psT = P.ps0, P.ps1, P.ps2, P.psT
    x1acc, xnwT = sb["x1acc"], sb["xnwT"]
    NT = [512, 512, 512, 512, 128]

    with contextlib.ExitStack() as p3:
        cenT = p3.enter_context(nc.sbuf_tensor("cenT", [128, 8, E], BF16))
        ebias = p3.enter_context(nc.sbuf_tensor("ebias", [128, E], F32))
        wsup = p3.enter_context(nc.sbuf_tensor("wsup", [128, 8, FE], BF16))
        wsdn = p3.enter_context(nc.sbuf_tensor("wsdn", [128, 4, D], BF16))
        aff = p3.enter_context(nc.sbuf_tensor("aff", [128, NW, E], F32))
        cw = p3.enter_context(nc.sbuf_tensor("cw", [128, NW, E], F32))
        hT = p3.enter_context(nc.sbuf_tensor("hT", [128, 4, NW * 128], BF16))
        wu = [p3.enter_context(nc.sbuf_tensor(f"wu{j}", [128, 8, FE], BF16)) for j in range(2)]
        wd = [p3.enter_context(nc.sbuf_tensor(f"wd{j}", [128, 4, D], BF16)) for j in range(2)]
        tkb = p3.enter_context(nc.sbuf_tensor("tkb", [128, E], F32))
        oha = p3.enter_context(nc.sbuf_tensor("oha", [128, 8, E], F32))
        ohs = p3.enter_context(nc.sbuf_tensor("ohs", [128, E], F32))
        tks = p3.enter_context(nc.sbuf_tensor("tks", [128, 32], F32))

        @blk.sync
        def _(sy):
            _phase_barrier(P, sy)
            ct.dmaq("DX", sy, out=ebias[:], in_=dram["ebias_rep"][:])
        eb_tick = ct.n["DX"]

        @blk.gpsimd
        def _(gp):
            _phase_barrier(P, gp)
            ct.dmaq("DW", gp, 
                out=cenT[:], in_=dram["centroidsT"].rearrange("(kc kp) e -> kp kc e", kp=128))
            ct.dmaq("DW", gp, 
                out=wsup[:], in_=dram["Ws_up"].rearrange("(kc kp) f -> kp kc f", kp=128))
            ct.dmaq("DW", gp, 
                out=wsdn[:], in_=dram["Ws_down"].rearrange("(fc fp) d -> fp fc d", fp=128))
        w3_tick = ct.n["DW"]

        aslot_tick = [0, 0]
        for i in range(NW):
            sl = i % 2
            pa = (ps2[0:128, 0:64], ps2[0:128, 512:576])[sl]

            @blk.tensor
            def _(t, i=i, pa=pa, w=aslot_tick[sl]):
                if i == 0:
                    t.wait_ge(ct.sems["DW"], w3_tick)
                    ct.wait(t, "V")
                if w:
                    t.wait_ge(ct.sems["V"], w)
                for kc in range(8):
                    t.matmul(pa, xnwT[:, kc, i * 128:(i + 1) * 128], cenT[:, kc],
                             start=(kc == 0), stop=(kc == 7))
                ct.incd("T", t)
            t_tick = ct.n["T"]

            @blk.vector
            def _(v, i=i, pa=pa, t_tick=t_tick):
                v.wait_ge(ct.sems["T"], t_tick)
                v.tensor_copy(aff[:, i, :], pa)
                ct.incd("V", v)
            aslot_tick[sl] = ct.n["V"]

        for i in range(NW):
            @blk.vector
            def _(v, i=i):
                if i == 0:
                    v.wait_ge(ct.sems["DX"], eb_tick)
                v.drain()
                v.tensor_tensor(tkb[:, :], aff[:, i, :], ebias[:, :], op=OP.add)
                for r in range(8):
                    v.drain()
                    v.reduce_max(tks[:, 0:1], tkb[:, :], axis=AX.X)
                    v.drain()
                    v.tensor_scalar(oha[:, r, :], tkb[:, :], tks[:, 0:1], None, op0=OP.is_equal)
                    v.drain()
                    v.tensor_tensor(ohs[:, :], aff[:, i, :], oha[:, r, :], op=OP.mult)
                    v.drain()
                    v.reduce_sum(tks[:, 8 + r:9 + r], ohs[:, :], axis=AX.X)
                    v.drain()
                    v.scalar_tensor_tensor(out=tkb[:, :], in0=oha[:, r, :], scalar=-1e30,
                                           in1=tkb[:, :], op0=OP.mult, op1=OP.add)
                ct.incd("V", v)
            sel_tick = ct.n["V"]

            @blk.scalar
            def _(s, sel_tick=sel_tick):
                s.wait_ge(ct.sems["V"], sel_tick)
                s.activation(tks[:, 16:24], tks[:, 8:16], AF.Sigmoid)
                ct.incd("S", s)
            sig_tick = ct.n["S"]

            @blk.vector
            def _(v, i=i, sig_tick=sig_tick):
                v.wait_ge(ct.sems["S"], sig_tick)
                v.tensor_reduce(tks[:, 24:25], tks[:, 16:24], axis=AX.X, op=OP.add)
                v.drain()
                v.tensor_scalar_add(tks[:, 24:25], tks[:, 24:25], 1e-8)
                v.drain()
                v.reciprocal(tks[:, 25:26], tks[:, 24:25])
                v.drain()
                v.tensor_scalar(tks[:, 16:24], tks[:, 16:24], tks[:, 25:26], None, op0=OP.mult)
                v.drain()
                v.tensor_scalar_mul(cw[:, i, :], oha[:, 0, :], tks[:, 16:17])
                for r in range(1, 8):
                    v.drain()
                    v.scalar_tensor_tensor(out=cw[:, i, :], in0=oha[:, r, :],
                                           scalar=tks[:, 16 + r:17 + r], in1=cw[:, i, :],
                                           op0=OP.mult, op1=OP.add)
                ct.incd("V", v)
        cw_tick = ct.n["V"]

        hs_tick = 0
        yv_tick = 0
        yT_tick = 0
        wdma_tick = [0, 0]
        pyb = (ps2[:, 0:512], ps2[:, 512:1024])
        for eidx in range(-1, E):
            par = eidx % 2
            if eidx >= 0:
                @blk.gpsimd
                def _(gp, eidx=eidx, par=par, w=wdma_tick[par]):
                    if w:
                        gp.wait_ge(ct.sems["T"], w)
                    ct.dmaq("DW", gp, 
                        out=wu[par][:],
                        in_=dram["W_up"][eidx].rearrange("(kc kp) f -> kp kc f", kp=128))
                    ct.dmaq("DW", gp, 
                        out=wd[par][:],
                        in_=dram["W_down"][eidx].rearrange("(fc fp) d -> fp fc d", fp=128))
                wdma = ct.n["DW"]
                wub, wdb = wu[par], wd[par]
            else:
                wdma = w3_tick
                wub, wdb = wsup, wsdn

            off = 0
            for nt, ntok in enumerate(NT):
                @blk.tensor
                def _(t, eidx=eidx, nt=nt, off=off, ntok=ntok, wdma=wdma,
                      hs_tick=hs_tick):
                    if nt == 0:
                        t.wait_ge(ct.sems["DW"], wdma)
                        if eidx == -1:
                            ct.wait(t, "V")
                    if hs_tick:
                        t.wait_ge(ct.sems["S"], hs_tick)
                    for m in range(4):
                        pb = (ps0, ps1)[m // 2][:, 512 * (m % 2):512 * (m % 2) + 512]
                        for kc in range(8):
                            t.matmul(pb[:, 0:ntok], wub[:, kc, 128 * m:128 * m + 128],
                                     xnwT[:, kc, off:off + ntok],
                                     start=(kc == 0), stop=(kc == 7))
                    ct.incd("T", t)
                h_tick = ct.n["T"]

                @blk.scalar
                def _(s, nt=nt, off=off, ntok=ntok, h_tick=h_tick):
                    s.wait_ge(ct.sems["T"], h_tick)
                    for m in range(4):
                        pb = (ps0, ps1)[m // 2][:, 512 * (m % 2):512 * (m % 2) + 512]
                        s.activation(hT[:, m, off:off + ntok], pb[:, 0:ntok], AF.Silu)
                    ct.incd("S", s)
                hs_tick = ct.n["S"]
                off += ntok

            for tt in range(NW):
                @blk.tensor
                def _(t, tt=tt, hs_tick=hs_tick, yv_tick=yv_tick):
                    if tt == 0:
                        t.wait_ge(ct.sems["S"], hs_tick)
                    if yv_tick:
                        t.wait_ge(ct.sems["V"], yv_tick)
                    for half in range(2):
                        for m in range(4):
                            t.matmul(pyb[half], hT[:, m, 128 * tt:128 * tt + 128],
                                     wdb[:, m, 512 * half:512 * half + 512],
                                     start=(m == 0), stop=(m == 3))
                    ct.incd("T", t)
                yT_tick = ct.n["T"]

                @blk.vector
                def _(v, eidx=eidx, tt=tt, yT_tick=yT_tick):
                    v.wait_ge(ct.sems["T"], yT_tick)
                    if eidx == -1 and tt == 0:
                        v.wait_ge(ct.sems["V"], cw_tick)
                    for half in range(2):
                        dst = x1acc[:, tt, 512 * half:512 * half + 512]
                        if eidx < 0:
                            v.scalar_tensor_tensor(out=dst, in0=pyb[half], scalar=1.0,
                                                   in1=dst, op0=OP.mult, op1=OP.add)
                        else:
                            v.scalar_tensor_tensor(out=dst, in0=pyb[half],
                                                   scalar=cw[:, tt, eidx:eidx + 1],
                                                   in1=dst, op0=OP.mult, op1=OP.add)
                    ct.incd("V", v)
                yv_tick = ct.n["V"]
            if eidx >= 0:
                wdma_tick[par] = yT_tick


def _emit_phase45(P, y, stag, hbuf):
    nc, blk, ct, sb, dram = P.nc, P.blk, P.ct, P.sb, P.dram
    ps0, ps1, ps2, psT = P.ps0, P.ps1, P.ps2, P.psT
    x1acc = sb["x1acc"]
    onesc = sb["onesc"]

    with contextlib.ExitStack() as p4:
        ln4 = p4.enter_context(nc.sbuf_tensor("ln4", [128, 32], F32))
        wsmstk = contextlib.ExitStack()
        wsm = wsmstk.enter_context(nc.sbuf_tensor("wsm", [128, 8, 16, 31], BF16))
        xq1stk = contextlib.ExitStack()
        xq1T = xq1stk.enter_context(nc.sbuf_tensor("xq1T", [128, 8, 2048], BF16, side="right"))
        xn1stk = contextlib.ExitStack()
        xn1 = xn1stk.enter_context(nc.sbuf_tensor("xn1", [128, NW, D], BF16, side="right"))

        with contextlib.ExitStack() as p4a:
            xn1T = p4a.enter_context(nc.sbuf_tensor("xn1T", [128, 8, 2048], BF16))
            wlt = p4a.enter_context(nc.sbuf_tensor("wlt", [128, 8, 496], BF16))
            wr = p4a.enter_context(nc.sbuf_tensor("wr", [128, 1, 496], F32))
            zer = p4a.enter_context(nc.sbuf_tensor("zer", [128, 1272], BF16))

            @blk.vector
            def _(v):
                v.memset(zer[:], 0.0)
                ct.incd("V", v)
            z_tick = ct.n["V"]

            @blk.scalar
            def _(s):
                s.wait_ge(ct.sems["V"], z_tick)
                half = 16 * 128 * 159 // 2
                for h in range(H):
                    for q in range(2):
                        ct.dmaq("DS", s, 
                            out=stag[h, q * half:(q + 1) * half].rearrange(
                                "(p f) -> p f", p=128),
                            in_=zer[:, :])
            zf_tick = ct.n["DS"]

            @blk.gpsimd
            def _(gp):
                _phase_barrier(P, gp)
                ct.dmaq("DW", gp, 
                    out=wlt[:], in_=dram["w_linT"].rearrange("(kc kp) f -> kp kc f", kp=128))
            wl_tick = ct.n["DW"]

            for i in range(NW):
                ln_tick = _emit_ln_tile(P, x1acc[:, i, :], xn1[:, i, :], ln4, pre=None)
                if i >= 1:
                    P.tr_seq([(xn1[:, i, m * 128:(m + 1) * 128],
                               xn1T[:, m, (i - 1) * 128:i * 128], None, None)
                              for m in range(8)], pre=("V", ln_tick))
            xn1_tick = ct.n["V"]
            xn1T_tick = ct.n["S"]

            wsl_tick = [0, 0]
            for bi in range(16):
                sl = 0
                pw = (ps2[0:128, 0:496], ps2[0:128, 512:1008])[bi % 2]

                @blk.tensor
                def _(t, bi=bi, pw=pw, w=wsl_tick[bi % 2]):
                    if bi == 0:
                        t.wait_ge(ct.sems["S"], xn1T_tick)
                        t.wait_ge(ct.sems["DW"], wl_tick)
                        ct.wait(t, "V")
                    if w:
                        t.wait_ge(ct.sems["V"], w)
                    t.wait_ge(ct.sems["V"], wsl_tick[0]) if False else None
                    tok0 = 128 * bi
                    for kc in range(8):
                        t.matmul(pw, xn1T[:, kc, tok0:tok0 + 128], wlt[:, kc],
                                 start=(kc == 0), stop=(kc == 7))
                    ct.incd("T", t)
                wp_tick = ct.n["T"]

                @blk.vector
                def _(v, pw=pw, wp_tick=wp_tick):
                    v.wait_ge(ct.sems["T"], wp_tick)
                    ct.wait(v, "S")
                    v.drain()
                    v.tensor_reduce(ln4[:, 0:16], split3(pw, 16, 31), axis=AX.X, op=OP.max)
                    v.drain()
                    v.tensor_tensor(split3(wr[:, 0, :], 16, 31), split3(pw, 16, 31),
                                    rep_ap(ln4[:, 0:16], 31, 2), op=OP.subtract)
                    ct.incd("V", v)
                sub_tick = ct.n["V"]

                @blk.scalar
                def _(s, sub_tick=sub_tick):
                    s.wait_ge(ct.sems["V"], sub_tick)
                    s.activation(wr[:, 0, :], wr[:, 0, :], AF.Exp)
                    ct.incd("S", s)
                ex_tick = ct.n["S"]

                @blk.vector
                def _(v, bi=bi, ex_tick=ex_tick):
                    v.wait_ge(ct.sems["S"], ex_tick)
                    if bi == 8:
                        v.wait_ge(ct.sems["DX"], stw1_tick)
                    w3 = split3(wr[:, 0, :], 16, 31)
                    v.drain()
                    v.tensor_reduce(ln4[:, 0:16], w3, axis=AX.X, op=OP.add)
                    v.drain()
                    v.reciprocal(ln4[:, 0:16], ln4[:, 0:16])
                    v.drain()
                    v.tensor_tensor(wsm[:, bi % 8], w3, rep_ap(ln4[:, 0:16], 31, 2),
                                    op=OP.mult)
                    ct.incd("V", v)
                wsl_tick[bi % 2] = ct.n["V"]
                if bi == 7:
                    sm_half = ct.n["V"]

                    @blk.sync
                    def _(sy, sm_half=sm_half):
                        sy.wait_ge(ct.sems["V"], sm_half)
                        sy.wait_ge(ct.sems["DS"], zf_tick)
                        dst = stag.rearrange("h (b t x) -> t b h x", b=16, t=128, x=159)
                        for hh in range(H):
                            ct.dmaq("DX", sy, out=dst[:, 0:8, hh, 0:31],
                                                      in_=wsm[:, :, hh, :])
                    stw1_tick = ct.n["DX"]
            sm_tick = ct.n["V"]

            @blk.sync
            def _(sy):
                sy.wait_ge(ct.sems["V"], sm_tick)
                dst = stag.rearrange("h (b t x) -> t b h x", b=16, t=128, x=159)
                for hh in range(H):
                    ct.dmaq("DX", sy, out=dst[:, 8:16, hh, 0:31],
                                              in_=wsm[:, :, hh, :])
            stw_tick = ct.n["DX"]

        with contextlib.ExitStack() as p4b:
            shift = p4b.enter_context(nc.sbuf_tensor("shift", [128, NW, D], BF16))
            weAt = p4b.enter_context(nc.sbuf_tensor("weAt", [128, 16, 158], BF16))
            weA = p4b.enter_context(nc.sbuf_tensor("weA", [128, 16, 128], BF16))
            weB = p4b.enter_context(nc.sbuf_tensor("weB", [32, 16, 128], BF16))

            @blk.scalar
            def _(s):
                _phase_barrier(P, s)
                s.wait_ge(ct.sems["V"], xn1_tick)
                for i in range(NW):
                    ct.dmaq("DS", s, out=shift[0:30, i, :], in_=xn1[98:128, i, :])
                    if i < NW - 1:
                        ct.dmaq("DS", s, out=shift[30:128, i, :],
                                                 in_=xn1[0:98, i + 1, :])
            sh_tick = ct.n["DS"]

            pslots = [ps0[:, 0:64], ps0[:, 512:576], ps1[:, 0:64], ps1[:, 512:576]]
            slot_v = [0, 0, 0, 0]
            web_tick = 0
            for h in range(H):
                @blk.sync
                def _(sy, h=h, w=web_tick):
                    if h == 0:
                        sy.wait_ge(ct.sems["DX"], stw_tick)
                        sy.wait_ge(ct.sems["DS"], sh_tick)
                        _phase_barrier(P, sy)
                    if w:
                        ct.wait(sy, "T", w)
                    srcT = bass.AP(stag.tensor, h * (16 * 128 * 159),
                                   [[158, 128], [20352, 16], [1, 158]])
                    ct.dmaq("DX", sy, out=weAt[:, :, :], in_=srcT)
                rb_tick = ct.n["DX"]

                @blk.tensor
                def _(t, rb_tick=rb_tick):
                    t.wait_ge(ct.sems["DX"], rb_tick)
                    ct.wait(t, "S")
                    for b_ in range(16):
                        t.transpose(psT[0:128, 128 * b_:128 * b_ + 128],
                                    weAt[:, b_, 0:128], sb["ident"][:, :])
                    ct.incd("T", t)
                trA_tick = ct.n["T"]

                @blk.scalar
                def _(s, trA_tick=trA_tick):
                    s.wait_ge(ct.sems["T"], trA_tick)
                    s.copy(weA[:, :, :], split3(psT[0:128, 0:2048], 16, 128))
                    ct.incd("S", s)
                evA_tick = ct.n["S"]

                @blk.tensor
                def _(t, evA_tick=evA_tick):
                    t.wait_ge(ct.sems["S"], evA_tick)
                    for b_ in range(16):
                        t.transpose(psT[0:30, 128 * b_:128 * b_ + 128],
                                    weAt[:, b_, 128:158], sb["ident"][:, :])
                    ct.incd("T", t)
                trB_tick = ct.n["T"]

                @blk.scalar
                def _(s, trB_tick=trB_tick):
                    s.wait_ge(ct.sems["T"], trB_tick)
                    s.copy(weB[0:30, :, :],
                           bass.AP(psT[0:30, 0:2048].tensor, psT[0:30, 0:2048].offset,
                                   [list(psT[0:30, 0:2048].ap[0]), [128, 16], [1, 128]]))
                    ct.incd("S", s)
                rb_tick = ct.n["S"]

                for bi in range(16):
                    k = h * 16 + bi
                    sl = k % 4
                    pband = pslots[sl]

                    @blk.tensor
                    def _(t, h=h, bi=bi, sl=sl, pband=pband, rb_tick=rb_tick, w=slot_v[sl]):
                        if bi == 0:
                            t.wait_ge(ct.sems["S"], rb_tick)
                        if w:
                            t.wait_ge(ct.sems["V"], w)
                        t.matmul(pband, weA[:, bi, :], shift[:, bi, 64 * h:64 * h + 64],
                                 start=True, stop=False)
                        t.matmul(pband, weB[0:30, bi, :], shift[0:30, bi + 1, 64 * h:64 * h + 64],
                                 start=False, stop=True)
                        ct.incd("T", t)
                    bT_tick = ct.n["T"]

                    @blk.vector
                    def _(v, h=h, bi=bi, pband=pband, bT_tick=bT_tick):
                        v.wait_ge(ct.sems["T"], bT_tick)
                        dst = x1acc[:, 1 + bi, 64 * h:64 * h + 64]
                        v.scalar_tensor_tensor(out=dst, in0=pband, scalar=1.0, in1=dst,
                                               op0=OP.mult, op1=OP.add)
                        ct.incd("V", v)
                    slot_v[sl] = ct.n["V"]
                web_tick = ct.n["T"]

            if "tap_p4" in P.dbg_out:
                @blk.sync
                def _(sy):
                    ct.wait(sy, "V")
                    ct.dmaq("DX", sy, 
                        out=P.dbg_out["tap_p4"].rearrange("(i p) d -> p i d", p=128),
                        in_=x1acc[:])
                p4tap = ct.n["DX"]

                @blk.vector
                def _(v):
                    v.wait_ge(ct.sems["DX"], p4tap)
                    ct.incd("V", v)

        with contextlib.ExitStack() as p4c:
            qf = p4c.enter_context(nc.sbuf_tensor("qf", [128, D], F32))
            qx = p4c.enter_context(nc.sbuf_tensor("qx", [128, 2, D], BF16))
            for ii in range(16):
                i = 1 + ii
                sl = ii % 2

                c0 = 3 * sl

                @blk.vector
                def _(v, i=i, sl=sl, c0=c0):
                    if i == 1:
                        ct.wait(v, "T")
                    else:
                        ct.wait(v, "S")
                    v.drain()
                    v.reduce_max(ln4[:, c0:c0 + 1], xn1[:, i, :], axis=AX.X,
                                 apply_absolute_value=True)
                    v.drain()
                    v.tensor_scalar(ln4[:, c0 + 1:c0 + 2], ln4[:, c0:c0 + 1], 1e-5, 1.0 / QB,
                                    op0=OP.max, op1=OP.mult)
                    v.drain()
                    v.reciprocal(ln4[:, c0 + 2:c0 + 3], ln4[:, c0 + 1:c0 + 2])
                    ct.incd("V", v)
                q1_tick = ct.n["V"]

                @blk.scalar
                def _(s, i=i, q1_tick=q1_tick, c0=c0):
                    s.wait_ge(ct.sems["V"], q1_tick)
                    ct.wait(s, "T")
                    s.activation(qf[:, :], xn1[:, i, :], AF.Identity,
                                 scale=ln4[:, c0 + 2:c0 + 3], bias=sb["magic"][:, :])
                    ct.incd("S", s)
                q2_tick = ct.n["S"]

                @blk.vector
                def _(v, sl=sl, q2_tick=q2_tick, c0=c0):
                    v.wait_ge(ct.sems["S"], q2_tick)
                    v.tensor_scalar(qf[:, :], qf[:, :], MAGIC, QB, op0=OP.subtract, op1=OP.min)
                    v.drain()
                    v.tensor_scalar(qx[:, sl, :], qf[:, :], -QB - 1.0, ln4[:, c0 + 1:c0 + 2],
                                    op0=OP.max, op1=OP.mult)
                    ct.incd("V", v)
                q3_tick = ct.n["V"]

                P.tr_seq([(qx[:, sl, m * 128:(m + 1) * 128],
                           xq1T[:, m, ii * 128:(ii + 1) * 128], None, None) for m in range(8)],
                         pre=("V", q3_tick))
        xn1stk.close()
        wsmstk.close()

        with contextlib.ExitStack() as p5:
            b1s = p5.enter_context(nc.sbuf_tensor("b1s", [1, HFF], BF16))
            b2s = p5.enter_context(nc.sbuf_tensor("b2s", [1, D], BF16))
            onesb = p5.enter_context(nc.sbuf_tensor("onesb", [1, 128], BF16))
            o128 = p5.enter_context(nc.sbuf_tensor("o128", [128, 1], F32))
            wsc = p5.enter_context(nc.sbuf_tensor("wsc", [1, 1], F32))

            @blk.gpsimd
            def _(gp):
                _phase_barrier(P, gp)
                ct.dmaq("DW", gp, out=b1s[:], in_=dram["b1"][:])
                ct.dmaq("DW", gp, out=b2s[:], in_=dram["b2"][:])
                ct.dmaq("DW", gp, out=onesb[:], in_=dram["ones_col"][:])
            bias_tick = ct.n["DW"]

            @blk.sync
            def _(sy):
                _phase_barrier(P, sy)
                ct.dmaq("DX", sy, out=o128[:], in_=dram["ones_col"].rearrange(
                    "o p -> p o"))
            o128_tick = ct.n["DX"]

            if "tap_b1" in P.dbg_out:
                @blk.sync
                def _(sy):
                    sy.wait_ge(ct.sems["DW"], bias_tick)
                    ct.dmaq("DX", sy, out=P.dbg_out["tap_b1"], in_=b1s[:])
                btap = ct.n["DX"]

                @blk.vector
                def _(v):
                    v.wait_ge(ct.sems["DX"], btap)
                    ct.incd("V", v)

            def quantize_W(name, nchunks, cw_, dst, wstg):
                tot = float(nchunks * 128 * cw_)
                wld = 0
                for ck in range(nchunks):
                    @blk.sync
                    def _(sy, ck=ck, wld=wld):
                        if ck == 0:
                            _phase_barrier(P, sy)
                            ct.wait(sy, "DS")
                        if wld:
                            ct.wait(sy, "V", wld)
                        ct.dmaq("DX", sy, out=wstg[:, 0:cw_],
                                                  in_=dram[name][128 * ck:128 * ck + 128, :])
                    d_tick = ct.n["DX"]

                    @blk.vector
                    def _(v, ck=ck, d_tick=d_tick):
                        v.wait_ge(ct.sems["DX"], d_tick)
                        v.drain()
                        if ck < 16:
                            v.reduce_sum(ln4[:, 8 + ck:9 + ck],
                                         wstg[:, 0:cw_], axis=AX.X, apply_absolute_value=True)
                        else:
                            v.reduce_sum(ln4[:, 24:25], wstg[:, 0:cw_], axis=AX.X,
                                         apply_absolute_value=True)
                            v.drain()
                            v.tensor_tensor(ln4[:, 8 + ck - 16:9 + ck - 16],
                                            ln4[:, 8 + ck - 16:9 + ck - 16],
                                            ln4[:, 24:25], op=OP.add)
                        ct.incd("V", v)
                    wld = ct.n["V"]
                ncol = 8 if nchunks <= 8 else 16

                @blk.tensor
                def _(t, wld=wld, ncol=ncol):
                    t.wait_ge(ct.sems["V"], wld)
                    t.wait_ge(ct.sems["DX"], o128_tick)
                    ct.wait(t, "V")
                    t.matmul(ps2[0:1, 0:ncol], o128[:, :], ln4[:, 8:8 + ncol],
                             start=True, stop=True)
                    ct.incd("T", t)
                s1_tick = ct.n["T"]

                @blk.vector
                def _(v, s1_tick=s1_tick, tot=tot, ncol=ncol):
                    v.wait_ge(ct.sems["T"], s1_tick)
                    v.tensor_reduce(wsc[0:1, 0:1], ps2[0:1, 0:ncol], axis=AX.X, op=OP.add)
                    v.drain()
                    v.tensor_scalar(wsc[0:1, 0:1], wsc[0:1, 0:1], 1.0 / tot, 1e-5,
                                    op0=OP.mult, op1=OP.max)
                    ct.incd("V", v)
                ws_tick = ct.n["V"]

                @blk.tensor
                def _(t, ws_tick=ws_tick):
                    t.wait_ge(ct.sems["V"], ws_tick)
                    t.matmul(ps2[0:128, 64:65], onesc[:, :], wsc[:, :],
                             start=True, stop=True)
                    ct.incd("T", t)
                bc_tick = ct.n["T"]

                @blk.vector
                def _(v, bc_tick=bc_tick):
                    v.wait_ge(ct.sems["T"], bc_tick)
                    v.tensor_copy(ln4[:, 0:1], ps2[0:128, 64:65])
                    v.drain()
                    v.reciprocal(ln4[:, 1:2], ln4[:, 0:1])
                    ct.incd("V", v)
                wsb_tick = ct.n["V"]

                qdone = 0
                for ck in range(nchunks):
                    @blk.sync
                    def _(sy, ck=ck, qdone=qdone, wsb_tick=wsb_tick):
                        if ck == 0:
                            sy.wait_ge(ct.sems["V"], wsb_tick)
                        if qdone:
                            sy.wait_ge(ct.sems["V"], qdone)
                        ct.dmaq("DX", sy, out=wstg[:, 0:cw_],
                                                  in_=dram[name][128 * ck:128 * ck + 128, :])
                    d_tick = ct.n["DX"]

                    @blk.scalar
                    def _(s, d_tick=d_tick):
                        s.wait_ge(ct.sems["DX"], d_tick)
                        ct.wait(s, "V")
                        s.activation(wstg[:, 0:cw_], wstg[:, 0:cw_], AF.Identity,
                                     scale=ln4[:, 1:2], bias=sb["magic"][:, :])
                        ct.incd("S", s)
                    qdone = ct.n["S"]

                    @blk.vector
                    def _(v, ck=ck, qdone=qdone):
                        v.wait_ge(ct.sems["S"], qdone)
                        v.tensor_scalar(wstg[:, 0:cw_], wstg[:, 0:cw_], MAGIC, 1.0,
                                        op0=OP.subtract, op1=OP.min)
                        v.drain()
                        v.tensor_scalar(dst[:, ck, :], wstg[:, 0:cw_], -1.0, ln4[:, 0:1],
                                        op0=OP.max, op1=OP.mult)
                        ct.incd("V", v)
                    qdone = ct.n["V"]
                return qdone

            with contextlib.ExitStack() as p5a:
                W1q = p5a.enter_context(nc.sbuf_tensor("W1q", [128, 8, HFF], BF16))
                with contextlib.ExitStack() as p5w:
                    wst1 = p5w.enter_context(nc.sbuf_tensor("wst1", [128, HFF], F32))
                    w1q_tick = quantize_W("W1T", 8, HFF, W1q, wst1)
                hbt = p5a.enter_context(nc.sbuf_tensor("hbt", [128, HFF], BF16))


                pslots5 = [ps0[:, 0:512], ps0[:, 512:1024], ps1[:, 0:512], ps1[:, 512:1024]]
                sl_tick = [0, 0, 0, 0]
                for tt in range(16):
                    hb_tick = 0
                    for ns in range(8):
                        k = tt * 8 + ns
                        sl = k % 4
                        pb = pslots5[sl]

                        @blk.tensor
                        def _(t, tt=tt, ns=ns, pb=pb, w=sl_tick[sl], w1q_tick=w1q_tick):
                            if tt == 0 and ns == 0:
                                t.wait_ge(ct.sems["V"], w1q_tick)
                                t.wait_ge(ct.sems["DW"], bias_tick)
                                ct.wait(t, "S")
                            if w:
                                t.wait_ge(ct.sems["S"], w)
                            import os
                            _nb = os.environ.get("KNB", "0") == "1"
                            for kc in range(8):
                                t.matmul(pb, xq1T[:, kc, 128 * tt:128 * tt + 128],
                                         W1q[:, kc, 512 * ns:512 * ns + 512],
                                         start=(kc == 0), stop=(_nb and kc == 7))
                            if not _nb:
                                t.matmul(pb, onesb[0:1, :], b1s[0:1, 512 * ns:512 * ns + 512],
                                         start=False, stop=True)
                            ct.incd("T", t)
                        m_tick = ct.n["T"]

                        @blk.scalar
                        def _(s, ns=ns, pb=pb, m_tick=m_tick, hb_tick=hb_tick):
                            import os
                            s.wait_ge(ct.sems["T"], m_tick)
                            if ns == 0 and hb_tick == 0:
                                ct.wait(s, "DS")
                            fn = AF.Identity if os.environ.get("KID", "0") == "1" else AF.Gelu
                            s.activation(hbt[:, 512 * ns:512 * ns + 512], pb, fn)
                            ct.incd("S", s)
                        sl_tick[sl] = ct.n["S"]
                    ge_tick = ct.n["S"]

                    @blk.scalar
                    def _(s, tt=tt, ge_tick=ge_tick):
                        s.wait_ge(ct.sems["S"], ge_tick)
                        s.drain()
                        ct.dmaq("DS", s, out=hbuf[128 * tt:128 * tt + 128, :],
                                                 in_=hbt[:, :])

                if "tap_w1q" in P.dbg_out:
                    @blk.sync
                    def _(sy):
                        ct.wait(sy, "S")
                        ct.dmaq("DX", sy, out=P.dbg_out["tap_w1q"], in_=W1q[:])
                    wtap2 = ct.n["DX"]

                    @blk.vector
                    def _(v):
                        v.wait_ge(ct.sems["DX"], wtap2)
                        ct.incd("V", v)
                if "tap_xq1" in P.dbg_out:
                    @blk.sync
                    def _(sy):
                        ct.wait(sy, "S")
                        ct.dmaq("DX", sy, out=P.dbg_out["tap_xq1"], in_=xq1T[:])
                    xtap2 = ct.n["DX"]

                    @blk.vector
                    def _(v):
                        v.wait_ge(ct.sems["DX"], xtap2)
                        ct.incd("V", v)
            xq1stk.close()
            with contextlib.ExitStack() as p5b:
                wst2 = p5b.enter_context(nc.sbuf_tensor("wst2", [128, D], F32))
                W2q = p5b.enter_context(nc.sbuf_tensor("W2q", [128, 32, D], BF16))
                hbt2 = p5b.enter_context(nc.sbuf_tensor("hbt2", [128, HFF], BF16))
                qf2 = p5b.enter_context(nc.sbuf_tensor("qf2", [128, HFF], F32))
                qx2 = p5b.enter_context(nc.sbuf_tensor("qx2", [128, HFF], BF16))
                xq2T = p5b.enter_context(nc.sbuf_tensor("xq2T", [128, 32, 128], BF16))
                w2q_tick = quantize_W("W2T", 32, D, W2q, wst2)

                for tt in range(16):
                    @blk.sync
                    def _(sy, tt=tt):
                        if tt == 0:
                            ct.wait(sy, "DS")
                        else:
                            ct.wait(sy, "S")
                        ct.dmaq("DX", sy, out=hbt2[:, :],
                                                  in_=hbuf[128 * tt:128 * tt + 128, :])
                    hld_tick = ct.n["DX"]

                    @blk.vector
                    def _(v, hld_tick=hld_tick):
                        v.wait_ge(ct.sems["DX"], hld_tick)
                        v.drain()
                        v.reduce_max(ln4[:, 4:5], hbt2[:, :], axis=AX.X, apply_absolute_value=True)
                        v.drain()
                        v.tensor_scalar(ln4[:, 5:6], ln4[:, 4:5], 1e-5, 1.0 / QB,
                                        op0=OP.max, op1=OP.mult)
                        v.drain()
                        v.reciprocal(ln4[:, 6:7], ln4[:, 5:6])
                        ct.incd("V", v)
                    qq1 = ct.n["V"]

                    @blk.scalar
                    def _(s, qq1=qq1, tt=tt):
                        s.wait_ge(ct.sems["V"], qq1)
                        if tt > 0:
                            ct.wait(s, "T")
                        s.activation(qf2[:, :], hbt2[:, :], AF.Identity,
                                     scale=ln4[:, 6:7], bias=sb["magic"][:, :])
                        ct.incd("S", s)
                    qq2 = ct.n["S"]

                    @blk.vector
                    def _(v, qq2=qq2):
                        v.wait_ge(ct.sems["S"], qq2)
                        v.tensor_scalar(qf2[:, :], qf2[:, :], MAGIC, QB,
                                        op0=OP.subtract, op1=OP.min)
                        v.drain()
                        v.tensor_scalar(qx2[:, :], qf2[:, :], -QB - 1.0, ln4[:, 5:6],
                                        op0=OP.max, op1=OP.mult)
                        ct.incd("V", v)
                    qq3 = ct.n["V"]

                    P.tr_seq([(qx2[:, m * 128:(m + 1) * 128], xq2T[:, m, :], None, None)
                              for m in range(32)], pre=("V", qq3))
                    x2_tick = ct.n["S"]

                    @blk.tensor
                    def _(t, tt=tt, x2_tick=x2_tick, w2q_tick=w2q_tick):
                        t.wait_ge(ct.sems["S"], x2_tick)
                        if tt == 0:
                            t.wait_ge(ct.sems["V"], w2q_tick)
                        else:
                            ct.wait(t, "V")
                        for half in range(2):
                            for kc in range(32):
                                t.matmul(ps2[:, 512 * half:512 * half + 512],
                                         xq2T[:, kc, :],
                                         W2q[:, kc, 512 * half:512 * half + 512],
                                         start=(kc == 0), stop=False)
                            t.matmul(ps2[:, 512 * half:512 * half + 512], onesb[0:1, :],
                                     b2s[0:1, 512 * half:512 * half + 512],
                                     start=False, stop=True)
                        ct.incd("T", t)
                    mm2_tick = ct.n["T"]

                    @blk.vector
                    def _(v, tt=tt, mm2_tick=mm2_tick):
                        v.wait_ge(ct.sems["T"], mm2_tick)
                        for half in range(2):
                            dst = x1acc[:, 1 + tt, 512 * half:512 * half + 512]
                            v.scalar_tensor_tensor(out=dst, in0=ps2[:, 512 * half:512 * half + 512],
                                                   scalar=1.0, in1=dst, op0=OP.mult, op1=OP.add)
                        ct.incd("V", v)

        @blk.sync
        def _(sy):
            ct.wait(sy, "V")
            ct.dmaq("DO", sy, 
                out=y.rearrange("(i p) d -> p i d", p=128),
                in_=x1acc[:, 1:17, :])
            ct.wait(sy, "DO")


def kernel(**inputs):
    in_maps = _host_prep(inputs)
    nc = build_program()
    res = run_bass_kernel_spmd(nc, in_maps, core_ids=list(range(8)))
    out = np.empty((B, T, D), np.float32)
    for core in range(8):
        b, g = core // 2, core % 2
        out[b, g * 2048:(g + 1) * 2048, :] = res.results[core]["y"]
    return out
